# version 06
# baseline (speedup 1.0000x reference)
import numpy as np
import ml_dtypes
import concourse.bass as bass
import concourse.mybir as mybir
from concourse.bass_utils import run_bass_kernel_spmd

F32 = mybir.dt.float32
BF16 = mybir.dt.bfloat16
AF = mybir.ActivationFunctionType
ALU = mybir.AluOpType
AX = mybir.AxisListType

DIM, DC = 2048, 16
NH, NKV, HD = 32, 8, 64
TOK, TOK2 = 1024, 2048
QT, STN = 8, 16
INTER, ICN = 5632, 44
KMC = 4
LN_EPS, RMS_EPS = 1e-5, 1e-6
NEGBIG = -1.0e30

SLOPES = [2.0 ** (-h / 4.0) for h in range(NH)]
CUT = 180.0
KEPT = [min(9, 2 + int(CUT / SLOPES[h]) // 128) for h in range(NH)]
for _h in range(NH):
    if KEPT[_h] < 9:
        assert SLOPES[_h] * (128 * KEPT[_h] - 127) >= CUT - 1e-6


def _vqt(h, st):
    lo, hi = max(0, st - 8), min(7, st + KEPT[h] - 9)
    return (lo, hi) if lo <= hi else None


def _st_list(h):
    return [st for st in range(STN) if _vqt(h, st) is not None]



def _tile_kmajor(W, n_kc, n_m):
    W = np.asarray(W, dtype=np.float32)
    Wr = W.reshape(n_m, 128, n_kc, 128)
    return np.ascontiguousarray(Wr.transpose(0, 3, 2, 1)).reshape(n_m, 128, n_kc * 128)


def _make_tables(Q0):
    sbias = np.empty((128, NH * STN), np.float32)
    S0 = Q0 - 1024
    for h in range(NH):
        for st in range(STN):
            s_glob = S0 + st * 128 + np.arange(128)
            if s_glob[0] < 0 or s_glob[0] >= 2048:
                sbias[:, h * STN + st] = NEGBIG
            else:
                sbias[:, h * STN + st] = (SLOPES[h] * s_glob).astype(np.float32)
    phiq8 = np.empty((32, QT * 128), np.float32)
    q_glob = Q0 + np.arange(QT * 128)
    for h in range(NH):
        phiq8[h, :] = (8.0 * SLOPES[h] * q_glob).astype(np.float32)
    return sbias, phiq8


def _stage_inputs(inputs):
    x = np.asarray(inputs["L0_attn_x"], np.float32)
    wo = np.asarray(inputs["L0_attn_wo"], np.float32)
    perm = np.empty(2048, np.int64)
    for h in range(NH):
        fp = (h // 2) * 128 + (h % 2) * 64
        perm[fp:fp + 64] = h * 64 + np.arange(64)

    def _fm(v, n):
        return np.ascontiguousarray(np.asarray(v, np.float32).reshape(n, 128).T)
    shared = dict(
        wq=_tile_kmajor(inputs["L0_attn_wq"], DC, DC),
        wk=_tile_kmajor(inputs["L0_attn_wk"], DC, KMC),
        wv=_tile_kmajor(inputs["L0_attn_wv"], DC, KMC),
        wo=_tile_kmajor(wo[:, perm], DC, DC),
        wg=_tile_kmajor(inputs["L0_ffn_gate_w"], DC, ICN),
        wu=_tile_kmajor(inputs["L0_ffn_up_w"], DC, ICN),
        wd=_tile_kmajor(inputs["L0_ffn_down_w"], ICN, DC),
        qnw=_fm(inputs["L0_attn_q_norm_w"], DC),
        knw=_fm(inputs["L0_attn_k_norm_w"], KMC),
        gs=_fm(inputs["L0_ffn_gate_s"], ICN),
        us=_fm(inputs["L0_ffn_up_s"], ICN),
        ds=_fm(inputs["L0_ffn_down_s"], DC))
    maps = []
    for c in range(8):
        b, hh = c // 2, c % 2
        xt = np.ascontiguousarray(x[b].T)
        if hh == 0:
            xcore = np.concatenate([np.zeros((DIM, 1024), np.float32), xt[:, 0:1024]], 1)
        else:
            xcore = xt
        sbias, phiq8 = _make_tables(hh * 1024)
        m = dict(shared)
        m["xt"] = np.ascontiguousarray(xcore)
        m["sbias"] = sbias
        m["phiq8"] = phiq8
        maps.append(m)
    return maps


def _gather_output(results):
    out = np.empty((4, 2048, 2048), np.float32)
    for c in range(8):
        b, hh = c // 2, c % 2
        out[b, hh * 1024:(hh + 1) * 1024, :] = results[c]["out"].T
    return out



class SemLog:
    def __init__(self):
        self.ev = {e: [] for e in ("te", "act", "dve", "pool", "sy")}

    def wait(self, eng, ch, val):
        self.ev[eng].append(("w", ch, val))

    def inc(self, eng, ch, val_after):
        self.ev[eng].append(("i", ch, val_after))

    def audit(self):
        pos = {e: 0 for e in self.ev}
        cur = {}
        progress = True
        while progress:
            progress = False
            for e, lst in self.ev.items():
                while pos[e] < len(lst):
                    kind, ch, val = lst[pos[e]]
                    if kind == "i":
                        cur[ch] = max(cur.get(ch, 0), val)
                    elif cur.get(ch, 0) < val:
                        break
                    pos[e] += 1
                    progress = True
        stuck = {e: (pos[e], len(lst)) for e, lst in self.ev.items() if pos[e] < len(lst)}
        if stuck:
            det = {e: self.ev[e][pos[e]] for e in stuck}
            raise RuntimeError(f"SEM AUDIT DEADLOCK: {stuck} next={det}")


_LAST_LOG = None


def build_nc(taps=()):
    nc = bass.Bass("TRN2", target_bir_lowering=False)
    taps = set(taps)

    xt = nc.dram_tensor("xt", [DIM, TOK2], F32, kind="ExternalInput").ap()
    wq_d = nc.dram_tensor("wq", [DC, 128, DC * 128], F32, kind="ExternalInput").ap()
    wk_d = nc.dram_tensor("wk", [KMC, 128, DC * 128], F32, kind="ExternalInput").ap()
    wv_d = nc.dram_tensor("wv", [KMC, 128, DC * 128], F32, kind="ExternalInput").ap()
    wo_d = nc.dram_tensor("wo", [DC, 128, DC * 128], F32, kind="ExternalInput").ap()
    wg_d = nc.dram_tensor("wg", [ICN, 128, DC * 128], F32, kind="ExternalInput").ap()
    wu_d = nc.dram_tensor("wu", [ICN, 128, DC * 128], F32, kind="ExternalInput").ap()
    wd_d = nc.dram_tensor("wd", [DC, 128, ICN * 128], F32, kind="ExternalInput").ap()
    qnw_d = nc.dram_tensor("qnw", [128, DC], F32, kind="ExternalInput").ap()
    knw_d = nc.dram_tensor("knw", [128, KMC], F32, kind="ExternalInput").ap()
    gs_d = nc.dram_tensor("gs", [128, ICN], F32, kind="ExternalInput").ap()
    us_d = nc.dram_tensor("us", [128, ICN], F32, kind="ExternalInput").ap()
    ds_d = nc.dram_tensor("ds", [128, DC], F32, kind="ExternalInput").ap()
    sbias_d = nc.dram_tensor("sbias", [128, NH * STN], F32, kind="ExternalInput").ap()
    phiq8_d = nc.dram_tensor("phiq8", [32, QT * 128], F32, kind="ExternalInput").ap()
    out_d = nc.dram_tensor("out", [DIM, TOK], F32, kind="ExternalOutput").ap()

    x2s = nc.dram_tensor("x2s", [DIM, TOK], F32, kind="Internal").ap()
    d4s = nc.dram_tensor("d4s", [DIM, TOK], BF16, kind="Internal").ap()
    phis = nc.dram_tensor("phis", [QT, 32, 2, 128], BF16, kind="Internal").ap()

    identF_d = nc.inline_tensor(np.eye(128, dtype=np.float32), "identF").ap()
    onesB_d = nc.inline_tensor(np.ones((128, 128), ml_dtypes.bfloat16), "onesB").ap()
    mdiagT_d = nc.inline_tensor(np.triu(np.ones((128, 128), np.float32), 1).astype(ml_dtypes.bfloat16), "mdiagT").ap()
    medgeT_d = nc.inline_tensor(np.tril(np.ones((128, 128), np.float32), -1).astype(ml_dtypes.bfloat16), "medgeT").ap()
    negI_d = nc.inline_tensor((NEGBIG * np.eye(128, dtype=np.float32)).astype(ml_dtypes.bfloat16), "negI").ap()

    tap_d = {}
    for name, shape, dt_ in (
            ("xn", [128, DC * TOK2], BF16), ("khat", [66, NKV * TOK2], BF16),
            ("qhat", [66, NH * TOK], BF16), ("vhat", [128, STN * NKV * 65], BF16),
            ("attnout", [128, DC * TOK], BF16), ("a2", [128, DC * TOK], BF16),
            ("x2", [DIM, TOK], F32), ("xn3", [128, DC * TOK], BF16),
            ("d4", [DIM, TOK], BF16), ("hb1", [128, DC * TOK], BF16),
            ("hb2", [128, (ICN - DC) * TOK], BF16)):
        if name in taps:
            tap_d[name] = nc.dram_tensor("tap_" + name, shape, dt_, kind="ExternalOutput").ap()

    global _LAST_LOG
    L = SemLog()
    _LAST_LOG = L
    C = {"w": 0, "s": 0, "te": 0, "act": 0, "dve": 0}

    with (
        nc.semaphore("sW") as sW,
        nc.semaphore("sS") as sS,
        nc.semaphore("sT") as sT,
        nc.semaphore("sA") as sA,
        nc.semaphore("sV") as sV,
        nc.Block() as block,
    ):
        SEM = {"w": sW, "s": sS, "te": sT, "act": sA, "dve": sV}

        def inc(engname, instr, ch, n=1):
            C[ch] += n
            instr.then_inc(SEM[ch], n)
            L.inc(engname, ch, C[ch])
            return C[ch]

        def wt(engname, eng, ch, val):
            if val is None or val <= 0:
                return
            L.wait(engname, ch, val)
            eng.wait_ge(SEM[ch], val)

        sb = nc.alloc_sbuf_tensor
        sbias_t = sb("sbias_t", [128, NH * STN], F32)
        phiq8_t = sb("phiq8_t", [32, QT * 128], F32)
        identF_t = sb("identF_t", [128, 128], F32)
        onesB_t = sb("onesB_t", [128, 128], BF16)
        mdiagT_t = sb("mdiagT_t", [128, 128], BF16)
        medgeT_t = sb("medgeT_t", [128, 128], BF16)
        negI_t = sb("negI_t", [128, 128], BF16)
        qnw_t = sb("qnw_t", [128, DC], F32)
        knw_t = sb("knw_t", [128, KMC], F32)
        gs_t = sb("gs_t", [128, ICN], F32)
        us_t = sb("us_t", [128, ICN], F32)
        ds_t = sb("ds_t", [128, DC], F32)
        wbufA = sb("wbufA", [128, DC * 128], BF16)
        wbufB = sb("wbufB", [128, DC * 128], BF16)
        stF = sb("stF", [128, TOK2], F32)
        stM = sb("stM", [128, TOK2], BF16)
        stR = sb("stR", [128, TOK2], BF16)
        sqscr = sb("sqscr", [128, TOK2], BF16)
        m0buf = sb("m0buf", [128, 32], F32)
        phisc = sb("phisc", [32, 2 * 128], BF16)
        phiA = sb("phiA", [32, 128], F32)
        pt_t = sb("pt_t", [128, 4 * QT * 128], BF16)
        zrow = sb("zrow", [1, 512], BF16)

        buf_last = {"A": 0, "B": 0}

        S0 = nc.alloc_psum_tensor("S0", [128, 1024], F32)
        S1 = nc.alloc_psum_tensor("S1", [128, 1024], F32)
        A0 = nc.alloc_psum_tensor("A0", [128, 1024], F32)
        A1 = nc.alloc_psum_tensor("A1", [128, 1024], F32)
        SP, APm = [S0, S1], [A0, A1]

        def p0(sy):
            for dst, src in ((sbias_t, sbias_d), (phiq8_t, phiq8_d), (identF_t, identF_d),
                             (onesB_t, onesB_d), (mdiagT_t, mdiagT_d), (medgeT_t, medgeT_d),
                             (negI_t, negI_d), (qnw_t, qnw_d), (knw_t, knw_d),
                             (gs_t, gs_d), (us_t, us_d), (ds_t, ds_d)):
                inc("sy", sy.dma_start(out=dst.ap(), in_=src), "s", 16)
        block.sync(p0)
        c_consts = C["s"]
        block.tensor(lambda t: wt("te", t, "s", c_consts))
        block.scalar(lambda s_: wt("act", s_, "s", c_consts))
        def _dve_init(v):
            wt("dve", v, "s", c_consts)
            inc("dve", v.memset(zrow.ap(), 0.0), "dve", 1)
        block.vector(_dve_init)
        c_zrow = C["dve"]

        def ln_stats(nchunks, ntok, src_fn, eps, rms_denom=None,
                     pre_wait_dve=None, chunk_wait=None, mm_pre_dve=None):
            nh = ntok // 1024
            sq_pred = [C["act"] + i + 1 for i in range(nchunks)]
            te0 = C["te"]
            rms = rms_denom is not None
            denom = float(rms_denom if rms else DIM)

            def _sq(sc):
                if pre_wait_dve is not None:
                    wt("act", sc, "dve", pre_wait_dve)
                for mc in range(nchunks):
                    if chunk_wait is not None:
                        ch, fn = chunk_wait
                        wt("act", sc, ch, fn(mc))
                    if mc >= 1:
                        wt("act", sc, "te", te0 + mc)
                    ins = sc.activation(sqscr.ap()[:, 0:ntok], src_fn(mc), AF.Square)
                    inc("act", ins, "act", 1)
            block.scalar(_sq)

            def _mm(t):
                if mm_pre_dve is not None:
                    wt("te", t, "dve", mm_pre_dve)
                for mc in range(nchunks):
                    wt("te", t, "act", sq_pred[mc])
                    ins = None
                    for half in range(nh):
                        for ts in range(2):
                            sl = half * 1024 + ts * 512
                            psl = slice(ts * 512, ts * 512 + 512)
                            if not rms:
                                t.matmul(APm[half].ap()[:, psl], onesB_t.ap(),
                                         src_fn(mc)[:, sl:sl + 512],
                                         start=(mc == 0), stop=(mc == nchunks - 1))
                                ins = t.matmul(SP[half].ap()[:, psl], onesB_t.ap(),
                                               sqscr.ap()[:, sl:sl + 512],
                                               start=(mc == 0), stop=(mc == nchunks - 1))
                            else:
                                ins = t.matmul(APm[half].ap()[:, psl], onesB_t.ap(),
                                               sqscr.ap()[:, sl:sl + 512],
                                               start=(mc == 0), stop=(mc == nchunks - 1))
                    inc("te", ins, "te", 1)
            block.tensor(_mm)
            c_stat = C["te"]

            def _mr(v):
                wt("dve", v, "te", c_stat)
                for half in range(nh):
                    hs = slice(half * 1024, half * 1024 + 1024)
                    if not rms:
                        v.tensor_scalar(stM.ap()[:, hs], APm[half].ap(), 1.0 / denom, None, ALU.mult)
                        v.tensor_scalar(stF.ap()[:, hs], SP[half].ap(), 1.0 / denom, None, ALU.mult)
                    else:
                        v.tensor_scalar(stF.ap()[:, hs], APm[half].ap(), 1.0 / denom, None, ALU.mult)
                if not rms:
                    v.tensor_tensor(sqscr.ap()[:, 0:ntok], stM.ap()[:, 0:ntok], stM.ap()[:, 0:ntok], ALU.mult)
                    v.tensor_tensor(stF.ap()[:, 0:ntok], stF.ap()[:, 0:ntok], sqscr.ap()[:, 0:ntok], ALU.subtract)
                v.tensor_scalar(stF.ap()[:, 0:ntok], stF.ap()[:, 0:ntok], float(eps), None, ALU.add)
                ins = v.reciprocal(stF.ap()[:, 0:ntok], stF.ap()[:, 0:ntok])
                inc("dve", ins, "dve", 1)
            block.vector(_mr)
            c_r1 = C["dve"]

            def _sqrt(sc):
                wt("act", sc, "dve", c_r1)
                ins = sc.activation(stR.ap()[:, 0:ntok], stF.ap()[:, 0:ntok], AF.Sqrt)
                inc("act", ins, "act", 1)
            block.scalar(_sqrt)
            return C["act"]

        def proj_sweep(n_mc, src_w, ntok, mov_fn, evac_fn,
                       pre_te_dve=None, pre_te_act=None):
            nh = ntok // 1024
            w0 = C["w"]
            te_done = []
            evac_pred = [C["dve"] + i + 1 for i in range(n_mc * nh)]

            def _te(t):
                if pre_te_dve is not None:
                    wt("te", t, "dve", pre_te_dve)
                if pre_te_act is not None:
                    wt("te", t, "act", pre_te_act)
                n = 0
                for mc in range(n_mc):
                    wt("te", t, "w", w0 + (mc + 1) * 16)
                    buf = wbufA if mc % 2 == 0 else wbufB
                    for half in range(nh):
                        if n >= 2:
                            wt("te", t, "dve", evac_pred[n - 2])
                        ps = SP[n % 2]
                        ins = None
                        for kc in range(DC):
                            for ts in range(2):
                                off = half * 1024 + ts * 512
                                ins = t.matmul(ps.ap()[:, ts * 512:ts * 512 + 512],
                                               buf.ap()[:, kc * 128:(kc + 1) * 128],
                                               mov_fn(kc, off),
                                               start=(kc == 0), stop=(kc == DC - 1))
                        inc("te", ins, "te", 1)
                        te_done.append(C["te"])
                        n += 1
            block.tensor(_te)

            def _dve(v):
                n = 0
                for mc in range(n_mc):
                    for half in range(nh):
                        wt("dve", v, "te", te_done[n])
                        ins = evac_fn(v, mc, half, SP[n % 2].ap())
                        inc("dve", ins, "dve", 1)
                        n += 1
            block.vector(_dve)
            assert C["dve"] == evac_pred[-1]

            def _dma(g):
                for mc in range(n_mc):
                    bname = "A" if mc % 2 == 0 else "B"
                    wv_ = te_done[(mc - 2) * nh + nh - 1] if mc >= 2 else buf_last[bname]
                    wt("pool", g, "te", wv_)
                    buf = wbufA if mc % 2 == 0 else wbufB
                    inc("pool", g.dma_start(out=buf.ap(), in_=src_w[mc]), "w", 16)
            block.gpsimd(_dma)
            for mc in range(max(0, n_mc - 2), n_mc):
                buf_last["A" if mc % 2 == 0 else "B"] = te_done[mc * nh + nh - 1]
            return te_done, evac_pred

        khat_g = nc.sbuf_tensor("khat", [66, NKV * TOK2], BF16)
        khat = khat_g.__enter__()
        vhat_g = nc.sbuf_tensor("vhat", [128, STN * NKV * 65], BF16)
        vhat = vhat_g.__enter__()
        xn_g = nc.sbuf_tensor("xn", [128, DC * TOK2], BF16)
        xn = xn_g.__enter__()
        ktmp_g = nc.sbuf_tensor("ktmp", [128, KMC * TOK2], BF16, side="right")
        ktmp = ktmp_g.__enter__()

        _scope_n = 'p1_ln1'; _scope_i = nc.enter_named_scope('p1_ln1', False)[0]
        c_xld0 = C["w"]

        def p1_dma(g):
            for kc in range(DC):
                ins = g.dma_start(out=xn.ap()[:, kc * TOK2:(kc + 1) * TOK2],
                                  in_=xt[kc * 128:(kc + 1) * 128, :])
                inc("pool", ins, "w", 16)
        block.gpsimd(p1_dma)

        c_ln1 = ln_stats(DC, TOK2, lambda mc: xn.ap()[:, mc * TOK2:(mc + 1) * TOK2],
                         LN_EPS, chunk_wait=("w", lambda mc: c_xld0 + (mc + 1) * 16))

        def p1_norm(v):
            wt("dve", v, "act", c_ln1)
            for kc in range(DC):
                sl = slice(kc * TOK2, (kc + 1) * TOK2)
                v.tensor_tensor(xn.ap()[:, sl], xn.ap()[:, sl], stM.ap(), ALU.subtract)
                ins = v.tensor_tensor(xn.ap()[:, sl], xn.ap()[:, sl], stR.ap(), ALU.mult)
                inc("dve", ins, "dve", 1)
        block.vector(p1_norm)
        c_xn = C["dve"]

        if "xn" in taps:
            def tp_xn(sy):
                wt("sy", sy, "dve", c_xn)
                inc("sy", sy.dma_start(out=tap_d["xn"], in_=xn.ap()), "s", 16)
            block.sync(tp_xn)

        nc.leave_named_scope(_scope_n, _scope_i, False)
        _scope_n = 'p2_k'; _scope_i = nc.enter_named_scope('p2_k', False)[0]
        k_te, k_ev = proj_sweep(
            KMC, wk_d, TOK2,
            lambda kc, off: xn.ap()[:, kc * TOK2 + off: kc * TOK2 + off + 512],
            lambda v, mc, half, ps: v.tensor_copy(
                ktmp.ap()[:, mc * TOK2 + half * 1024: mc * TOK2 + half * 1024 + 1024], ps),
            pre_te_dve=c_xn)

        c_krms = ln_stats(KMC, TOK2, lambda mc: ktmp.ap()[:, mc * TOK2:(mc + 1) * TOK2],
                          RMS_EPS, rms_denom=512.0,
                          chunk_wait=("dve", lambda mc: k_ev[2 * mc + 1]))

        def p2_khat(v):
            wt("dve", v, "act", c_krms)
            v.memset(khat.ap()[64:66, :], 1.0)
            for mc in range(KMC):
                for par in range(2):
                    g_head = 2 * mc + par
                    ins = v.scalar_tensor_tensor(
                        khat.ap()[0:64, g_head * TOK2:(g_head + 1) * TOK2],
                        ktmp.ap()[par * 64:(par + 1) * 64, mc * TOK2:(mc + 1) * TOK2],
                        knw_t.ap()[par * 64:(par + 1) * 64, mc:mc + 1],
                        stR.ap()[par * 64:(par + 1) * 64, 0:TOK2],
                        ALU.mult, ALU.mult)
                    inc("dve", ins, "dve", 1)
        block.vector(p2_khat)
        c_khat = C["dve"]

        if "khat" in taps:
            def tp_kh(sy):
                wt("sy", sy, "dve", c_khat)
                inc("sy", sy.dma_start(out=tap_d["khat"], in_=khat.ap()), "s", 16)
            block.sync(tp_kh)

        nc.leave_named_scope(_scope_n, _scope_i, False)
        _scope_n = 'p2_v'; _scope_i = nc.enter_named_scope('p2_v', False)[0]
        vW0 = C["w"]
        v_te = []
        v_stf, v_evh = {}, {}
        cd = C["dve"]
        for mc in range(KMC):
            for half in range(2):
                cd += 1
                v_stf[(mc, half)] = cd
            for grp in range(4):
                cd += 1
                v_evh[(mc, grp)] = cd

        def p2_v_te(t):
            wt("te", t, "dve", c_khat)
            n = 0
            for mc in range(KMC):
                wt("te", t, "w", vW0 + (mc + 1) * 16)
                buf = wbufA if mc % 2 == 0 else wbufB
                for half in range(2):
                    if n >= 2:
                        wt("te", t, "dve", v_stf[(mc - 1, half)])
                    ps = SP[n % 2]
                    ins = None
                    for kc in range(DC):
                        for ts in range(2):
                            off = half * 1024 + ts * 512
                            ins = t.matmul(ps.ap()[:, ts * 512:ts * 512 + 512],
                                           buf.ap()[:, kc * 128:(kc + 1) * 128],
                                           xn.ap()[:, kc * TOK2 + off: kc * TOK2 + off + 512],
                                           start=(kc == 0), stop=(kc == DC - 1))
                    inc("te", ins, "te", 1)
                    v_te.append(C["te"])
                    n += 1
                wt("te", t, "dve", v_stf[(mc, 1)])
                for grp in range(4):
                    if mc > 0 or grp >= 2:
                        key = (mc, grp - 2) if grp >= 2 else (mc - 1, grp + 2)
                        wt("te", t, "dve", v_evh[key])
                    ins = None
                    for j in range(4):
                        sc16 = grp * 4 + j
                        cps = ((grp % 2) * 4 + j) * 128
                        ins = t.transpose(A0.ap()[:, cps:cps + 128],
                                          stF.ap()[:, sc16 * 128:(sc16 + 1) * 128],
                                          identF_t.ap())
                    inc("te", ins, "te", 1)
        block.tensor(p2_v_te)
        v_tr = {}
        ctr = C["te"]
        tt = v_te[0] - 1
        tt = None
        cnt = v_te[0] - 1
        idx = 0
        for mc in range(KMC):
            cnt = v_te[2 * mc + 1]
            for grp in range(4):
                cnt += 1
                v_tr[(mc, grp)] = cnt
        assert cnt == C["te"]

        def p2_v_dve(v):
            n = 0
            for mc in range(KMC):
                for half in range(2):
                    wt("dve", v, "te", v_te[n])
                    ins = v.tensor_copy(stF.ap()[:, half * 1024:half * 1024 + 1024],
                                        SP[n % 2].ap())
                    inc("dve", ins, "dve", 1)
                    assert C["dve"] == v_stf[(mc, half)]
                    n += 1
                for grp in range(4):
                    wt("dve", v, "te", v_tr[(mc, grp)])
                    ins = None
                    for j in range(4):
                        sc16 = grp * 4 + j
                        cps = ((grp % 2) * 4 + j) * 128
                        for par in range(2):
                            g_head = 2 * mc + par
                            col = (sc16 * NKV + g_head) * 65
                            ins = v.tensor_copy(vhat.ap()[:, col:col + 64],
                                                A0.ap()[:, cps + par * 64: cps + par * 64 + 64])
                    inc("dve", ins, "dve", 1)
                    assert C["dve"] == v_evh[(mc, grp)]
        block.vector(p2_v_dve)

        def p2_v_dma(g):
            for mc in range(KMC):
                bname = "A" if mc % 2 == 0 else "B"
                wv_ = v_te[(mc - 2) * 2 + 1] if mc >= 2 else buf_last[bname]
                wt("pool", g, "te", wv_)
                buf = wbufA if mc % 2 == 0 else wbufB
                inc("pool", g.dma_start(out=buf.ap(), in_=wv_d[mc]), "w", 16)
        block.gpsimd(p2_v_dma)
        buf_last["A"] = v_te[5]
        buf_last["B"] = v_te[7]

        def p2_v_ones(v):
            ins = v.memset(vhat.ap().rearrange("p (b c) -> p b c", c=65)[:, :, 64:65], 1.0)
            inc("dve", ins, "dve", 1)
        block.vector(p2_v_ones)
        c_vhat = C["dve"]

        if "vhat" in taps:
            def tp_vh(sy):
                wt("sy", sy, "dve", c_vhat)
                inc("sy", sy.dma_start(out=tap_d["vhat"], in_=vhat.ap()), "s", 16)
            block.sync(tp_vh)

        ktmp_g.__exit__(None, None, None)

        nc.leave_named_scope(_scope_n, _scope_i, False)
        _scope_n = 'p2_q'; _scope_i = nc.enter_named_scope('p2_q', False)[0]
        qtmp_g = nc.sbuf_tensor("qtmp", [128, DC * TOK], BF16, side="right")
        qtmp = qtmp_g.__enter__()

        q_te, q_ev = proj_sweep(
            DC, wq_d, TOK,
            lambda kc, off: xn.ap()[:, kc * TOK2 + 1024 + off: kc * TOK2 + 1024 + off + 512],
            lambda v, mc, half, ps: v.tensor_copy(qtmp.ap()[:, mc * TOK:(mc + 1) * TOK], ps),
            pre_te_dve=c_vhat)

        c_qrms = ln_stats(DC, TOK, lambda mc: qtmp.ap()[:, mc * TOK:(mc + 1) * TOK],
                          RMS_EPS, rms_denom=float(DIM),
                          chunk_wait=("dve", lambda mc: q_ev[mc]))

        xn_g.__exit__(None, None, None)
        qhat_g = nc.sbuf_tensor("qhat", [66, NH * TOK], BF16)
        qhat = qhat_g.__enter__()

        def p2_qhat(v):
            wt("dve", v, "act", c_qrms)
            for mc in range(DC):
                for par in range(2):
                    head = 2 * mc + par
                    ins = v.scalar_tensor_tensor(
                        qhat.ap()[0:64, head * TOK:(head + 1) * TOK],
                        qtmp.ap()[par * 64:(par + 1) * 64, mc * TOK:(mc + 1) * TOK],
                        qnw_t.ap()[par * 64:(par + 1) * 64, mc:mc + 1],
                        stR.ap()[par * 64:(par + 1) * 64, 0:TOK],
                        ALU.mult, ALU.mult)
                    inc("dve", ins, "dve", 1)
        block.vector(p2_qhat)
        c_qhat_rows = C["dve"]

        nc.leave_named_scope(_scope_n, _scope_i, False)
        _scope_n = 'p3a_diag'; _scope_i = nc.enter_named_scope('p3a_diag', False)[0]
        diag_te = {}
        diag_max = {}
        ph_stt = {}
        cd = C["dve"]
        for qt in range(QT):
            for grp in range(4):
                cd += 1
                diag_max[(qt, grp)] = cd
            cd += 1
            ph_stt[qt] = cd
        te_p3a0 = C["te"]
        ph_tr_pred = {qt: te_p3a0 + 5 * qt + 5 for qt in range(QT)}

        def p3a_te(t):
            wt("te", t, "dve", c_qhat_rows)
            for qt in range(QT):
                st = qt + 8
                for grp in range(4):
                    if qt > 0 or grp >= 2:
                        key = (qt, grp - 2) if grp >= 2 else (qt - 1, grp + 2)
                        wt("te", t, "dve", diag_max[key])
                    ins = None
                    for j in range(8):
                        h = grp * 8 + j
                        g_head = h // 4
                        ins = t.matmul(SP[grp % 2].ap()[:, j * 128:(j + 1) * 128],
                                       qhat.ap()[0:64, h * TOK + qt * 128: h * TOK + qt * 128 + 128],
                                       khat.ap()[0:64, g_head * TOK2 + st * 128: g_head * TOK2 + st * 128 + 128],
                                       start=True, stop=True)
                    inc("te", ins, "te", 1)
                    diag_te[(qt, grp)] = C["te"]
                wt("te", t, "dve", diag_max[(qt, 3)])
                if qt >= 2:
                    wt("te", t, "dve", ph_stt[qt - 2])
                ins = t.transpose(A1.ap()[0:32, (qt % 2) * 512:(qt % 2) * 512 + 128],
                                  m0buf.ap(), identF_t.ap())
                inc("te", ins, "te", 1)
                assert C["te"] == ph_tr_pred[qt]
        block.tensor(p3a_te)

        s_p3a0 = C["s"]

        def p3a_dve(v):
            for qt in range(QT):
                for grp in range(4):
                    wt("dve", v, "te", diag_te[(qt, grp)])
                    if qt >= 1 and grp == 0:
                        wt("dve", v, "te", ph_tr_pred[qt - 1])
                    ins = None
                    for j in range(8):
                        h = grp * 8 + j
                        ins = v.reduce_max(m0buf.ap()[:, h:h + 1],
                                           SP[grp % 2].ap()[:, j * 128:(j + 1) * 128],
                                           axis=AX.X, negate=True)
                    inc("dve", ins, "dve", 1)
                    assert C["dve"] == diag_max[(qt, grp)]
                wt("dve", v, "te", ph_tr_pred[qt])
                if qt >= 1:
                    wt("dve", v, "s", s_p3a0 + 32 * (qt - 1) + 16)
                ps = A1.ap()[0:32, (qt % 2) * 512:(qt % 2) * 512 + 128]
                v.tensor_tensor(phiA.ap(), ps, phiq8_t.ap()[:, qt * 128:(qt + 1) * 128], ALU.subtract)
                v.tensor_copy(phisc.ap()[:, 0:128], phiA.ap())
                ins = v.tensor_tensor(phisc.ap()[:, 128:256], phiA.ap(), phisc.ap()[:, 0:128], ALU.subtract)
                inc("dve", ins, "dve", 1)
                assert C["dve"] == ph_stt[qt]
        block.vector(p3a_dve)

        def p3a_dma(sy):
            for qt in range(QT):
                wt("sy", sy, "dve", ph_stt[qt])
                ins = sy.dma_start(out=phis[qt],
                                   in_=phisc.ap().rearrange("h (e q) -> h e q", e=2))
                inc("sy", ins, "s", 16)
                wt("sy", sy, "s", C["s"])
                ins = sy.dma_start(
                    out=qhat.ap()[64:66, :].rearrange("e (h q) -> e h q", h=32)[:, :, qt * 128:(qt + 1) * 128],
                    in_=phis[qt].rearrange("h e q -> e h q"))
                inc("sy", ins, "s", 16)
        block.sync(p3a_dma)
        c_phi_all = C["s"]

        if "qhat" in taps:
            def tp_qh(sy):
                wt("sy", sy, "s", c_phi_all)
                inc("sy", sy.dma_start(out=tap_d["qhat"], in_=qhat.ap()), "s", 16)
            block.sync(tp_qh)

        nc.leave_named_scope(_scope_n, _scope_i, False)
        _scope_n = 'p3b_attn'; _scope_i = nc.enter_named_scope('p3b_attn', False)[0]
        def pt_ap(h, st, lo, hi):
            base = ((h % 2) * 2 + (st % 2)) * QT * 128
            return pt_t.ap()[:, base + lo * 128: base + (hi + 1) * 128]

        sched = []
        exp_order = []
        for hp in range(0, NH, 2):
            hs = [hp, hp + 1]
            stl = {h: _st_list(h) for h in hs}
            union = sorted(set(stl[hp]) | set(stl[hp + 1]))
            pend = {h: [] for h in hs}
            zf_done = {h: False for h in hs}
            ndiv = 0
            for st in union:
                for h in hs:
                    if st in stl[h]:
                        sched.append(("kq", h, st))
                        exp_order.append((h, st))
                        pend[h].append(st)
                if hp >= 2 and ndiv < 2:
                    sched.append(("div", hp - 2 + ndiv))
                    ndiv += 1
                for h in hs:
                    if len(pend[h]) >= 2:
                        if not zf_done[h]:
                            sched.append(("zf", h))
                            zf_done[h] = True
                        sched.append(("av", h, pend[h].pop(0)))
            for h in hs:
                while pend[h]:
                    if not zf_done[h]:
                        sched.append(("zf", h))
                        zf_done[h] = True
                    sched.append(("av", h, pend[h].pop(0)))
        sched.append(("div", NH - 2))
        sched.append(("div", NH - 1))

        exp_pred = {}
        ca = C["act"]
        for key in exp_order:
            ca += 1
            exp_pred[key] = ca
        prev_s_tenant = {}
        lastp = {0: None, 1: None}
        for (h, st) in exp_order:
            p = h % 2
            if lastp[p] is not None:
                prev_s_tenant[(h, st)] = lastp[p]
            lastp[p] = (h, st)
        prev_pt_tenant = {}
        hist = {}
        for (h, st) in exp_order:
            hist.setdefault(h, []).append(st)
        for h, sts_h in hist.items():
            for i, st in enumerate(sts_h):
                if i >= 2:
                    prev_pt_tenant[(h, st)] = (h, sts_h[i - 2])
        den_pred, div_pred = {}, {}
        cd = C["dve"]
        for h in range(NH):
            cd += 1
            den_pred[h] = cd
            cd += 1
            div_pred[h] = cd
        kq_done, av_done, div_te = {}, {}, {}
        c_diag_last = diag_max[(QT - 1, 3)]
        c_ph_last = ph_stt[QT - 1]

        def p3b_te(t):
            wt("te", t, "s", c_phi_all)
            wt("te", t, "dve", c_zrow)
            for op in sched:
                kind = op[0]
                if kind == "kq":
                    _, h, st = op
                    g_head = h // 4
                    lo, hi = _vqt(h, st)
                    prev = prev_s_tenant.get((h, st))
                    if prev is not None:
                        wt("te", t, "act", exp_pred[prev])
                    else:
                        wt("te", t, "dve", c_diag_last)
                    diag_q = st - 8
                    edge_q = st if (KEPT[h] == 9 and st <= 7) else None
                    ins = None
                    for bank in range(2):
                        qlist = [qt for qt in range(lo, hi + 1) if qt // 4 == bank]
                        masked = [qt for qt in qlist
                                  if qt == diag_q or (edge_q is not None and qt == edge_q)]
                        normal = [qt for qt in qlist if qt not in masked]
                        for qt in normal:
                            ins = t.matmul(SP[h % 2].ap()[:, qt * 128:(qt + 1) * 128],
                                           khat.ap()[0:66, g_head * TOK2 + st * 128: g_head * TOK2 + st * 128 + 128],
                                           qhat.ap()[0:66, h * TOK + qt * 128: h * TOK + qt * 128 + 128],
                                           start=True, stop=True)
                        for qt in masked:
                            mt = mdiagT_t if qt == diag_q else medgeT_t
                            t.matmul(SP[h % 2].ap()[:, qt * 128:(qt + 1) * 128],
                                     mt.ap(), negI_t.ap(), start=True, stop=False)
                            ins = t.matmul(SP[h % 2].ap()[:, qt * 128:(qt + 1) * 128],
                                           khat.ap()[0:66, g_head * TOK2 + st * 128: g_head * TOK2 + st * 128 + 128],
                                           qhat.ap()[0:66, h * TOK + qt * 128: h * TOK + qt * 128 + 128],
                                           start=False, stop=True)
                    inc("te", ins, "te", 1)
                    kq_done[(h, st)] = C["te"]
                elif kind == "av":
                    _, h, st = op
                    g_head = h // 4
                    lo, hi = _vqt(h, st)
                    wt("te", t, "act", exp_pred[(h, st)])
                    ins = None
                    for qt in range(lo, hi + 1):
                        last = (st == qt + 8)
                        vcol = (st * NKV + g_head) * 65
                        ins = t.matmul(APm[h % 2].ap()[0:65, qt * 128:(qt + 1) * 128],
                                       vhat.ap()[:, vcol:vcol + 65],
                                       pt_ap(h, st, qt, qt),
                                       start=False, stop=last)
                    inc("te", ins, "te", 1)
                    av_done[(h, st)] = C["te"]
                elif kind == "zf":
                    _, h = op
                    wt("te", t, "dve", div_pred[h - 2] if h >= 2 else c_ph_last)
                    for bank in range(2):
                        t.matmul(APm[h % 2].ap()[0:65, bank * 512:bank * 512 + 512],
                                 onesB_t.ap()[0:1, 0:65], zrow.ap(),
                                 start=True, stop=False)
                else:
                    _, hd = op
                    wt("te", t, "dve", den_pred[hd])
                    ins = None
                    for ts in range(2):
                        t.matmul(APm[hd % 2].ap()[64:128, ts * 512:ts * 512 + 512],
                                 onesB_t.ap()[0:1, 0:64],
                                 stM.ap()[0:1, ts * 512:ts * 512 + 512],
                                 start=True, stop=False, tile_position=(0, 64))
                        ins = t.matmul(APm[hd % 2].ap()[64:128, ts * 512:ts * 512 + 512],
                                       onesB_t.ap()[0:1, 0:64],
                                       stM.ap()[0:1, 1024 + ts * 512: 1024 + ts * 512 + 512],
                                       start=False, stop=True, tile_position=(0, 64))
                    inc("te", ins, "te", 1)
                    div_te[hd] = C["te"]
        block.tensor(p3b_te)

        def p3b_act(sc):
            for (h, st) in exp_order:
                lo, hi = _vqt(h, st)
                wt("act", sc, "te", kq_done[(h, st)])
                pv = prev_pt_tenant.get((h, st))
                if pv is not None:
                    wt("act", sc, "te", av_done[pv])
                ins = sc.activation(
                    pt_ap(h, st, lo, hi),
                    SP[h % 2].ap()[:, lo * 128:(hi + 1) * 128],
                    AF.Exp, bias=sbias_t.ap()[:, h * STN + st: h * STN + st + 1],
                    scale=0.125)
                inc("act", ins, "act", 1)
                assert C["act"] == exp_pred[(h, st)]
        block.scalar(p3b_act)

        attnout = qtmp

        avE = sqscr

        def p3b_dve(v):
            for h in range(NH):
                sts_h = hist[h]
                wt("dve", v, "te", av_done[(h, sts_h[-1])])
                if h >= 1:
                    wt("dve", v, "te", div_te[h - 1])
                v.tensor_copy(stF.ap()[0:1, 0:TOK], APm[h % 2].ap()[64:65, :])
                v.tensor_copy(stM.ap()[0:1, 0:TOK], stF.ap()[0:1, 0:TOK])
                ins = v.tensor_tensor(stM.ap()[0:1, 1024:1024 + TOK],
                                      stF.ap()[0:1, 0:TOK],
                                      stM.ap()[0:1, 0:TOK], ALU.subtract)
                inc("dve", ins, "dve", 1)
                assert C["dve"] == den_pred[h]
                wt("dve", v, "te", div_te[h])
                ebuf = avE.ap()[:, (h % 2) * TOK:(h % 2) * TOK + TOK]
                rbuf = avE.ap()[64:128, (h % 2) * TOK:(h % 2) * TOK + TOK]
                v.reciprocal(stF.ap()[0:64, 0:TOK], APm[h % 2].ap()[64:128, :])
                ins = v.tensor_copy(ebuf[0:64, :], APm[h % 2].ap()[0:64, :])
                inc("dve", ins, "dve", 1)
                assert C["dve"] == div_pred[h]
                mc = h // 2
                if h % 2 == 0:
                    v.tensor_tensor(attnout.ap()[0:64, mc * TOK:(mc + 1) * TOK],
                                    ebuf[0:64, :], stF.ap()[0:64, 0:TOK], ALU.mult)
                else:
                    v.tensor_tensor(stR.ap()[0:64, 0:TOK],
                                    ebuf[0:64, :], stF.ap()[0:64, 0:TOK], ALU.mult)
                    v.tensor_copy(attnout.ap()[64:128, mc * TOK:(mc + 1) * TOK],
                                  stR.ap()[0:64, 0:TOK])
            ins = v.engine_nop()
            inc("dve", ins, "dve", 1)
        block.vector(p3b_dve)
        c_attnout = C["dve"]
        c_exp_last = exp_pred[exp_order[-1]]

        if "attnout" in taps:
            def tp_ao(sy):
                wt("sy", sy, "dve", c_attnout)
                inc("sy", sy.dma_start(out=tap_d["attnout"], in_=attnout.ap()), "s", 16)
            block.sync(tp_ao)

        nc.leave_named_scope(_scope_n, _scope_i, False)
        _scope_n = 'p4_wo'; _scope_i = nc.enter_named_scope('p4_wo', False)[0]
        qhat_g.__exit__(None, None, None)
        a2_g = nc.sbuf_tensor("a2", [128, DC * TOK], BF16)
        a2 = a2_g.__enter__()
        ot_g = nc.sbuf_tensor("ot_t", [128, TOK], F32, side="right")
        ot_t = ot_g.__enter__()
        ot2_g = nc.sbuf_tensor("ot_t2", [128, TOK], F32, side="right")
        ot_t2 = ot2_g.__enter__()
        xrs_g = nc.sbuf_tensor("xrs", [128, TOK], F32, side="right")
        xrs = xrs_g.__enter__()
        xrs2_g = nc.sbuf_tensor("xrs2", [128, TOK], F32, side="right")
        xrs2 = xrs2_g.__enter__()
        x2bf_g = nc.sbuf_tensor("x2bf", [128, DC * TOK], BF16, side="right")
        x2bf = x2bf_g.__enter__()

        wo_te, wo_ev = proj_sweep(
            DC, wo_d, TOK,
            lambda kc, off: attnout.ap()[:, kc * TOK + off: kc * TOK + off + 512],
            lambda v, mc, half, ps: v.tensor_copy(a2.ap()[:, mc * TOK:(mc + 1) * TOK], ps),
            pre_te_dve=c_attnout, pre_te_act=c_exp_last)

        c_ln2 = ln_stats(DC, TOK, lambda mc: a2.ap()[:, mc * TOK:(mc + 1) * TOK],
                         LN_EPS, chunk_wait=("dve", lambda mc: wo_ev[mc]),
                         mm_pre_dve=wo_ev[-1])

        if "a2" in taps:
            def tp_a2(sy):
                wt("sy", sy, "dve", wo_ev[-1])
                inc("sy", sy.dma_start(out=tap_d["a2"], in_=a2.ap()), "s", 16)
            block.sync(tp_a2)

        x2_dve_pred = [C["dve"] + mc + 1 for mc in range(DC)]
        cs = C["s"]
        ld_pred, x2_st_pred = {}, {}
        for mc in range(DC):
            cs += 16
            ld_pred[mc] = cs
            if mc >= 1:
                cs += 16
                x2_st_pred[mc - 1] = cs
        cs += 16
        x2_st_pred[DC - 1] = cs

        def p4b_sync(sy):
            for mc in range(DC):
                if mc >= 2:
                    wt("sy", sy, "dve", x2_dve_pred[mc - 2])
                ins = sy.dma_start(out=(xrs if mc % 2 == 0 else xrs2).ap(),
                                   in_=xt[mc * 128:(mc + 1) * 128, 1024:2048])
                inc("sy", ins, "s", 16)
                assert C["s"] == ld_pred[mc]
                if mc >= 1:
                    wt("sy", sy, "dve", x2_dve_pred[mc - 1])
                    ins = sy.dma_start(out=x2s[(mc - 1) * 128:mc * 128, :],
                                       in_=(ot_t if (mc - 1) % 2 == 0 else ot_t2).ap())
                    inc("sy", ins, "s", 16)
                    assert C["s"] == x2_st_pred[mc - 1]
            wt("sy", sy, "dve", x2_dve_pred[DC - 1])
            ins = sy.dma_start(out=x2s[(DC - 1) * 128:DC * 128, :],
                               in_=(ot_t if (DC - 1) % 2 == 0 else ot_t2).ap())
            inc("sy", ins, "s", 16)
            assert C["s"] == x2_st_pred[DC - 1]
        block.sync(p4b_sync)

        x2bf_cast_pred = [c_ln2 + mc + 1 for mc in range(DC)]

        def p4b_dve(v):
            wt("dve", v, "act", c_ln2)
            for mc in range(DC):
                wt("dve", v, "s", ld_pred[mc])
                if mc >= 2:
                    wt("dve", v, "s", x2_st_pred[mc - 2])
                    wt("dve", v, "act", x2bf_cast_pred[mc - 2])
                buf = (xrs if mc % 2 == 0 else xrs2).ap()
                ob = (ot_t if mc % 2 == 0 else ot_t2).ap()
                a2c = a2.ap()[:, mc * TOK:(mc + 1) * TOK]
                v.tensor_tensor(ob, a2c, stM.ap()[:, 0:TOK], ALU.subtract)
                v.tensor_tensor(ob, ob, stR.ap()[:, 0:TOK], ALU.mult)
                ins = v.tensor_tensor(ob, ob, buf, ALU.add)
                inc("dve", ins, "dve", 1)
                assert C["dve"] == x2_dve_pred[mc]
        block.vector(p4b_dve)
        c_x2 = C["dve"]

        def p4b_cast(sc):
            for mc in range(DC):
                wt("act", sc, "dve", x2_dve_pred[mc])
                ob = (ot_t if mc % 2 == 0 else ot_t2).ap()
                ins = sc.activation(x2bf.ap()[:, mc * TOK:(mc + 1) * TOK], ob, AF.Copy)
                assert inc("act", ins, "act", 1) == x2bf_cast_pred[mc]
        block.scalar(p4b_cast)

        if "x2" in taps:
            def tp_x2(sy):
                wt("sy", sy, "s", x2_st_pred[DC - 1])
                inc("sy", sy.dma_start(out=tap_d["x2"], in_=x2s), "s", 16)
            block.sync(tp_x2)



        nc.leave_named_scope(_scope_n, _scope_i, False)
        _scope_n = 'p5_ffn'; _scope_i = nc.enter_named_scope('p5_ffn', False)[0]
        a2_g.__exit__(None, None, None)
        vhat_g.__exit__(None, None, None)
        khat_g.__exit__(None, None, None)
        xn3_g = nc.sbuf_tensor("xn3", [128, DC * TOK], BF16)
        xn3 = xn3_g.__enter__()

        c_ln3 = ln_stats(DC, TOK, lambda mc: x2bf.ap()[:, mc * TOK:(mc + 1) * TOK],
                         LN_EPS, chunk_wait=("act", lambda mc: x2bf_cast_pred[mc]))

        def p5_xn3(v):
            wt("dve", v, "act", c_ln3)
            for mc in range(DC):
                sl = slice(mc * TOK, (mc + 1) * TOK)
                v.tensor_tensor(xn3.ap()[:, sl], x2bf.ap()[:, sl], stM.ap()[:, 0:TOK], ALU.subtract)
                ins = v.tensor_tensor(xn3.ap()[:, sl], xn3.ap()[:, sl], stR.ap()[:, 0:TOK], ALU.mult)
                inc("dve", ins, "dve", 1)
        block.vector(p5_xn3)
        c_xn3 = C["dve"]

        if "xn3" in taps:
            def tp_x3(sy):
                wt("sy", sy, "dve", c_xn3)
                inc("sy", sy.dma_start(out=tap_d["xn3"], in_=xn3.ap()), "s", 16)
            block.sync(tp_x3)

        x2bf_g.__exit__(None, None, None)
        xrs2_g.__exit__(None, None, None)
        xrs_g.__exit__(None, None, None)
        ot2_g.__exit__(None, None, None)
        ot_g.__exit__(None, None, None)

        hbuf2_g = nc.sbuf_tensor("hbuf2", [128, (ICN - DC) * TOK], BF16)
        hbuf2 = hbuf2_g.__enter__()
        wbufC_g = nc.sbuf_tensor("wbufC", [128, DC * 128], BF16)
        wbufC = wbufC_g.__enter__()
        wbufD_g = nc.sbuf_tensor("wbufD", [128, DC * 128], BF16)
        wbufD = wbufD_g.__enter__()
        wdb1_g = nc.sbuf_tensor("wdbuf", [128, 22 * 128], BF16)
        wdbuf = wdb1_g.__enter__()
        wdb2_g = nc.sbuf_tensor("wdbuf2", [128, 22 * 128], BF16)
        wdbuf2 = wdb2_g.__enter__()
        wdb3_g = nc.sbuf_tensor("wdbuf3", [128, 22 * 128], BF16)
        wdbuf3 = wdb3_g.__enter__()
        WDB = [wdbuf, wdbuf2, wdbuf3]

        def h_ap(ic):
            if ic < DC:
                return attnout.ap()[:, ic * TOK:(ic + 1) * TOK]
            return hbuf2.ap()[:, (ic - DC) * TOK:(ic - DC + 1) * TOK]

        guW0 = C["w"]
        gu_te = []
        gu_silu_pred = [C["act"] + ic + 1 for ic in range(ICN)]
        gu_h_pred = [C["dve"] + 2 * (ic + 1) for ic in range(ICN)]

        def p5_gu_te(t):
            wt("te", t, "dve", c_xn3)
            for ic in range(ICN):
                wt("te", t, "w", guW0 + (ic + 1) * 32)
                if ic >= 2:
                    wt("te", t, "act", gu_silu_pred[ic - 2])
                    wt("te", t, "dve", gu_h_pred[ic - 2])
                gps, ups = SP[ic % 2], APm[ic % 2]
                gbuf = wbufA if ic % 2 == 0 else wbufB
                ubuf = wbufC if ic % 2 == 0 else wbufD
                insu = None
                for kc in range(DC):
                    for ts in range(2):
                        mv = xn3.ap()[:, kc * TOK + ts * 512: kc * TOK + ts * 512 + 512]
                        t.matmul(gps.ap()[:, ts * 512:ts * 512 + 512],
                                 gbuf.ap()[:, kc * 128:(kc + 1) * 128], mv,
                                 start=(kc == 0), stop=(kc == DC - 1))
                        insu = t.matmul(ups.ap()[:, ts * 512:ts * 512 + 512],
                                        ubuf.ap()[:, kc * 128:(kc + 1) * 128], mv,
                                        start=(kc == 0), stop=(kc == DC - 1))
                inc("te", insu, "te", 1)
                gu_te.append(C["te"])
        block.tensor(p5_gu_te)

        def p5_gu_dma(g):
            wt("pool", g, "dve", c_xn3)
            for ic in range(ICN):
                if ic >= 2:
                    wt("pool", g, "te", gu_te[ic - 2])
                else:
                    wt("pool", g, "te", max(buf_last["A"], buf_last["B"]))
                gbuf = wbufA if ic % 2 == 0 else wbufB
                ubuf = wbufC if ic % 2 == 0 else wbufD
                inc("pool", g.dma_start(out=gbuf.ap(), in_=wg_d[ic]), "w", 16)
                inc("pool", g.dma_start(out=ubuf.ap(), in_=wu_d[ic]), "w", 16)
        block.gpsimd(p5_gu_dma)

        def p5_gu_act(sc):
            for ic in range(ICN):
                wt("act", sc, "te", gu_te[ic])
                ins = sc.activation(h_ap(ic), SP[ic % 2].ap(),
                                    AF.Silu, scale=gs_t.ap()[:, ic:ic + 1])
                inc("act", ins, "act", 1)
                assert C["act"] == gu_silu_pred[ic]
        block.scalar(p5_gu_act)

        def p5_gu_dve(v):
            for ic in range(ICN):
                wt("dve", v, "te", gu_te[ic])
                wt("dve", v, "act", gu_silu_pred[ic])
                ins = v.tensor_scalar(sqscr.ap()[:, 0:TOK], APm[ic % 2].ap(),
                                      us_t.ap()[:, ic:ic + 1], None, ALU.mult)
                inc("dve", ins, "dve", 1)
                ins = v.tensor_tensor(h_ap(ic), h_ap(ic), sqscr.ap()[:, 0:TOK], ALU.mult)
                inc("dve", ins, "dve", 1)
                assert C["dve"] == gu_h_pred[ic]
        block.vector(p5_gu_dve)
        c_h_done = C["dve"]

        if "hb1" in taps:
            def tp_hb(sy):
                wt("sy", sy, "dve", c_h_done)
                inc("sy", sy.dma_start(out=tap_d["hb1"], in_=attnout.ap()), "s", 16)
                inc("sy", sy.dma_start(out=tap_d["hb2"], in_=hbuf2.ap()), "s", 16)
            block.sync(tp_hb)

        dnW0 = C["w"]
        dn_te = []
        dn_half = []
        dn_ev_pred = [C["dve"] + mc + 1 for mc in range(DC)]

        def p5_dn_te(t):
            wt("te", t, "dve", c_h_done)
            wt("te", t, "act", gu_silu_pred[-1])
            for mc in range(DC):
                if mc >= 2:
                    wt("te", t, "dve", dn_ev_pred[mc - 2])
                ps = SP[mc % 2]
                ins = None
                for half in range(2):
                    n = 2 * mc + half
                    wt("te", t, "w", dnW0 + (n + 1) * 16)
                    buf = WDB[n % 3]
                    for ici in range(22):
                        ic = half * 22 + ici
                        for ts in range(2):
                            ins = t.matmul(ps.ap()[:, ts * 512:ts * 512 + 512],
                                           buf.ap()[:, ici * 128:(ici + 1) * 128],
                                           h_ap(ic)[:, ts * 512:ts * 512 + 512],
                                           start=(ic == 0), stop=(ic == ICN - 1))
                    inc("te", ins, "te", 1)
                    dn_half.append(C["te"])
                dn_te.append(C["te"])
        block.tensor(p5_dn_te)

        def p5_dn_dma(g):
            wt("pool", g, "dve", c_xn3)
            for n in range(2 * DC):
                if n >= 3:
                    wt("pool", g, "te", dn_half[n - 3])
                buf = WDB[n % 3]
                half = n % 2
                inc("pool", g.dma_start(out=buf.ap(),
                                        in_=wd_d[n // 2][:, half * 22 * 128:(half + 1) * 22 * 128]),
                    "w", 16)
        block.gpsimd(p5_dn_dma)

        d4r = xn3

        def p5_dn_ev(v):
            for mc in range(DC):
                wt("dve", v, "te", dn_te[mc])
                ins = v.tensor_scalar(d4r.ap()[:, mc * TOK:(mc + 1) * TOK],
                                      SP[mc % 2].ap(), ds_t.ap()[:, mc:mc + 1], None, ALU.mult)
                inc("dve", ins, "dve", 1)
                assert C["dve"] == dn_ev_pred[mc]
        block.vector(p5_dn_ev)

        if "d4" in taps:
            def tp_d4(sy):
                wt("sy", sy, "dve", dn_ev_pred[-1])
                inc("sy", sy.dma_start(
                    out=tap_d["d4"].rearrange("(c p) t -> p c t", p=128),
                    in_=d4r.ap().rearrange("p (c t) -> p c t", t=TOK)), "s", 16)
            block.sync(tp_d4)

        nc.leave_named_scope(_scope_n, _scope_i, False)
        _scope_n = 'p6_out'; _scope_i = nc.enter_named_scope('p6_out', False)[0]
        wdb3_g.__exit__(None, None, None)
        wdb2_g.__exit__(None, None, None)
        wdb1_g.__exit__(None, None, None)
        wbufD_g.__exit__(None, None, None)
        wbufC_g.__exit__(None, None, None)
        hbuf2_g.__exit__(None, None, None)
        c_ln4 = ln_stats(DC, TOK, lambda mc: d4r.ap()[:, mc * TOK:(mc + 1) * TOK],
                         LN_EPS, chunk_wait=("dve", lambda mc: dn_ev_pred[mc]))

        fx_g = nc.sbuf_tensor("fx", [128, TOK], F32, side="right")
        fx = fx_g.__enter__()
        fx2_g = nc.sbuf_tensor("fx2", [128, TOK], F32, side="right")
        fx2 = fx2_g.__enter__()
        fot_g = nc.sbuf_tensor("fot", [128, TOK], F32, side="right")
        fot = fot_g.__enter__()
        fot2_g = nc.sbuf_tensor("fot2", [128, TOK], F32, side="right")
        fot2 = fot2_g.__enter__()
        FXB = [fx, fx2]

        fin_dve_pred = [C["dve"] + mc + 1 for mc in range(DC)]
        cs = C["s"]
        fin_ld_pred, out_st_pred = {}, {}
        for mc in range(DC):
            cs += 16
            fin_ld_pred[mc] = cs
            if mc >= 3:
                cs += 16
                out_st_pred[mc - 3] = cs
        for mc in range(DC - 3, DC):
            cs += 16
            out_st_pred[mc] = cs

        def p6b_sync(sy):
            wt("sy", sy, "s", x2_st_pred[DC - 1])
            wt("sy", sy, "act", x2bf_cast_pred[DC - 1])
            wt("sy", sy, "dve", x2_dve_pred[DC - 1])
            for mc in range(DC):
                if mc >= 2:
                    wt("sy", sy, "dve", fin_dve_pred[mc - 2])
                ins = sy.dma_start(out=FXB[mc % 2].ap(),
                                   in_=x2s[mc * 128:(mc + 1) * 128, :])
                inc("sy", ins, "s", 16)
                assert C["s"] == fin_ld_pred[mc]
                if mc >= 3:
                    wt("sy", sy, "dve", fin_dve_pred[mc - 3])
                    ins = sy.dma_start(out=out_d[(mc - 3) * 128:(mc - 2) * 128, :],
                                       in_=(fot if (mc - 3) % 2 == 0 else fot2).ap())
                    inc("sy", ins, "s", 16)
                    assert C["s"] == out_st_pred[mc - 3]
            for mc in range(DC - 3, DC):
                wt("sy", sy, "dve", fin_dve_pred[mc])
                ins = sy.dma_start(out=out_d[mc * 128:(mc + 1) * 128, :],
                                   in_=(fot if mc % 2 == 0 else fot2).ap())
                inc("sy", ins, "s", 16)
                assert C["s"] == out_st_pred[mc]
        block.sync(p6b_sync)

        def p6b_dve(v):
            wt("dve", v, "act", c_ln4)
            for mc in range(DC):
                wt("dve", v, "s", fin_ld_pred[mc])
                if mc >= 2:
                    wt("dve", v, "s", out_st_pred[mc - 2])
                bufx = FXB[mc % 2].ap()
                ob = (fot if mc % 2 == 0 else fot2).ap()
                v.tensor_tensor(ob, d4r.ap()[:, mc * TOK:(mc + 1) * TOK],
                                stM.ap()[:, 0:TOK], ALU.subtract)
                v.tensor_tensor(ob, ob, stR.ap()[:, 0:TOK], ALU.mult)
                ins = v.tensor_tensor(ob, ob, bufx, ALU.add)
                inc("dve", ins, "dve", 1)
                assert C["dve"] == fin_dve_pred[mc]
        block.vector(p6b_dve)



        nc.leave_named_scope(_scope_n, _scope_i, False)

        def p_final(sy):
            wt("sy", sy, "s", C["s"])
            wt("sy", sy, "w", C["w"])
        block.sync(p_final)

    L.audit()
    def _rng(n):
        ml = nc.lookup_mloc(n)
        return (ml.addr, ml.addr + ml.size() // 128)
    live_p5 = [_rng(n) for n in ("xn3", "hbuf2", "wbufC", "wbufD", "wdbuf", "wdbuf2", "qtmp")]
    for n in ("fx", "fx2", "fot", "fot2"):
        a, b = _rng(n)
        for (c0, c1) in live_p5:
            assert b <= c0 or a >= c1, (n, (a, b), (c0, c1))
    return nc


_BUILD_CACHE = {}


def get_nc(taps=()):
    key = tuple(sorted(taps))
    if key not in _BUILD_CACHE:
        _BUILD_CACHE[key] = build_nc(taps)
    return _BUILD_CACHE[key]


def kernel(**inputs):
    maps = _stage_inputs(inputs)
    nc = get_nc()
    res = run_bass_kernel_spmd(nc, maps, core_ids=list(range(8)))
    return _gather_output(res.results)


# version 07
# speedup vs baseline: 1.0488x; 1.0488x over previous
import numpy as np
import ml_dtypes
import concourse.bass as bass
import concourse.mybir as mybir
from concourse.bass_utils import run_bass_kernel_spmd

F32 = mybir.dt.float32
BF16 = mybir.dt.bfloat16
AF = mybir.ActivationFunctionType
ALU = mybir.AluOpType
AX = mybir.AxisListType

DIM, DC = 2048, 16
NH, NKV, HD = 32, 8, 64
TOK, TOK2 = 1024, 2048
QT, STN = 8, 16
INTER, ICN = 5632, 44
KMC = 4
LN_EPS, RMS_EPS = 1e-5, 1e-6
NEGBIG = -1.0e30
P3B_BG_ON_DVE = False
PT_LAG = 3
LN1_SQ2 = True

SLOPES = [2.0 ** (-h / 4.0) for h in range(NH)]
CUT = 180.0
KEPT = [min(9, 2 + int(CUT / SLOPES[h]) // 128) for h in range(NH)]
for _h in range(NH):
    if KEPT[_h] < 9:
        assert SLOPES[_h] * (128 * KEPT[_h] - 127) >= CUT - 1e-6


def _vqt(h, st):
    lo, hi = max(0, st - 8), min(7, st + KEPT[h] - 9)
    return (lo, hi) if lo <= hi else None


def _st_list(h):
    return [st for st in range(STN) if _vqt(h, st) is not None]



def _tile_kmajor(W, n_kc, n_m):
    W = np.asarray(W, dtype=np.float32)
    Wr = W.reshape(n_m, 128, n_kc, 128)
    return np.ascontiguousarray(Wr.transpose(0, 3, 2, 1)).reshape(n_m, 128, n_kc * 128)


def _make_tables(Q0):
    sbias = np.empty((128, NH * STN), np.float32)
    S0 = Q0 - 1024
    for h in range(NH):
        for st in range(STN):
            s_glob = S0 + st * 128 + np.arange(128)
            if s_glob[0] < 0 or s_glob[0] >= 2048:
                sbias[:, h * STN + st] = NEGBIG
            else:
                sbias[:, h * STN + st] = (SLOPES[h] * s_glob).astype(np.float32)
    phiq8 = np.empty((32, QT * 128), np.float32)
    q_glob = Q0 + np.arange(QT * 128)
    for h in range(NH):
        phiq8[h, :] = (8.0 * SLOPES[h] * q_glob).astype(np.float32)
    return sbias, phiq8


def _stage_inputs(inputs):
    x = np.asarray(inputs["L0_attn_x"], np.float32)
    wo = np.asarray(inputs["L0_attn_wo"], np.float32)
    perm = np.empty(2048, np.int64)
    for h in range(NH):
        fp = (h // 2) * 128 + (h % 2) * 64
        perm[fp:fp + 64] = h * 64 + np.arange(64)

    def _fm(v, n):
        return np.ascontiguousarray(np.asarray(v, np.float32).reshape(n, 128).T)
    shared = dict(
        wq=_tile_kmajor(inputs["L0_attn_wq"], DC, DC),
        wk=_tile_kmajor(inputs["L0_attn_wk"], DC, KMC),
        wv=_tile_kmajor(inputs["L0_attn_wv"], DC, KMC),
        wo=_tile_kmajor(wo[:, perm], DC, DC),
        wg=_tile_kmajor(inputs["L0_ffn_gate_w"], DC, ICN),
        wu=_tile_kmajor(inputs["L0_ffn_up_w"], DC, ICN),
        wd=_tile_kmajor(inputs["L0_ffn_down_w"], ICN, DC),
        qnw=_fm(inputs["L0_attn_q_norm_w"], DC),
        knw=_fm(inputs["L0_attn_k_norm_w"], KMC),
        gs=_fm(inputs["L0_ffn_gate_s"], ICN),
        us=_fm(inputs["L0_ffn_up_s"], ICN),
        ds=_fm(inputs["L0_ffn_down_s"], DC))
    maps = []
    for c in range(8):
        b, hh = c // 2, c % 2
        xt = np.ascontiguousarray(x[b].T)
        if hh == 0:
            xcore = np.concatenate([np.zeros((DIM, 1024), np.float32), xt[:, 0:1024]], 1)
        else:
            xcore = xt
        sbias, phiq8 = _make_tables(hh * 1024)
        m = dict(shared)
        m["xt"] = np.ascontiguousarray(xcore)
        m["sbias"] = sbias
        m["phiq8"] = phiq8
        maps.append(m)
    return maps


def _gather_output(results):
    out = np.empty((4, 2048, 2048), np.float32)
    for c in range(8):
        b, hh = c // 2, c % 2
        out[b, hh * 1024:(hh + 1) * 1024, :] = results[c]["out"].T
    return out



class SemLog:
    def __init__(self):
        self.ev = {e: [] for e in ("te", "act", "dve", "pool", "sy")}

    def wait(self, eng, ch, val):
        self.ev[eng].append(("w", ch, val))

    def inc(self, eng, ch, val_after):
        self.ev[eng].append(("i", ch, val_after))

    def audit(self):
        pos = {e: 0 for e in self.ev}
        cur = {}
        progress = True
        while progress:
            progress = False
            for e, lst in self.ev.items():
                while pos[e] < len(lst):
                    kind, ch, val = lst[pos[e]]
                    if kind == "i":
                        cur[ch] = max(cur.get(ch, 0), val)
                    elif cur.get(ch, 0) < val:
                        break
                    pos[e] += 1
                    progress = True
        stuck = {e: (pos[e], len(lst)) for e, lst in self.ev.items() if pos[e] < len(lst)}
        if stuck:
            det = {e: self.ev[e][pos[e]] for e in stuck}
            raise RuntimeError(f"SEM AUDIT DEADLOCK: {stuck} next={det}")


_LAST_LOG = None


def build_nc(taps=()):
    nc = bass.Bass("TRN2", target_bir_lowering=False)
    taps = set(taps)

    xt = nc.dram_tensor("xt", [DIM, TOK2], F32, kind="ExternalInput").ap()
    wq_d = nc.dram_tensor("wq", [DC, 128, DC * 128], F32, kind="ExternalInput").ap()
    wk_d = nc.dram_tensor("wk", [KMC, 128, DC * 128], F32, kind="ExternalInput").ap()
    wv_d = nc.dram_tensor("wv", [KMC, 128, DC * 128], F32, kind="ExternalInput").ap()
    wo_d = nc.dram_tensor("wo", [DC, 128, DC * 128], F32, kind="ExternalInput").ap()
    wg_d = nc.dram_tensor("wg", [ICN, 128, DC * 128], F32, kind="ExternalInput").ap()
    wu_d = nc.dram_tensor("wu", [ICN, 128, DC * 128], F32, kind="ExternalInput").ap()
    wd_d = nc.dram_tensor("wd", [DC, 128, ICN * 128], F32, kind="ExternalInput").ap()
    qnw_d = nc.dram_tensor("qnw", [128, DC], F32, kind="ExternalInput").ap()
    knw_d = nc.dram_tensor("knw", [128, KMC], F32, kind="ExternalInput").ap()
    gs_d = nc.dram_tensor("gs", [128, ICN], F32, kind="ExternalInput").ap()
    us_d = nc.dram_tensor("us", [128, ICN], F32, kind="ExternalInput").ap()
    ds_d = nc.dram_tensor("ds", [128, DC], F32, kind="ExternalInput").ap()
    sbias_d = nc.dram_tensor("sbias", [128, NH * STN], F32, kind="ExternalInput").ap()
    phiq8_d = nc.dram_tensor("phiq8", [32, QT * 128], F32, kind="ExternalInput").ap()
    out_d = nc.dram_tensor("out", [DIM, TOK], F32, kind="ExternalOutput").ap()

    x2s = nc.dram_tensor("x2s", [DIM, TOK], F32, kind="Internal").ap()
    d4s = nc.dram_tensor("d4s", [DIM, TOK], BF16, kind="Internal").ap()
    phis = nc.dram_tensor("phis", [QT, 32, 2, 128], BF16, kind="Internal").ap()

    identF_d = nc.inline_tensor(np.eye(128, dtype=np.float32), "identF").ap()
    onesB_d = nc.inline_tensor(np.ones((128, 128), ml_dtypes.bfloat16), "onesB").ap()
    mdiagT_d = nc.inline_tensor(np.triu(np.ones((128, 128), np.float32), 1).astype(ml_dtypes.bfloat16), "mdiagT").ap()
    medgeT_d = nc.inline_tensor(np.tril(np.ones((128, 128), np.float32), -1).astype(ml_dtypes.bfloat16), "medgeT").ap()
    negI_d = nc.inline_tensor((NEGBIG * np.eye(128, dtype=np.float32)).astype(ml_dtypes.bfloat16), "negI").ap()

    tap_d = {}
    for name, shape, dt_ in (
            ("xn", [128, DC * TOK2], BF16), ("khat", [66, NKV * TOK2], BF16),
            ("qhat", [66, NH * TOK], BF16), ("vhat", [128, STN * NKV * 65], BF16),
            ("attnout", [128, DC * TOK], BF16), ("a2", [128, DC * TOK], BF16),
            ("x2", [DIM, TOK], F32), ("xn3", [128, DC * TOK], BF16),
            ("d4", [DIM, TOK], BF16), ("hb1", [128, DC * TOK], BF16),
            ("hb2", [128, (ICN - DC) * TOK], BF16)):
        if name in taps:
            tap_d[name] = nc.dram_tensor("tap_" + name, shape, dt_, kind="ExternalOutput").ap()

    global _LAST_LOG
    L = SemLog()
    _LAST_LOG = L
    C = {"w": 0, "s": 0, "te": 0, "act": 0, "dve": 0, "g": 0}

    with (
        nc.semaphore("sW") as sW,
        nc.semaphore("sS") as sS,
        nc.semaphore("sT") as sT,
        nc.semaphore("sA") as sA,
        nc.semaphore("sV") as sV,
        nc.semaphore("sG") as sG,
        nc.Block() as block,
    ):
        SEM = {"w": sW, "s": sS, "te": sT, "act": sA, "dve": sV, "g": sG}

        def inc(engname, instr, ch, n=1):
            C[ch] += n
            instr.then_inc(SEM[ch], n)
            L.inc(engname, ch, C[ch])
            return C[ch]

        def wt(engname, eng, ch, val):
            if val is None or val <= 0:
                return
            L.wait(engname, ch, val)
            eng.wait_ge(SEM[ch], val)

        sb = nc.alloc_sbuf_tensor
        sbias_t = sb("sbias_t", [128, NH * STN], F32)
        phiq8_t = sb("phiq8_t", [32, QT * 128], F32)
        identF_t = sb("identF_t", [128, 128], F32)
        onesB_t = sb("onesB_t", [128, 128], BF16)
        mdiagT_t = sb("mdiagT_t", [128, 128], BF16)
        medgeT_t = sb("medgeT_t", [128, 128], BF16)
        negI_t = sb("negI_t", [128, 128], BF16)
        qnw_t = sb("qnw_t", [128, DC], F32)
        knw_t = sb("knw_t", [128, KMC], F32)
        gs_t = sb("gs_t", [128, ICN], F32)
        us_t = sb("us_t", [128, ICN], F32)
        ds_t = sb("ds_t", [128, DC], F32)
        wbufA = sb("wbufA", [128, DC * 128], BF16)
        wbufB = sb("wbufB", [128, DC * 128], BF16)
        stF = sb("stF", [128, TOK2], F32)
        stM = sb("stM", [128, TOK2], BF16)
        stR = sb("stR", [128, TOK2], BF16)
        sqscr = sb("sqscr", [128, TOK2], BF16)
        m0buf = sb("m0buf", [128, 32], F32)
        phisc = sb("phisc", [32, 2 * 128], BF16)
        phiA = sb("phiA", [32, 128], F32)
        pt_t = sb("pt_t", [128, 6 * QT * 128], BF16)
        zrow = sb("zrow", [1, 512], BF16)
        gptmp = sb("gptmp", [128, TOK], BF16)

        buf_last = {"A": 0, "B": 0}

        S0 = nc.alloc_psum_tensor("S0", [128, 1024], F32)
        S1 = nc.alloc_psum_tensor("S1", [128, 1024], F32)
        A0 = nc.alloc_psum_tensor("A0", [128, 1024], F32)
        A1 = nc.alloc_psum_tensor("A1", [128, 1024], F32)
        SP, APm = [S0, S1], [A0, A1]

        def p0(sy):
            for dst, src in ((sbias_t, sbias_d), (phiq8_t, phiq8_d), (identF_t, identF_d),
                             (onesB_t, onesB_d), (mdiagT_t, mdiagT_d), (medgeT_t, medgeT_d),
                             (negI_t, negI_d), (qnw_t, qnw_d), (knw_t, knw_d),
                             (gs_t, gs_d), (us_t, us_d), (ds_t, ds_d)):
                inc("sy", sy.dma_start(out=dst.ap(), in_=src), "s", 16)
        block.sync(p0)
        c_consts = C["s"]
        block.tensor(lambda t: wt("te", t, "s", c_consts))
        block.scalar(lambda s_: wt("act", s_, "s", c_consts))
        def _dve_init(v):
            wt("dve", v, "s", c_consts)
            inc("dve", v.memset(zrow.ap(), 0.0), "dve", 1)
        block.vector(_dve_init)
        c_zrow = C["dve"]

        def ln_stats(nchunks, ntok, src_fn, eps, rms_denom=None,
                     pre_wait_dve=None, chunk_wait=None, mm_pre_dve=None,
                     sq2=None):
            nh = ntok // 1024
            sq_pred = [C["act"] + i + 1 for i in range(nchunks)]
            te0 = C["te"]
            rms = rms_denom is not None
            denom = float(rms_denom if rms else DIM)
            if ntok == 1024 and sq2 is None:
                sq_bufs = [sqscr.ap()[:, 0:1024], sqscr.ap()[:, 1024:2048]]
            elif sq2 is not None:
                sq_bufs = [sqscr.ap()[:, 0:ntok], sq2]
            else:
                sq_bufs = [sqscr.ap()[:, 0:ntok]]
            nb = len(sq_bufs)

            def _sq(sc):
                if pre_wait_dve is not None:
                    wt("act", sc, "dve", pre_wait_dve)
                for mc in range(nchunks):
                    if chunk_wait is not None:
                        ch, fn = chunk_wait
                        wt("act", sc, ch, fn(mc))
                    if mc >= nb:
                        wt("act", sc, "te", te0 + mc - nb + 1)
                    ins = sc.activation(sq_bufs[mc % nb], src_fn(mc), AF.Square)
                    inc("act", ins, "act", 1)
            block.scalar(_sq)

            def _mm(t):
                if mm_pre_dve is not None:
                    wt("te", t, "dve", mm_pre_dve)
                for mc in range(nchunks):
                    wt("te", t, "act", sq_pred[mc])
                    ins = None
                    sqb = sq_bufs[mc % nb]
                    for half in range(nh):
                        for ts in range(2):
                            sl = half * 1024 + ts * 512
                            psl = slice(ts * 512, ts * 512 + 512)
                            if not rms:
                                t.matmul(APm[half].ap()[:, psl], onesB_t.ap(),
                                         src_fn(mc)[:, sl:sl + 512],
                                         start=(mc == 0), stop=(mc == nchunks - 1))
                                ins = t.matmul(SP[half].ap()[:, psl], onesB_t.ap(),
                                               sqb[:, sl:sl + 512],
                                               start=(mc == 0), stop=(mc == nchunks - 1))
                            else:
                                ins = t.matmul(APm[half].ap()[:, psl], onesB_t.ap(),
                                               sqb[:, sl:sl + 512],
                                               start=(mc == 0), stop=(mc == nchunks - 1))
                    inc("te", ins, "te", 1)
            block.tensor(_mm)
            c_stat = C["te"]

            def _mr(v):
                wt("dve", v, "te", c_stat)
                for half in range(nh):
                    hs = slice(half * 1024, half * 1024 + 1024)
                    if not rms:
                        v.tensor_scalar(stM.ap()[:, hs], APm[half].ap(), 1.0 / denom, None, ALU.mult)
                        v.tensor_scalar(stF.ap()[:, hs], SP[half].ap(), 1.0 / denom, None, ALU.mult)
                    else:
                        v.tensor_scalar(stF.ap()[:, hs], APm[half].ap(), 1.0 / denom, None, ALU.mult)
                if not rms:
                    v.tensor_tensor(sqscr.ap()[:, 0:ntok], stM.ap()[:, 0:ntok], stM.ap()[:, 0:ntok], ALU.mult)
                    v.tensor_tensor(stF.ap()[:, 0:ntok], stF.ap()[:, 0:ntok], sqscr.ap()[:, 0:ntok], ALU.subtract)
                v.tensor_scalar(stF.ap()[:, 0:ntok], stF.ap()[:, 0:ntok], float(eps), None, ALU.add)
                ins = v.reciprocal(stF.ap()[:, 0:ntok], stF.ap()[:, 0:ntok])
                inc("dve", ins, "dve", 1)
            block.vector(_mr)
            c_r1 = C["dve"]

            def _sqrt(sc):
                wt("act", sc, "dve", c_r1)
                ins = sc.activation(stR.ap()[:, 0:ntok], stF.ap()[:, 0:ntok], AF.Sqrt)
                inc("act", ins, "act", 1)
            block.scalar(_sqrt)
            return C["act"]

        def proj_sweep(n_mc, src_w, ntok, mov_fn, evac_fn,
                       pre_te_dve=None, pre_te_act=None, pre_te_g=None):
            nh = ntok // 1024
            w0 = C["w"]
            te_done = []
            evac_pred = [C["dve"] + i + 1 for i in range(n_mc * nh)]

            def _te(t):
                if pre_te_dve is not None:
                    wt("te", t, "dve", pre_te_dve)
                if pre_te_act is not None:
                    wt("te", t, "act", pre_te_act)
                if pre_te_g is not None:
                    wt("te", t, "g", pre_te_g)
                n = 0
                for mc in range(n_mc):
                    wt("te", t, "w", w0 + (mc + 1) * 16)
                    buf = wbufA if mc % 2 == 0 else wbufB
                    for half in range(nh):
                        if n >= 2:
                            wt("te", t, "dve", evac_pred[n - 2])
                        ps = SP[n % 2]
                        ins = None
                        for kc in range(DC):
                            for ts in range(2):
                                off = half * 1024 + ts * 512
                                ins = t.matmul(ps.ap()[:, ts * 512:ts * 512 + 512],
                                               buf.ap()[:, kc * 128:(kc + 1) * 128],
                                               mov_fn(kc, off),
                                               start=(kc == 0), stop=(kc == DC - 1))
                        inc("te", ins, "te", 1)
                        te_done.append(C["te"])
                        n += 1
            block.tensor(_te)

            def _dve(v):
                n = 0
                for mc in range(n_mc):
                    for half in range(nh):
                        wt("dve", v, "te", te_done[n])
                        ins = evac_fn(v, mc, half, SP[n % 2].ap())
                        inc("dve", ins, "dve", 1)
                        n += 1
            block.vector(_dve)
            assert C["dve"] == evac_pred[-1]

            def _dma(g):
                for mc in range(n_mc):
                    bname = "A" if mc % 2 == 0 else "B"
                    wv_ = te_done[(mc - 2) * nh + nh - 1] if mc >= 2 else buf_last[bname]
                    wt("pool", g, "te", wv_)
                    buf = wbufA if mc % 2 == 0 else wbufB
                    inc("pool", g.dma_start(out=buf.ap(), in_=src_w[mc]), "w", 16)
            block.gpsimd(_dma)
            for mc in range(max(0, n_mc - 2), n_mc):
                buf_last["A" if mc % 2 == 0 else "B"] = te_done[mc * nh + nh - 1]
            return te_done, evac_pred

        khat_g = nc.sbuf_tensor("khat", [66, NKV * TOK2], BF16)
        khat = khat_g.__enter__()
        vhat_g = nc.sbuf_tensor("vhat", [128, STN * NKV * 65], BF16)
        vhat = vhat_g.__enter__()
        xn_g = nc.sbuf_tensor("xn", [128, DC * TOK2], BF16)
        xn = xn_g.__enter__()
        ktmp_g = nc.sbuf_tensor("ktmp", [128, KMC * TOK2], BF16, side="right")
        ktmp = ktmp_g.__enter__()

        _scope_n = 'p1_ln1'; _scope_i = nc.enter_named_scope('p1_ln1', False)[0]
        c_xld0 = C["w"]

        def p1_dma(g):
            for kc in range(DC):
                ins = g.dma_start(out=xn.ap()[:, kc * TOK2:(kc + 1) * TOK2],
                                  in_=xt[kc * 128:(kc + 1) * 128, :])
                inc("pool", ins, "w", 16)
        block.gpsimd(p1_dma)

        c_ln1 = ln_stats(DC, TOK2, lambda mc: xn.ap()[:, mc * TOK2:(mc + 1) * TOK2],
                         LN_EPS, chunk_wait=("w", lambda mc: c_xld0 + (mc + 1) * 16),
                         sq2=(ktmp.ap()[:, 0:TOK2] if LN1_SQ2 else None))

        def p1_norm(v):
            wt("dve", v, "act", c_ln1)
            for kc in range(DC):
                sl = slice(kc * TOK2, (kc + 1) * TOK2)
                v.tensor_tensor(xn.ap()[:, sl], xn.ap()[:, sl], stM.ap(), ALU.subtract)
                ins = v.tensor_tensor(xn.ap()[:, sl], xn.ap()[:, sl], stR.ap(), ALU.mult)
                inc("dve", ins, "dve", 1)
        block.vector(p1_norm)
        c_xn = C["dve"]

        if "xn" in taps:
            def tp_xn(sy):
                wt("sy", sy, "dve", c_xn)
                inc("sy", sy.dma_start(out=tap_d["xn"], in_=xn.ap()), "s", 16)
            block.sync(tp_xn)

        nc.leave_named_scope(_scope_n, _scope_i, False)
        _scope_n = 'p2_k'; _scope_i = nc.enter_named_scope('p2_k', False)[0]
        k_te, k_ev = proj_sweep(
            KMC, wk_d, TOK2,
            lambda kc, off: xn.ap()[:, kc * TOK2 + off: kc * TOK2 + off + 512],
            lambda v, mc, half, ps: v.tensor_copy(
                ktmp.ap()[:, mc * TOK2 + half * 1024: mc * TOK2 + half * 1024 + 1024], ps),
            pre_te_dve=c_xn)

        c_krms = ln_stats(KMC, TOK2, lambda mc: ktmp.ap()[:, mc * TOK2:(mc + 1) * TOK2],
                          RMS_EPS, rms_denom=512.0,
                          chunk_wait=("dve", lambda mc: k_ev[2 * mc + 1]))

        def p2_khat(v):
            wt("dve", v, "act", c_krms)
            v.memset(khat.ap()[64:66, :], 1.0)
            for mc in range(KMC):
                for par in range(2):
                    g_head = 2 * mc + par
                    ins = v.scalar_tensor_tensor(
                        khat.ap()[0:64, g_head * TOK2:(g_head + 1) * TOK2],
                        ktmp.ap()[par * 64:(par + 1) * 64, mc * TOK2:(mc + 1) * TOK2],
                        knw_t.ap()[par * 64:(par + 1) * 64, mc:mc + 1],
                        stR.ap()[par * 64:(par + 1) * 64, 0:TOK2],
                        ALU.mult, ALU.mult)
                    inc("dve", ins, "dve", 1)
        block.vector(p2_khat)
        c_khat = C["dve"]

        if "khat" in taps:
            def tp_kh(sy):
                wt("sy", sy, "dve", c_khat)
                inc("sy", sy.dma_start(out=tap_d["khat"], in_=khat.ap()), "s", 16)
            block.sync(tp_kh)

        nc.leave_named_scope(_scope_n, _scope_i, False)
        _scope_n = 'p2_v'; _scope_i = nc.enter_named_scope('p2_v', False)[0]
        vW0 = C["w"]
        v_te = []
        v_stf, v_evh = {}, {}
        cd = C["dve"]
        for mc in range(KMC):
            for half in range(2):
                cd += 1
                v_stf[(mc, half)] = cd
            for grp in range(4):
                cd += 1
                v_evh[(mc, grp)] = cd

        def p2_v_te(t):
            wt("te", t, "dve", c_khat)
            n = 0
            for mc in range(KMC):
                wt("te", t, "w", vW0 + (mc + 1) * 16)
                buf = wbufA if mc % 2 == 0 else wbufB
                for half in range(2):
                    if n >= 2:
                        wt("te", t, "dve", v_stf[(mc - 1, half)])
                    ps = SP[n % 2]
                    ins = None
                    for kc in range(DC):
                        for ts in range(2):
                            off = half * 1024 + ts * 512
                            ins = t.matmul(ps.ap()[:, ts * 512:ts * 512 + 512],
                                           buf.ap()[:, kc * 128:(kc + 1) * 128],
                                           xn.ap()[:, kc * TOK2 + off: kc * TOK2 + off + 512],
                                           start=(kc == 0), stop=(kc == DC - 1))
                    inc("te", ins, "te", 1)
                    v_te.append(C["te"])
                    n += 1
                wt("te", t, "dve", v_stf[(mc, 1)])
                for grp in range(4):
                    if mc > 0 or grp >= 2:
                        key = (mc, grp - 2) if grp >= 2 else (mc - 1, grp + 2)
                        wt("te", t, "dve", v_evh[key])
                    ins = None
                    for j in range(4):
                        sc16 = grp * 4 + j
                        cps = ((grp % 2) * 4 + j) * 128
                        ins = t.transpose(A0.ap()[:, cps:cps + 128],
                                          stF.ap()[:, sc16 * 128:(sc16 + 1) * 128],
                                          identF_t.ap())
                    inc("te", ins, "te", 1)
        block.tensor(p2_v_te)
        v_tr = {}
        ctr = C["te"]
        tt = v_te[0] - 1
        tt = None
        cnt = v_te[0] - 1
        idx = 0
        for mc in range(KMC):
            cnt = v_te[2 * mc + 1]
            for grp in range(4):
                cnt += 1
                v_tr[(mc, grp)] = cnt
        assert cnt == C["te"]

        def p2_v_dve(v):
            n = 0
            for mc in range(KMC):
                for half in range(2):
                    wt("dve", v, "te", v_te[n])
                    ins = v.tensor_copy(stF.ap()[:, half * 1024:half * 1024 + 1024],
                                        SP[n % 2].ap())
                    inc("dve", ins, "dve", 1)
                    assert C["dve"] == v_stf[(mc, half)]
                    n += 1
                for grp in range(4):
                    wt("dve", v, "te", v_tr[(mc, grp)])
                    ins = None
                    for j in range(4):
                        sc16 = grp * 4 + j
                        cps = ((grp % 2) * 4 + j) * 128
                        for par in range(2):
                            g_head = 2 * mc + par
                            col = (sc16 * NKV + g_head) * 65
                            ins = v.tensor_copy(vhat.ap()[:, col:col + 64],
                                                A0.ap()[:, cps + par * 64: cps + par * 64 + 64])
                    inc("dve", ins, "dve", 1)
                    assert C["dve"] == v_evh[(mc, grp)]
        block.vector(p2_v_dve)

        def p2_v_dma(g):
            for mc in range(KMC):
                bname = "A" if mc % 2 == 0 else "B"
                wv_ = v_te[(mc - 2) * 2 + 1] if mc >= 2 else buf_last[bname]
                wt("pool", g, "te", wv_)
                buf = wbufA if mc % 2 == 0 else wbufB
                inc("pool", g.dma_start(out=buf.ap(), in_=wv_d[mc]), "w", 16)
        block.gpsimd(p2_v_dma)
        buf_last["A"] = v_te[5]
        buf_last["B"] = v_te[7]

        def p2_v_ones(v):
            ins = v.memset(vhat.ap().rearrange("p (b c) -> p b c", c=65)[:, :, 64:65], 1.0)
            inc("dve", ins, "dve", 1)
        block.vector(p2_v_ones)
        c_vhat = C["dve"]

        if "vhat" in taps:
            def tp_vh(sy):
                wt("sy", sy, "dve", c_vhat)
                inc("sy", sy.dma_start(out=tap_d["vhat"], in_=vhat.ap()), "s", 16)
            block.sync(tp_vh)

        ktmp_g.__exit__(None, None, None)

        nc.leave_named_scope(_scope_n, _scope_i, False)
        _scope_n = 'p2_q'; _scope_i = nc.enter_named_scope('p2_q', False)[0]
        qtmp_g = nc.sbuf_tensor("qtmp", [128, DC * TOK], BF16, side="right")
        qtmp = qtmp_g.__enter__()

        q_te, q_ev = proj_sweep(
            DC, wq_d, TOK,
            lambda kc, off: xn.ap()[:, kc * TOK2 + 1024 + off: kc * TOK2 + 1024 + off + 512],
            lambda v, mc, half, ps: v.tensor_copy(qtmp.ap()[:, mc * TOK:(mc + 1) * TOK], ps),
            pre_te_dve=c_vhat)

        c_qrms = ln_stats(DC, TOK, lambda mc: qtmp.ap()[:, mc * TOK:(mc + 1) * TOK],
                          RMS_EPS, rms_denom=float(DIM),
                          chunk_wait=("dve", lambda mc: q_ev[mc]))

        xn_g.__exit__(None, None, None)
        qhat_g = nc.sbuf_tensor("qhat", [66, NH * TOK], BF16)
        qhat = qhat_g.__enter__()

        def p2_qhat(v):
            wt("dve", v, "act", c_qrms)
            for mc in range(DC):
                for par in range(2):
                    head = 2 * mc + par
                    ins = v.scalar_tensor_tensor(
                        qhat.ap()[0:64, head * TOK:(head + 1) * TOK],
                        qtmp.ap()[par * 64:(par + 1) * 64, mc * TOK:(mc + 1) * TOK],
                        qnw_t.ap()[par * 64:(par + 1) * 64, mc:mc + 1],
                        stR.ap()[par * 64:(par + 1) * 64, 0:TOK],
                        ALU.mult, ALU.mult)
                    inc("dve", ins, "dve", 1)
        block.vector(p2_qhat)
        c_qhat_rows = C["dve"]

        nc.leave_named_scope(_scope_n, _scope_i, False)
        _scope_n = 'p3a_diag'; _scope_i = nc.enter_named_scope('p3a_diag', False)[0]
        diag_te = {}
        diag_max = {}
        ph_stt = {}
        cd = C["dve"]
        for qt in range(QT):
            for grp in range(4):
                cd += 1
                diag_max[(qt, grp)] = cd
            cd += 1
            ph_stt[qt] = cd
        te_p3a0 = C["te"]
        ph_tr_pred = {qt: te_p3a0 + 5 * qt + 5 for qt in range(QT)}

        def p3a_te(t):
            wt("te", t, "dve", c_qhat_rows)
            for qt in range(QT):
                st = qt + 8
                for grp in range(4):
                    if qt > 0 or grp >= 2:
                        key = (qt, grp - 2) if grp >= 2 else (qt - 1, grp + 2)
                        wt("te", t, "dve", diag_max[key])
                    ins = None
                    for j in range(8):
                        h = grp * 8 + j
                        g_head = h // 4
                        ins = t.matmul(SP[grp % 2].ap()[:, j * 128:(j + 1) * 128],
                                       qhat.ap()[0:64, h * TOK + qt * 128: h * TOK + qt * 128 + 128],
                                       khat.ap()[0:64, g_head * TOK2 + st * 128: g_head * TOK2 + st * 128 + 128],
                                       start=True, stop=True)
                    inc("te", ins, "te", 1)
                    diag_te[(qt, grp)] = C["te"]
                wt("te", t, "dve", diag_max[(qt, 3)])
                if qt >= 2:
                    wt("te", t, "dve", ph_stt[qt - 2])
                ins = t.transpose(A1.ap()[0:32, (qt % 2) * 512:(qt % 2) * 512 + 128],
                                  m0buf.ap(), identF_t.ap())
                inc("te", ins, "te", 1)
                assert C["te"] == ph_tr_pred[qt]
        block.tensor(p3a_te)

        s_p3a0 = C["s"]

        def p3a_dve(v):
            for qt in range(QT):
                for grp in range(4):
                    wt("dve", v, "te", diag_te[(qt, grp)])
                    if qt >= 1 and grp == 0:
                        wt("dve", v, "te", ph_tr_pred[qt - 1])
                    ins = None
                    for j in range(8):
                        h = grp * 8 + j
                        ins = v.reduce_max(m0buf.ap()[:, h:h + 1],
                                           SP[grp % 2].ap()[:, j * 128:(j + 1) * 128],
                                           axis=AX.X, negate=True)
                    inc("dve", ins, "dve", 1)
                    assert C["dve"] == diag_max[(qt, grp)]
                wt("dve", v, "te", ph_tr_pred[qt])
                if qt >= 1:
                    wt("dve", v, "s", s_p3a0 + 32 * (qt - 1) + 16)
                ps = A1.ap()[0:32, (qt % 2) * 512:(qt % 2) * 512 + 128]
                v.tensor_tensor(phiA.ap(), ps, phiq8_t.ap()[:, qt * 128:(qt + 1) * 128], ALU.subtract)
                v.tensor_copy(phisc.ap()[:, 0:128], phiA.ap())
                ins = v.tensor_tensor(phisc.ap()[:, 128:256], phiA.ap(), phisc.ap()[:, 0:128], ALU.subtract)
                inc("dve", ins, "dve", 1)
                assert C["dve"] == ph_stt[qt]
        block.vector(p3a_dve)

        def p3a_dma(sy):
            for qt in range(QT):
                wt("sy", sy, "dve", ph_stt[qt])
                ins = sy.dma_start(out=phis[qt],
                                   in_=phisc.ap().rearrange("h (e q) -> h e q", e=2))
                inc("sy", ins, "s", 16)
                wt("sy", sy, "s", C["s"])
                ins = sy.dma_start(
                    out=qhat.ap()[64:66, :].rearrange("e (h q) -> e h q", h=32)[:, :, qt * 128:(qt + 1) * 128],
                    in_=phis[qt].rearrange("h e q -> e h q"))
                inc("sy", ins, "s", 16)
        block.sync(p3a_dma)
        c_phi_all = C["s"]

        if "qhat" in taps:
            def tp_qh(sy):
                wt("sy", sy, "s", c_phi_all)
                inc("sy", sy.dma_start(out=tap_d["qhat"], in_=qhat.ap()), "s", 16)
            block.sync(tp_qh)

        nc.leave_named_scope(_scope_n, _scope_i, False)
        _scope_n = 'p3b_attn'; _scope_i = nc.enter_named_scope('p3b_attn', False)[0]
        def pt_ap(h, st, lo, hi):
            base = ((h % 2) * 3 + (st % PT_LAG)) * QT * 128
            return pt_t.ap()[:, base + lo * 128: base + (hi + 1) * 128]

        sched = []
        exp_order = []
        for hp in range(0, NH, 2):
            hs = [hp, hp + 1]
            stl = {h: _st_list(h) for h in hs}
            union = sorted(set(stl[hp]) | set(stl[hp + 1]))
            pend = {h: [] for h in hs}
            zf_done = {h: False for h in hs}
            ndiv = 0
            for st in union:
                for h in hs:
                    if st in stl[h]:
                        sched.append(("kq", h, st))
                        exp_order.append((h, st))
                        pend[h].append(st)
                if hp >= 2 and ndiv < 2:
                    sched.append(("div", hp - 2 + ndiv))
                    ndiv += 1
                for h in hs:
                    if len(pend[h]) >= PT_LAG:
                        if not zf_done[h]:
                            sched.append(("zf", h))
                            zf_done[h] = True
                        sched.append(("av", h, pend[h].pop(0)))
            for h in hs:
                while pend[h]:
                    if not zf_done[h]:
                        sched.append(("zf", h))
                        zf_done[h] = True
                    sched.append(("av", h, pend[h].pop(0)))
        sched.append(("div", NH - 2))
        sched.append(("div", NH - 1))

        exp_pred = {}
        ca = C["act"]
        for key in exp_order:
            ca += 1
            exp_pred[key] = ca
        prev_s_tenant = {}
        lastp = {0: None, 1: None}
        for (h, st) in exp_order:
            p = h % 2
            if lastp[p] is not None:
                prev_s_tenant[(h, st)] = lastp[p]
            lastp[p] = (h, st)
        prev_pt_tenant = {}
        hist = {}
        for (h, st) in exp_order:
            hist.setdefault(h, []).append(st)
        for h, sts_h in hist.items():
            for i, st in enumerate(sts_h):
                if i >= PT_LAG:
                    prev_pt_tenant[(h, st)] = (h, sts_h[i - PT_LAG])
        den_pred, div_pred = {}, {}
        cd = C["dve"]
        for h in range(NH):
            cd += 1
            den_pred[h] = cd
            cd += 1
            div_pred[h] = cd
        kq_done, av_done, div_te = {}, {}, {}
        c_diag_last = diag_max[(QT - 1, 3)]
        c_ph_last = ph_stt[QT - 1]

        def p3b_te(t):
            wt("te", t, "s", c_phi_all)
            wt("te", t, "dve", c_zrow)
            for op in sched:
                kind = op[0]
                if kind == "kq":
                    _, h, st = op
                    g_head = h // 4
                    lo, hi = _vqt(h, st)
                    prev = prev_s_tenant.get((h, st))
                    if prev is not None:
                        wt("te", t, "act", exp_pred[prev])
                    else:
                        wt("te", t, "dve", c_diag_last)
                    diag_q = st - 8
                    edge_q = st if (KEPT[h] == 9 and st <= 7) else None
                    ins = None
                    for bank in range(2):
                        qlist = [qt for qt in range(lo, hi + 1) if qt // 4 == bank]
                        masked = [qt for qt in qlist
                                  if qt == diag_q or (edge_q is not None and qt == edge_q)]
                        normal = [qt for qt in qlist if qt not in masked]
                        for qt in normal:
                            ins = t.matmul(SP[h % 2].ap()[:, qt * 128:(qt + 1) * 128],
                                           khat.ap()[0:66, g_head * TOK2 + st * 128: g_head * TOK2 + st * 128 + 128],
                                           qhat.ap()[0:66, h * TOK + qt * 128: h * TOK + qt * 128 + 128],
                                           start=True, stop=True)
                        for qt in masked:
                            mt = mdiagT_t if qt == diag_q else medgeT_t
                            t.matmul(SP[h % 2].ap()[:, qt * 128:(qt + 1) * 128],
                                     mt.ap(), negI_t.ap(), start=True, stop=False)
                            ins = t.matmul(SP[h % 2].ap()[:, qt * 128:(qt + 1) * 128],
                                           khat.ap()[0:66, g_head * TOK2 + st * 128: g_head * TOK2 + st * 128 + 128],
                                           qhat.ap()[0:66, h * TOK + qt * 128: h * TOK + qt * 128 + 128],
                                           start=False, stop=True)
                    inc("te", ins, "te", 1)
                    kq_done[(h, st)] = C["te"]
                elif kind == "av":
                    _, h, st = op
                    g_head = h // 4
                    lo, hi = _vqt(h, st)
                    wt("te", t, "act", exp_pred[(h, st)])
                    ins = None
                    for qt in range(lo, hi + 1):
                        last = (st == qt + 8)
                        vcol = (st * NKV + g_head) * 65
                        ins = t.matmul(APm[h % 2].ap()[0:65, qt * 128:(qt + 1) * 128],
                                       vhat.ap()[:, vcol:vcol + 65],
                                       pt_ap(h, st, qt, qt),
                                       start=False, stop=last)
                    inc("te", ins, "te", 1)
                    av_done[(h, st)] = C["te"]
                elif kind == "zf":
                    _, h = op
                    wt("te", t, "dve", div_pred[h - 2] if h >= 2 else c_ph_last)
                    for bank in range(2):
                        t.matmul(APm[h % 2].ap()[0:65, bank * 512:bank * 512 + 512],
                                 onesB_t.ap()[0:1, 0:65], zrow.ap(),
                                 start=True, stop=False)
                else:
                    _, hd = op
                    wt("te", t, "dve", den_pred[hd])
                    ins = None
                    for ts in range(2):
                        t.matmul(APm[hd % 2].ap()[64:128, ts * 512:ts * 512 + 512],
                                 onesB_t.ap()[0:1, 0:64],
                                 stM.ap()[0:1, ts * 512:ts * 512 + 512],
                                 start=True, stop=False, tile_position=(0, 64))
                        ins = t.matmul(APm[hd % 2].ap()[64:128, ts * 512:ts * 512 + 512],
                                       onesB_t.ap()[0:1, 0:64],
                                       stM.ap()[0:1, 1024 + ts * 512: 1024 + ts * 512 + 512],
                                       start=False, stop=True, tile_position=(0, 64))
                    inc("te", ins, "te", 1)
                    div_te[hd] = C["te"]
        block.tensor(p3b_te)

        def p3b_act(sc):
            for (h, st) in exp_order:
                lo, hi = _vqt(h, st)
                wt("act", sc, "te", kq_done[(h, st)])
                pv = prev_pt_tenant.get((h, st))
                if pv is not None:
                    wt("act", sc, "te", av_done[pv])
                ins = sc.activation(
                    pt_ap(h, st, lo, hi),
                    SP[h % 2].ap()[:, lo * 128:(hi + 1) * 128],
                    AF.Exp, bias=sbias_t.ap()[:, h * STN + st: h * STN + st + 1],
                    scale=0.125)
                inc("act", ins, "act", 1)
                assert C["act"] == exp_pred[(h, st)]
        block.scalar(p3b_act)

        attnout = qtmp

        avE = sqscr

        bg_pred = {}
        cg = C["g"]
        for h in range(NH):
            cg += 1
            bg_pred[h] = cg
        P3B_BG = P3B_BG_ON_DVE

        def p3b_dve(v):
            for h in range(NH):
                sts_h = hist[h]
                wt("dve", v, "te", av_done[(h, sts_h[-1])])
                if h >= 1:
                    wt("dve", v, "te", div_te[h - 1])
                if h >= 2:
                    wt("dve", v, "g", bg_pred[h - 2])
                v.tensor_copy(stF.ap()[0:1, 0:TOK], APm[h % 2].ap()[64:65, :])
                v.tensor_copy(stM.ap()[0:1, 0:TOK], stF.ap()[0:1, 0:TOK])
                ins = v.tensor_tensor(stM.ap()[0:1, 1024:1024 + TOK],
                                      stF.ap()[0:1, 0:TOK],
                                      stM.ap()[0:1, 0:TOK], ALU.subtract)
                inc("dve", ins, "dve", 1)
                assert C["dve"] == den_pred[h]
                wt("dve", v, "te", div_te[h])
                ebuf = avE.ap()[64:128, (h % 2) * TOK:(h % 2) * TOK + TOK]
                v.reciprocal(stF.ap()[64:128, (h % 2) * TOK:(h % 2) * TOK + TOK],
                             APm[h % 2].ap()[64:128, :])
                ins = v.tensor_copy(ebuf, APm[h % 2].ap()[0:64, :])
                inc("dve", ins, "dve", 1)
                assert C["dve"] == div_pred[h]
                if P3B_BG_ON_DVE:
                    rb = stF.ap()[64:128, (h % 2) * TOK:(h % 2) * TOK + TOK]
                    mc = h // 2
                    if h % 2 == 1:
                        ins = v.tensor_tensor(attnout.ap()[64:128, mc * TOK:(mc + 1) * TOK],
                                              ebuf, rb, ALU.mult)
                    else:
                        v.tensor_tensor(gptmp.ap()[64:128, :], ebuf, rb, ALU.mult)
                        ins = v.tensor_copy(attnout.ap()[0:64, mc * TOK:(mc + 1) * TOK],
                                            gptmp.ap()[64:128, :])
                    inc("dve", ins, "g", 1)
        block.vector(p3b_dve)

        def p3b_gp(g):
            if P3B_BG_ON_DVE:
                return
            for h in range(NH):
                wt("pool", g, "dve", div_pred[h])
                ebuf = avE.ap()[64:128, (h % 2) * TOK:(h % 2) * TOK + TOK]
                rb = stF.ap()[64:128, (h % 2) * TOK:(h % 2) * TOK + TOK]
                mc = h // 2
                if h % 2 == 1:
                    ins = g.tensor_tensor(attnout.ap()[64:128, mc * TOK:(mc + 1) * TOK],
                                          ebuf, rb, ALU.mult)
                else:
                    g.tensor_tensor(gptmp.ap()[64:128, :], ebuf, rb, ALU.mult)
                    ins = g.tensor_copy(attnout.ap()[0:64, mc * TOK:(mc + 1) * TOK],
                                        gptmp.ap()[64:128, :])
                inc("dve" if P3B_BG_ON_DVE else "pool", ins, "g", 1)
                assert C["g"] == bg_pred[h]
        (block.vector if P3B_BG_ON_DVE else block.gpsimd)(p3b_gp)
        c_attnout = C["dve"]
        c_bg_all = C["g"]
        c_exp_last = exp_pred[exp_order[-1]]

        if "attnout" in taps:
            def tp_ao(sy):
                wt("sy", sy, "dve", c_attnout)
                inc("sy", sy.dma_start(out=tap_d["attnout"], in_=attnout.ap()), "s", 16)
            block.sync(tp_ao)

        nc.leave_named_scope(_scope_n, _scope_i, False)
        _scope_n = 'p4_wo'; _scope_i = nc.enter_named_scope('p4_wo', False)[0]
        qhat_g.__exit__(None, None, None)
        vhat_g.__exit__(None, None, None)
        khat_g.__exit__(None, None, None)
        a2_g = nc.sbuf_tensor("a2", [128, DC * TOK], BF16)
        a2 = a2_g.__enter__()
        ot_g = nc.sbuf_tensor("ot_t", [128, TOK], F32, side="right")
        ot_t = ot_g.__enter__()
        ot2_g = nc.sbuf_tensor("ot_t2", [128, TOK], F32, side="right")
        ot_t2 = ot2_g.__enter__()
        xrs_g = nc.sbuf_tensor("xrs", [128, TOK], F32, side="right")
        xrs = xrs_g.__enter__()
        xrs2_g = nc.sbuf_tensor("xrs2", [128, TOK], F32, side="right")
        xrs2 = xrs2_g.__enter__()
        x2bf_g = nc.sbuf_tensor("x2bf", [128, DC * TOK], BF16, side="right")
        x2bf = x2bf_g.__enter__()

        wo_te, wo_ev = proj_sweep(
            DC, wo_d, TOK,
            lambda kc, off: attnout.ap()[:, kc * TOK + off: kc * TOK + off + 512],
            lambda v, mc, half, ps: v.tensor_copy(a2.ap()[:, mc * TOK:(mc + 1) * TOK], ps),
            pre_te_dve=c_attnout, pre_te_act=c_exp_last, pre_te_g=c_bg_all)

        c_ln2 = ln_stats(DC, TOK, lambda mc: a2.ap()[:, mc * TOK:(mc + 1) * TOK],
                         LN_EPS, chunk_wait=("dve", lambda mc: wo_ev[mc]),
                         mm_pre_dve=wo_ev[-1])

        if "a2" in taps:
            def tp_a2(sy):
                wt("sy", sy, "dve", wo_ev[-1])
                inc("sy", sy.dma_start(out=tap_d["a2"], in_=a2.ap()), "s", 16)
            block.sync(tp_a2)

        x2_dve_pred = [C["dve"] + mc + 1 for mc in range(DC)]
        cs = C["s"]
        ld_pred, x2_st_pred = {}, {}
        for mc in range(DC):
            cs += 16
            ld_pred[mc] = cs
            if mc >= 1:
                cs += 16
                x2_st_pred[mc - 1] = cs
        cs += 16
        x2_st_pred[DC - 1] = cs

        def p4b_sync(sy):
            wt("sy", sy, "dve", c_attnout)
            wt("sy", sy, "g", c_bg_all)
            for mc in range(DC):
                if mc >= 2:
                    wt("sy", sy, "dve", x2_dve_pred[mc - 2])
                ins = sy.dma_start(out=(xrs if mc % 2 == 0 else xrs2).ap(),
                                   in_=xt[mc * 128:(mc + 1) * 128, 1024:2048])
                inc("sy", ins, "s", 16)
                assert C["s"] == ld_pred[mc]
                if mc >= 1:
                    wt("sy", sy, "dve", x2_dve_pred[mc - 1])
                    ins = sy.dma_start(out=x2s[(mc - 1) * 128:mc * 128, :],
                                       in_=(ot_t if (mc - 1) % 2 == 0 else ot_t2).ap())
                    inc("sy", ins, "s", 16)
                    assert C["s"] == x2_st_pred[mc - 1]
            wt("sy", sy, "dve", x2_dve_pred[DC - 1])
            ins = sy.dma_start(out=x2s[(DC - 1) * 128:DC * 128, :],
                               in_=(ot_t if (DC - 1) % 2 == 0 else ot_t2).ap())
            inc("sy", ins, "s", 16)
            assert C["s"] == x2_st_pred[DC - 1]
        block.sync(p4b_sync)

        x2bf_cast_pred = [c_ln2 + mc + 1 for mc in range(DC)]

        def p4b_dve(v):
            wt("dve", v, "act", c_ln2)
            for mc in range(DC):
                wt("dve", v, "s", ld_pred[mc])
                if mc >= 2:
                    wt("dve", v, "s", x2_st_pred[mc - 2])
                    wt("dve", v, "act", x2bf_cast_pred[mc - 2])
                buf = (xrs if mc % 2 == 0 else xrs2).ap()
                ob = (ot_t if mc % 2 == 0 else ot_t2).ap()
                a2c = a2.ap()[:, mc * TOK:(mc + 1) * TOK]
                v.tensor_tensor(ob, a2c, stM.ap()[:, 0:TOK], ALU.subtract)
                v.tensor_tensor(ob, ob, stR.ap()[:, 0:TOK], ALU.mult)
                ins = v.tensor_tensor(ob, ob, buf, ALU.add)
                inc("dve", ins, "dve", 1)
                assert C["dve"] == x2_dve_pred[mc]
        block.vector(p4b_dve)
        c_x2 = C["dve"]

        def p4b_cast(sc):
            for mc in range(DC):
                wt("act", sc, "dve", x2_dve_pred[mc])
                ob = (ot_t if mc % 2 == 0 else ot_t2).ap()
                ins = sc.activation(x2bf.ap()[:, mc * TOK:(mc + 1) * TOK], ob, AF.Copy)
                assert inc("act", ins, "act", 1) == x2bf_cast_pred[mc]
        block.scalar(p4b_cast)

        if "x2" in taps:
            def tp_x2(sy):
                wt("sy", sy, "s", x2_st_pred[DC - 1])
                inc("sy", sy.dma_start(out=tap_d["x2"], in_=x2s), "s", 16)
            block.sync(tp_x2)



        nc.leave_named_scope(_scope_n, _scope_i, False)
        _scope_n = 'p5_ffn'; _scope_i = nc.enter_named_scope('p5_ffn', False)[0]
        a2_g.__exit__(None, None, None)
        xn3_g = nc.sbuf_tensor("xn3", [128, DC * TOK], BF16)
        xn3 = xn3_g.__enter__()

        c_ln3 = ln_stats(DC, TOK, lambda mc: x2bf.ap()[:, mc * TOK:(mc + 1) * TOK],
                         LN_EPS, chunk_wait=("act", lambda mc: x2bf_cast_pred[mc]))

        def p5_xn3(v):
            wt("dve", v, "act", c_ln3)
            for mc in range(DC):
                sl = slice(mc * TOK, (mc + 1) * TOK)
                v.tensor_tensor(xn3.ap()[:, sl], x2bf.ap()[:, sl], stM.ap()[:, 0:TOK], ALU.subtract)
                ins = v.tensor_tensor(xn3.ap()[:, sl], xn3.ap()[:, sl], stR.ap()[:, 0:TOK], ALU.mult)
                inc("dve", ins, "dve", 1)
        block.vector(p5_xn3)
        c_xn3 = C["dve"]

        if "xn3" in taps:
            def tp_x3(sy):
                wt("sy", sy, "dve", c_xn3)
                inc("sy", sy.dma_start(out=tap_d["xn3"], in_=xn3.ap()), "s", 16)
            block.sync(tp_x3)

        x2bf_g.__exit__(None, None, None)
        xrs2_g.__exit__(None, None, None)
        xrs_g.__exit__(None, None, None)
        ot2_g.__exit__(None, None, None)
        ot_g.__exit__(None, None, None)

        hbuf2_g = nc.sbuf_tensor("hbuf2", [128, (ICN - DC) * TOK], BF16)
        hbuf2 = hbuf2_g.__enter__()
        wbufC_g = nc.sbuf_tensor("wbufC", [128, DC * 128], BF16)
        wbufC = wbufC_g.__enter__()
        wbufD_g = nc.sbuf_tensor("wbufD", [128, DC * 128], BF16)
        wbufD = wbufD_g.__enter__()
        wdb1_g = nc.sbuf_tensor("wdbuf", [128, 22 * 128], BF16)
        wdbuf = wdb1_g.__enter__()
        wdb2_g = nc.sbuf_tensor("wdbuf2", [128, 22 * 128], BF16)
        wdbuf2 = wdb2_g.__enter__()
        wdb3_g = nc.sbuf_tensor("wdbuf3", [128, 22 * 128], BF16)
        wdbuf3 = wdb3_g.__enter__()
        WDB = [wdbuf, wdbuf2, wdbuf3]

        def h_ap(ic):
            if ic < DC:
                return attnout.ap()[:, ic * TOK:(ic + 1) * TOK]
            return hbuf2.ap()[:, (ic - DC) * TOK:(ic - DC + 1) * TOK]

        guW0 = C["w"]
        gu_te = []
        gu_silu_pred = [C["act"] + ic + 1 for ic in range(ICN)]
        gu_h_pred = [C["dve"] + 2 * (ic + 1) for ic in range(ICN)]

        def p5_gu_te(t):
            wt("te", t, "dve", c_xn3)
            for ic in range(ICN):
                wt("te", t, "w", guW0 + (ic + 1) * 32)
                if ic >= 2:
                    wt("te", t, "act", gu_silu_pred[ic - 2])
                    wt("te", t, "dve", gu_h_pred[ic - 2])
                gps, ups = SP[ic % 2], APm[ic % 2]
                gbuf = wbufA if ic % 2 == 0 else wbufB
                ubuf = wbufC if ic % 2 == 0 else wbufD
                insu = None
                for kc in range(DC):
                    for ts in range(2):
                        mv = xn3.ap()[:, kc * TOK + ts * 512: kc * TOK + ts * 512 + 512]
                        t.matmul(gps.ap()[:, ts * 512:ts * 512 + 512],
                                 gbuf.ap()[:, kc * 128:(kc + 1) * 128], mv,
                                 start=(kc == 0), stop=(kc == DC - 1))
                        insu = t.matmul(ups.ap()[:, ts * 512:ts * 512 + 512],
                                        ubuf.ap()[:, kc * 128:(kc + 1) * 128], mv,
                                        start=(kc == 0), stop=(kc == DC - 1))
                inc("te", insu, "te", 1)
                gu_te.append(C["te"])
        block.tensor(p5_gu_te)

        def p5_gu_dma(g):
            wt("pool", g, "dve", c_xn3)
            for ic in range(ICN):
                if ic >= 2:
                    wt("pool", g, "te", gu_te[ic - 2])
                else:
                    wt("pool", g, "te", max(buf_last["A"], buf_last["B"]))
                gbuf = wbufA if ic % 2 == 0 else wbufB
                ubuf = wbufC if ic % 2 == 0 else wbufD
                inc("pool", g.dma_start(out=gbuf.ap(), in_=wg_d[ic]), "w", 16)
                inc("pool", g.dma_start(out=ubuf.ap(), in_=wu_d[ic]), "w", 16)
        block.gpsimd(p5_gu_dma)

        def p5_gu_act(sc):
            for ic in range(ICN):
                wt("act", sc, "te", gu_te[ic])
                ins = sc.activation(h_ap(ic), SP[ic % 2].ap(),
                                    AF.Silu, scale=gs_t.ap()[:, ic:ic + 1])
                inc("act", ins, "act", 1)
                assert C["act"] == gu_silu_pred[ic]
        block.scalar(p5_gu_act)

        def p5_gu_dve(v):
            for ic in range(ICN):
                wt("dve", v, "te", gu_te[ic])
                wt("dve", v, "act", gu_silu_pred[ic])
                ins = v.tensor_scalar(sqscr.ap()[:, 0:TOK], APm[ic % 2].ap(),
                                      us_t.ap()[:, ic:ic + 1], None, ALU.mult)
                inc("dve", ins, "dve", 1)
                ins = v.tensor_tensor(h_ap(ic), h_ap(ic), sqscr.ap()[:, 0:TOK], ALU.mult)
                inc("dve", ins, "dve", 1)
                assert C["dve"] == gu_h_pred[ic]
        block.vector(p5_gu_dve)
        c_h_done = C["dve"]

        if "hb1" in taps:
            def tp_hb(sy):
                wt("sy", sy, "dve", c_h_done)
                inc("sy", sy.dma_start(out=tap_d["hb1"], in_=attnout.ap()), "s", 16)
                inc("sy", sy.dma_start(out=tap_d["hb2"], in_=hbuf2.ap()), "s", 16)
            block.sync(tp_hb)

        dnW0 = C["w"]
        dn_te = []
        dn_half = []
        dn_ev_pred = [C["dve"] + mc + 1 for mc in range(DC)]

        def p5_dn_te(t):
            wt("te", t, "dve", c_h_done)
            wt("te", t, "act", gu_silu_pred[-1])
            for mc in range(DC):
                if mc >= 2:
                    wt("te", t, "dve", dn_ev_pred[mc - 2])
                ps = SP[mc % 2]
                ins = None
                for half in range(2):
                    n = 2 * mc + half
                    wt("te", t, "w", dnW0 + (n + 1) * 16)
                    buf = WDB[n % 3]
                    for ici in range(22):
                        ic = half * 22 + ici
                        for ts in range(2):
                            ins = t.matmul(ps.ap()[:, ts * 512:ts * 512 + 512],
                                           buf.ap()[:, ici * 128:(ici + 1) * 128],
                                           h_ap(ic)[:, ts * 512:ts * 512 + 512],
                                           start=(ic == 0), stop=(ic == ICN - 1))
                    inc("te", ins, "te", 1)
                    dn_half.append(C["te"])
                dn_te.append(C["te"])
        block.tensor(p5_dn_te)

        def p5_dn_dma(g):
            wt("pool", g, "dve", c_xn3)
            for n in range(2 * DC):
                if n >= 3:
                    wt("pool", g, "te", dn_half[n - 3])
                buf = WDB[n % 3]
                half = n % 2
                inc("pool", g.dma_start(out=buf.ap(),
                                        in_=wd_d[n // 2][:, half * 22 * 128:(half + 1) * 22 * 128]),
                    "w", 16)
        block.gpsimd(p5_dn_dma)

        d4r = xn3

        def p5_dn_ev(v):
            for mc in range(DC):
                wt("dve", v, "te", dn_te[mc])
                ins = v.tensor_scalar(d4r.ap()[:, mc * TOK:(mc + 1) * TOK],
                                      SP[mc % 2].ap(), ds_t.ap()[:, mc:mc + 1], None, ALU.mult)
                inc("dve", ins, "dve", 1)
                assert C["dve"] == dn_ev_pred[mc]
        block.vector(p5_dn_ev)

        if "d4" in taps:
            def tp_d4(sy):
                wt("sy", sy, "dve", dn_ev_pred[-1])
                inc("sy", sy.dma_start(
                    out=tap_d["d4"].rearrange("(c p) t -> p c t", p=128),
                    in_=d4r.ap().rearrange("p (c t) -> p c t", t=TOK)), "s", 16)
            block.sync(tp_d4)

        nc.leave_named_scope(_scope_n, _scope_i, False)
        _scope_n = 'p6_out'; _scope_i = nc.enter_named_scope('p6_out', False)[0]
        wdb3_g.__exit__(None, None, None)
        wdb2_g.__exit__(None, None, None)
        wdb1_g.__exit__(None, None, None)
        wbufD_g.__exit__(None, None, None)
        wbufC_g.__exit__(None, None, None)
        hbuf2_g.__exit__(None, None, None)
        c_ln4 = ln_stats(DC, TOK, lambda mc: d4r.ap()[:, mc * TOK:(mc + 1) * TOK],
                         LN_EPS, chunk_wait=("dve", lambda mc: dn_ev_pred[mc]))

        fx_g = nc.sbuf_tensor("fx", [128, TOK], F32, side="right")
        fx = fx_g.__enter__()
        fx2_g = nc.sbuf_tensor("fx2", [128, TOK], F32, side="right")
        fx2 = fx2_g.__enter__()
        fot_g = nc.sbuf_tensor("fot", [128, TOK], F32, side="right")
        fot = fot_g.__enter__()
        fot2_g = nc.sbuf_tensor("fot2", [128, TOK], F32, side="right")
        fot2 = fot2_g.__enter__()
        FXB = [fx, fx2]

        fin_dve_pred = [C["dve"] + mc + 1 for mc in range(DC)]
        cs = C["s"]
        fin_ld_pred, out_st_pred = {}, {}
        for mc in range(DC):
            cs += 16
            fin_ld_pred[mc] = cs
            if mc >= 3:
                cs += 16
                out_st_pred[mc - 3] = cs
        for mc in range(DC - 3, DC):
            cs += 16
            out_st_pred[mc] = cs

        def p6b_sync(sy):
            wt("sy", sy, "s", x2_st_pred[DC - 1])
            wt("sy", sy, "act", x2bf_cast_pred[DC - 1])
            wt("sy", sy, "dve", x2_dve_pred[DC - 1])
            for mc in range(DC):
                if mc >= 2:
                    wt("sy", sy, "dve", fin_dve_pred[mc - 2])
                ins = sy.dma_start(out=FXB[mc % 2].ap(),
                                   in_=x2s[mc * 128:(mc + 1) * 128, :])
                inc("sy", ins, "s", 16)
                assert C["s"] == fin_ld_pred[mc]
                if mc >= 3:
                    wt("sy", sy, "dve", fin_dve_pred[mc - 3])
                    ins = sy.dma_start(out=out_d[(mc - 3) * 128:(mc - 2) * 128, :],
                                       in_=(fot if (mc - 3) % 2 == 0 else fot2).ap())
                    inc("sy", ins, "s", 16)
                    assert C["s"] == out_st_pred[mc - 3]
            for mc in range(DC - 3, DC):
                wt("sy", sy, "dve", fin_dve_pred[mc])
                ins = sy.dma_start(out=out_d[mc * 128:(mc + 1) * 128, :],
                                   in_=(fot if mc % 2 == 0 else fot2).ap())
                inc("sy", ins, "s", 16)
                assert C["s"] == out_st_pred[mc]
        block.sync(p6b_sync)

        def p6b_dve(v):
            wt("dve", v, "act", c_ln4)
            for mc in range(DC):
                wt("dve", v, "s", fin_ld_pred[mc])
                if mc >= 2:
                    wt("dve", v, "s", out_st_pred[mc - 2])
                bufx = FXB[mc % 2].ap()
                ob = (fot if mc % 2 == 0 else fot2).ap()
                v.tensor_tensor(ob, d4r.ap()[:, mc * TOK:(mc + 1) * TOK],
                                stM.ap()[:, 0:TOK], ALU.subtract)
                v.tensor_tensor(ob, ob, stR.ap()[:, 0:TOK], ALU.mult)
                ins = v.tensor_tensor(ob, ob, bufx, ALU.add)
                inc("dve", ins, "dve", 1)
                assert C["dve"] == fin_dve_pred[mc]
        block.vector(p6b_dve)



        nc.leave_named_scope(_scope_n, _scope_i, False)

        def p_final(sy):
            wt("sy", sy, "s", C["s"])
            wt("sy", sy, "w", C["w"])
        block.sync(p_final)

    L.audit()
    def _rng(n):
        ml = nc.lookup_mloc(n)
        return (ml.addr, ml.addr + ml.size() // 128)
    live_p5 = [_rng(n) for n in ("xn3", "hbuf2", "wbufC", "wbufD", "wdbuf", "wdbuf2", "qtmp")]
    for n in ("fx", "fx2", "fot", "fot2"):
        a, b = _rng(n)
        for (c0, c1) in live_p5:
            assert b <= c0 or a >= c1, (n, (a, b), (c0, c1))
    return nc


_BUILD_CACHE = {}


def get_nc(taps=()):
    key = tuple(sorted(taps))
    if key not in _BUILD_CACHE:
        _BUILD_CACHE[key] = build_nc(taps)
    return _BUILD_CACHE[key]


def kernel(**inputs):
    maps = _stage_inputs(inputs)
    nc = get_nc()
    res = run_bass_kernel_spmd(nc, maps, core_ids=list(range(8)))
    return _gather_output(res.results)


# version 08
# speedup vs baseline: 1.0828x; 1.0324x over previous
import numpy as np
import ml_dtypes
import concourse.bass as bass
import concourse.mybir as mybir
from concourse.bass_utils import run_bass_kernel_spmd

F32 = mybir.dt.float32
BF16 = mybir.dt.bfloat16
AF = mybir.ActivationFunctionType
ALU = mybir.AluOpType
AX = mybir.AxisListType

DIM, DC = 2048, 16
NH, NKV, HD = 32, 8, 64
TOK, TOK2 = 1024, 2048
QT, STN = 8, 16
INTER, ICN = 5632, 44
KMC = 4
LN_EPS, RMS_EPS = 1e-5, 1e-6
NEGBIG = -1.0e30
P3B_BG_ON_DVE = False
PT_LAG = 3
LN1_SQ2 = True

SLOPES = [2.0 ** (-h / 4.0) for h in range(NH)]
CUT = 180.0
KEPT = [min(9, 2 + int(CUT / SLOPES[h]) // 128) for h in range(NH)]
for _h in range(NH):
    if KEPT[_h] < 9:
        assert SLOPES[_h] * (128 * KEPT[_h] - 127) >= CUT - 1e-6


def _vqt(h, st):
    lo, hi = max(0, st - 8), min(7, st + KEPT[h] - 9)
    return (lo, hi) if lo <= hi else None


def _st_list(h):
    return [st for st in range(STN) if _vqt(h, st) is not None]



def _tile_kmajor(W, n_kc, n_m):
    W = np.asarray(W, dtype=np.float32)
    Wr = W.reshape(n_m, 128, n_kc, 128)
    return np.ascontiguousarray(Wr.transpose(0, 3, 2, 1)).reshape(n_m, 128, n_kc * 128)


def _make_tables(Q0):
    sbias = np.empty((128, NH * STN), np.float32)
    S0 = Q0 - 1024
    for h in range(NH):
        for st in range(STN):
            s_glob = S0 + st * 128 + np.arange(128)
            if s_glob[0] < 0 or s_glob[0] >= 2048:
                sbias[:, h * STN + st] = NEGBIG
            else:
                sbias[:, h * STN + st] = (SLOPES[h] * s_glob).astype(np.float32)
    phiq8 = np.empty((32, QT * 128), np.float32)
    q_glob = Q0 + np.arange(QT * 128)
    for h in range(NH):
        phiq8[h, :] = (8.0 * SLOPES[h] * q_glob).astype(np.float32)
    return sbias, phiq8


def _stage_inputs(inputs):
    x = np.asarray(inputs["L0_attn_x"], np.float32)
    wo = np.asarray(inputs["L0_attn_wo"], np.float32)
    perm = np.empty(2048, np.int64)
    for h in range(NH):
        fp = (h // 2) * 128 + (h % 2) * 64
        perm[fp:fp + 64] = h * 64 + np.arange(64)

    def _fm(v, n):
        return np.ascontiguousarray(np.asarray(v, np.float32).reshape(n, 128).T)
    shared = dict(
        wq=_tile_kmajor(inputs["L0_attn_wq"], DC, DC),
        wk=_tile_kmajor(inputs["L0_attn_wk"], DC, KMC),
        wv=_tile_kmajor(inputs["L0_attn_wv"], DC, KMC),
        wo=_tile_kmajor(wo[:, perm], DC, DC),
        wg=_tile_kmajor(inputs["L0_ffn_gate_w"], DC, ICN),
        wu=_tile_kmajor(inputs["L0_ffn_up_w"], DC, ICN),
        wd=_tile_kmajor(inputs["L0_ffn_down_w"], ICN, DC),
        qnw=_fm(inputs["L0_attn_q_norm_w"], DC),
        knw=_fm(inputs["L0_attn_k_norm_w"], KMC),
        gs=_fm(inputs["L0_ffn_gate_s"], ICN),
        us=_fm(inputs["L0_ffn_up_s"], ICN),
        ds=_fm(inputs["L0_ffn_down_s"], DC))
    maps = []
    for c in range(8):
        b, hh = c // 2, c % 2
        xt = np.ascontiguousarray(x[b].T)
        if hh == 0:
            xcore = np.concatenate([np.zeros((DIM, 1024), np.float32), xt[:, 0:1024]], 1)
        else:
            xcore = xt
        sbias, phiq8 = _make_tables(hh * 1024)
        m = dict(shared)
        m["xt"] = np.ascontiguousarray(xcore)
        m["sbias"] = sbias
        m["phiq8"] = phiq8
        maps.append(m)
    return maps


def _gather_output(results):
    out = np.empty((4, 2048, 2048), np.float32)
    for c in range(8):
        b, hh = c // 2, c % 2
        out[b, hh * 1024:(hh + 1) * 1024, :] = results[c]["out"].T
    return out



class SemLog:
    def __init__(self):
        self.ev = {e: [] for e in ("te", "act", "dve", "pool", "sy")}

    def wait(self, eng, ch, val):
        self.ev[eng].append(("w", ch, val))

    def inc(self, eng, ch, val_after):
        self.ev[eng].append(("i", ch, val_after))

    def audit(self):
        pos = {e: 0 for e in self.ev}
        cur = {}
        progress = True
        while progress:
            progress = False
            for e, lst in self.ev.items():
                while pos[e] < len(lst):
                    kind, ch, val = lst[pos[e]]
                    if kind == "i":
                        cur[ch] = max(cur.get(ch, 0), val)
                    elif cur.get(ch, 0) < val:
                        break
                    pos[e] += 1
                    progress = True
        stuck = {e: (pos[e], len(lst)) for e, lst in self.ev.items() if pos[e] < len(lst)}
        if stuck:
            det = {e: self.ev[e][pos[e]] for e in stuck}
            raise RuntimeError(f"SEM AUDIT DEADLOCK: {stuck} next={det}")


_LAST_LOG = None


def build_nc(taps=()):
    nc = bass.Bass("TRN2", target_bir_lowering=False)
    taps = set(taps)

    xt = nc.dram_tensor("xt", [DIM, TOK2], F32, kind="ExternalInput").ap()
    wq_d = nc.dram_tensor("wq", [DC, 128, DC * 128], F32, kind="ExternalInput").ap()
    wk_d = nc.dram_tensor("wk", [KMC, 128, DC * 128], F32, kind="ExternalInput").ap()
    wv_d = nc.dram_tensor("wv", [KMC, 128, DC * 128], F32, kind="ExternalInput").ap()
    wo_d = nc.dram_tensor("wo", [DC, 128, DC * 128], F32, kind="ExternalInput").ap()
    wg_d = nc.dram_tensor("wg", [ICN, 128, DC * 128], F32, kind="ExternalInput").ap()
    wu_d = nc.dram_tensor("wu", [ICN, 128, DC * 128], F32, kind="ExternalInput").ap()
    wd_d = nc.dram_tensor("wd", [DC, 128, ICN * 128], F32, kind="ExternalInput").ap()
    qnw_d = nc.dram_tensor("qnw", [128, DC], F32, kind="ExternalInput").ap()
    knw_d = nc.dram_tensor("knw", [128, KMC], F32, kind="ExternalInput").ap()
    gs_d = nc.dram_tensor("gs", [128, ICN], F32, kind="ExternalInput").ap()
    us_d = nc.dram_tensor("us", [128, ICN], F32, kind="ExternalInput").ap()
    ds_d = nc.dram_tensor("ds", [128, DC], F32, kind="ExternalInput").ap()
    sbias_d = nc.dram_tensor("sbias", [128, NH * STN], F32, kind="ExternalInput").ap()
    phiq8_d = nc.dram_tensor("phiq8", [32, QT * 128], F32, kind="ExternalInput").ap()
    out_d = nc.dram_tensor("out", [DIM, TOK], F32, kind="ExternalOutput").ap()

    x2s = nc.dram_tensor("x2s", [DIM, TOK], F32, kind="Internal").ap()
    d4s = nc.dram_tensor("d4s", [DIM, TOK], BF16, kind="Internal").ap()
    phis = nc.dram_tensor("phis", [QT, 32, 2, 128], BF16, kind="Internal").ap()

    identF_d = nc.inline_tensor(np.eye(128, dtype=np.float32), "identF").ap()
    onesB_d = nc.inline_tensor(np.ones((128, 128), ml_dtypes.bfloat16), "onesB").ap()
    mdiagT_d = nc.inline_tensor(np.triu(np.ones((128, 128), np.float32), 1).astype(ml_dtypes.bfloat16), "mdiagT").ap()
    medgeT_d = nc.inline_tensor(np.tril(np.ones((128, 128), np.float32), -1).astype(ml_dtypes.bfloat16), "medgeT").ap()
    negI_d = nc.inline_tensor((NEGBIG * np.eye(128, dtype=np.float32)).astype(ml_dtypes.bfloat16), "negI").ap()

    tap_d = {}
    for name, shape, dt_ in (
            ("xn", [128, DC * TOK2], BF16), ("khat", [66, NKV * TOK2], BF16),
            ("qhat", [66, NH * TOK], BF16), ("vhat", [128, STN * NKV * 65], BF16),
            ("attnout", [128, DC * TOK], BF16), ("a2", [128, DC * TOK], BF16),
            ("x2", [DIM, TOK], F32), ("xn3", [128, DC * TOK], BF16),
            ("d4", [DIM, TOK], BF16), ("hb1", [128, DC * TOK], BF16),
            ("hb2", [128, (ICN - DC) * TOK], BF16)):
        if name in taps:
            tap_d[name] = nc.dram_tensor("tap_" + name, shape, dt_, kind="ExternalOutput").ap()

    global _LAST_LOG
    L = SemLog()
    _LAST_LOG = L
    C = {"w": 0, "s": 0, "te": 0, "act": 0, "dve": 0, "g": 0}

    with (
        nc.semaphore("sW") as sW,
        nc.semaphore("sS") as sS,
        nc.semaphore("sT") as sT,
        nc.semaphore("sA") as sA,
        nc.semaphore("sV") as sV,
        nc.semaphore("sG") as sG,
        nc.Block() as block,
    ):
        SEM = {"w": sW, "s": sS, "te": sT, "act": sA, "dve": sV, "g": sG}

        def inc(engname, instr, ch, n=1):
            C[ch] += n
            instr.then_inc(SEM[ch], n)
            L.inc(engname, ch, C[ch])
            return C[ch]

        def wt(engname, eng, ch, val):
            if val is None or val <= 0:
                return
            L.wait(engname, ch, val)
            eng.wait_ge(SEM[ch], val)

        sb = nc.alloc_sbuf_tensor
        sbias_t = sb("sbias_t", [128, NH * STN], F32)
        phiq8_t = sb("phiq8_t", [32, QT * 128], F32)
        identF_t = sb("identF_t", [128, 128], F32)
        onesB_t = sb("onesB_t", [128, 128], BF16)
        mdiagT_t = sb("mdiagT_t", [128, 128], BF16)
        medgeT_t = sb("medgeT_t", [128, 128], BF16)
        negI_t = sb("negI_t", [128, 128], BF16)
        qnw_t = sb("qnw_t", [128, DC], F32)
        knw_t = sb("knw_t", [128, KMC], F32)
        gs_t = sb("gs_t", [128, ICN], F32)
        us_t = sb("us_t", [128, ICN], F32)
        ds_t = sb("ds_t", [128, DC], F32)
        wbufA = sb("wbufA", [128, DC * 128], BF16)
        wbufB = sb("wbufB", [128, DC * 128], BF16)
        stF = sb("stF", [128, TOK2], F32)
        stM = sb("stM", [128, TOK2], BF16)
        stR = sb("stR", [128, TOK2], BF16)
        sqscr = sb("sqscr", [128, TOK2], BF16)
        m0buf = sb("m0buf", [128, 32], F32)
        phisc = sb("phisc", [32, 2 * 128], BF16)
        phiA = sb("phiA", [32, 128], F32)
        pt_t = sb("pt_t", [128, 6 * QT * 128], BF16)
        zrow = sb("zrow", [1, 512], BF16)
        gptmp = sb("gptmp", [128, TOK], BF16)

        buf_last = {"A": 0, "B": 0}

        S0 = nc.alloc_psum_tensor("S0", [128, 1024], F32)
        S1 = nc.alloc_psum_tensor("S1", [128, 1024], F32)
        A0 = nc.alloc_psum_tensor("A0", [128, 1024], F32)
        A1 = nc.alloc_psum_tensor("A1", [128, 1024], F32)
        SP, APm = [S0, S1], [A0, A1]

        def p0(sy):
            for dst, src in ((sbias_t, sbias_d), (phiq8_t, phiq8_d), (identF_t, identF_d),
                             (onesB_t, onesB_d), (mdiagT_t, mdiagT_d), (medgeT_t, medgeT_d),
                             (negI_t, negI_d), (qnw_t, qnw_d), (knw_t, knw_d),
                             (gs_t, gs_d), (us_t, us_d), (ds_t, ds_d)):
                inc("sy", sy.dma_start(out=dst.ap(), in_=src), "s", 16)
        block.sync(p0)
        c_consts = C["s"]
        block.tensor(lambda t: wt("te", t, "s", c_consts))
        block.scalar(lambda s_: wt("act", s_, "s", c_consts))
        def _dve_init(v):
            wt("dve", v, "s", c_consts)
            inc("dve", v.memset(zrow.ap(), 0.0), "dve", 1)
        block.vector(_dve_init)
        c_zrow = C["dve"]

        def ln_stats(nchunks, ntok, src_fn, eps, rms_denom=None,
                     pre_wait_dve=None, chunk_wait=None, mm_pre_dve=None,
                     sq2=None):
            nh = ntok // 1024
            sq_pred = [C["act"] + i + 1 for i in range(nchunks)]
            te0 = C["te"]
            rms = rms_denom is not None
            denom = float(rms_denom if rms else DIM)
            if ntok == 1024 and sq2 is None:
                sq_bufs = [sqscr.ap()[:, 0:1024], sqscr.ap()[:, 1024:2048]]
            elif sq2 is not None:
                sq_bufs = [sqscr.ap()[:, 0:ntok], sq2]
            else:
                sq_bufs = [sqscr.ap()[:, 0:ntok]]
            nb = len(sq_bufs)

            def _sq(sc):
                if pre_wait_dve is not None:
                    wt("act", sc, "dve", pre_wait_dve)
                for mc in range(nchunks):
                    if chunk_wait is not None:
                        ch, fn = chunk_wait
                        wt("act", sc, ch, fn(mc))
                    if mc >= nb:
                        wt("act", sc, "te", te0 + mc - nb + 1)
                    ins = sc.activation(sq_bufs[mc % nb], src_fn(mc), AF.Square)
                    inc("act", ins, "act", 1)
            block.scalar(_sq)

            def _mm(t):
                if mm_pre_dve is not None:
                    wt("te", t, "dve", mm_pre_dve)
                for mc in range(nchunks):
                    wt("te", t, "act", sq_pred[mc])
                    ins = None
                    sqb = sq_bufs[mc % nb]
                    for half in range(nh):
                        for ts in range(2):
                            sl = half * 1024 + ts * 512
                            psl = slice(ts * 512, ts * 512 + 512)
                            if not rms:
                                t.matmul(APm[half].ap()[:, psl], onesB_t.ap(),
                                         src_fn(mc)[:, sl:sl + 512],
                                         start=(mc == 0), stop=(mc == nchunks - 1))
                                ins = t.matmul(SP[half].ap()[:, psl], onesB_t.ap(),
                                               sqb[:, sl:sl + 512],
                                               start=(mc == 0), stop=(mc == nchunks - 1))
                            else:
                                ins = t.matmul(APm[half].ap()[:, psl], onesB_t.ap(),
                                               sqb[:, sl:sl + 512],
                                               start=(mc == 0), stop=(mc == nchunks - 1))
                    inc("te", ins, "te", 1)
            block.tensor(_mm)
            c_stat = C["te"]

            def _mr(v):
                wt("dve", v, "te", c_stat)
                for half in range(nh):
                    hs = slice(half * 1024, half * 1024 + 1024)
                    if not rms:
                        v.tensor_scalar(stM.ap()[:, hs], APm[half].ap(), 1.0 / denom, None, ALU.mult)
                        v.tensor_scalar(stF.ap()[:, hs], SP[half].ap(), 1.0 / denom, None, ALU.mult)
                    else:
                        v.tensor_scalar(stF.ap()[:, hs], APm[half].ap(), 1.0 / denom, None, ALU.mult)
                if not rms:
                    v.tensor_tensor(sqscr.ap()[:, 0:ntok], stM.ap()[:, 0:ntok], stM.ap()[:, 0:ntok], ALU.mult)
                    v.tensor_tensor(stF.ap()[:, 0:ntok], stF.ap()[:, 0:ntok], sqscr.ap()[:, 0:ntok], ALU.subtract)
                v.tensor_scalar(stF.ap()[:, 0:ntok], stF.ap()[:, 0:ntok], float(eps), None, ALU.add)
                ins = v.reciprocal(stF.ap()[:, 0:ntok], stF.ap()[:, 0:ntok])
                inc("dve", ins, "dve", 1)
            block.vector(_mr)
            c_r1 = C["dve"]

            def _sqrt(sc):
                wt("act", sc, "dve", c_r1)
                ins = sc.activation(stR.ap()[:, 0:ntok], stF.ap()[:, 0:ntok], AF.Sqrt)
                inc("act", ins, "act", 1)
            block.scalar(_sqrt)
            return C["act"]

        def proj_sweep(n_mc, src_w, ntok, mov_fn, evac_fn,
                       pre_te_dve=None, pre_te_act=None, pre_te_g=None):
            nh = ntok // 1024
            w0 = C["w"]
            te_done = []
            evac_pred = [C["dve"] + i + 1 for i in range(n_mc * nh)]

            def _te(t):
                if pre_te_dve is not None:
                    wt("te", t, "dve", pre_te_dve)
                if pre_te_act is not None:
                    wt("te", t, "act", pre_te_act)
                if pre_te_g is not None:
                    wt("te", t, "g", pre_te_g)
                n = 0
                for mc in range(n_mc):
                    wt("te", t, "w", w0 + (mc + 1) * 16)
                    buf = wbufA if mc % 2 == 0 else wbufB
                    for half in range(nh):
                        if n >= 2:
                            wt("te", t, "dve", evac_pred[n - 2])
                        ps = SP[n % 2]
                        ins = None
                        for kc in range(DC):
                            for ts in range(2):
                                off = half * 1024 + ts * 512
                                ins = t.matmul(ps.ap()[:, ts * 512:ts * 512 + 512],
                                               buf.ap()[:, kc * 128:(kc + 1) * 128],
                                               mov_fn(kc, off),
                                               start=(kc == 0), stop=(kc == DC - 1))
                        inc("te", ins, "te", 1)
                        te_done.append(C["te"])
                        n += 1
            block.tensor(_te)

            def _dve(v):
                n = 0
                for mc in range(n_mc):
                    for half in range(nh):
                        wt("dve", v, "te", te_done[n])
                        ins = evac_fn(v, mc, half, SP[n % 2].ap())
                        inc("dve", ins, "dve", 1)
                        n += 1
            block.vector(_dve)
            assert C["dve"] == evac_pred[-1]

            def _dma(g):
                for mc in range(n_mc):
                    bname = "A" if mc % 2 == 0 else "B"
                    wv_ = te_done[(mc - 2) * nh + nh - 1] if mc >= 2 else buf_last[bname]
                    wt("pool", g, "te", wv_)
                    buf = wbufA if mc % 2 == 0 else wbufB
                    inc("pool", g.dma_start(out=buf.ap(), in_=src_w[mc]), "w", 16)
            block.gpsimd(_dma)
            for mc in range(max(0, n_mc - 2), n_mc):
                buf_last["A" if mc % 2 == 0 else "B"] = te_done[mc * nh + nh - 1]
            return te_done, evac_pred

        khat_g = nc.sbuf_tensor("khat", [66, NKV * TOK2], BF16)
        khat = khat_g.__enter__()
        vhat_g = nc.sbuf_tensor("vhat", [128, STN * NKV * 65], BF16)
        vhat = vhat_g.__enter__()
        xn_g = nc.sbuf_tensor("xn", [128, DC * TOK2], BF16)
        xn = xn_g.__enter__()
        ktmp_g = nc.sbuf_tensor("ktmp", [128, KMC * TOK2], BF16, side="right")
        ktmp = ktmp_g.__enter__()

        _scope_n = 'p1_ln1'; _scope_i = nc.enter_named_scope('p1_ln1', False)[0]
        c_xld0 = C["w"]

        def p1_dma(g):
            for kc in range(DC):
                ins = g.dma_start(out=xn.ap()[:, kc * TOK2:(kc + 1) * TOK2],
                                  in_=xt[kc * 128:(kc + 1) * 128, :])
                inc("pool", ins, "w", 16)
        block.gpsimd(p1_dma)

        c_ln1 = ln_stats(DC, TOK2, lambda mc: xn.ap()[:, mc * TOK2:(mc + 1) * TOK2],
                         LN_EPS, chunk_wait=("w", lambda mc: c_xld0 + (mc + 1) * 16),
                         sq2=(ktmp.ap()[:, 0:TOK2] if LN1_SQ2 else None))

        def p1_norm(v):
            wt("dve", v, "act", c_ln1)
            for kc in range(DC):
                sl = slice(kc * TOK2, (kc + 1) * TOK2)
                v.tensor_tensor(xn.ap()[:, sl], xn.ap()[:, sl], stM.ap(), ALU.subtract)
                ins = v.tensor_tensor(xn.ap()[:, sl], xn.ap()[:, sl], stR.ap(), ALU.mult)
                inc("dve", ins, "dve", 1)
        block.vector(p1_norm)
        c_xn = C["dve"]

        if "xn" in taps:
            def tp_xn(sy):
                wt("sy", sy, "dve", c_xn)
                inc("sy", sy.dma_start(out=tap_d["xn"], in_=xn.ap()), "s", 16)
            block.sync(tp_xn)

        nc.leave_named_scope(_scope_n, _scope_i, False)
        _scope_n = 'p2_k'; _scope_i = nc.enter_named_scope('p2_k', False)[0]
        k_te, k_ev = proj_sweep(
            KMC, wk_d, TOK2,
            lambda kc, off: xn.ap()[:, kc * TOK2 + off: kc * TOK2 + off + 512],
            lambda v, mc, half, ps: v.tensor_copy(
                ktmp.ap()[:, mc * TOK2 + half * 1024: mc * TOK2 + half * 1024 + 1024], ps),
            pre_te_dve=c_xn)

        c_krms = ln_stats(KMC, TOK2, lambda mc: ktmp.ap()[:, mc * TOK2:(mc + 1) * TOK2],
                          RMS_EPS, rms_denom=512.0,
                          chunk_wait=("dve", lambda mc: k_ev[2 * mc + 1]))

        def p2_khat(v):
            wt("dve", v, "act", c_krms)
            v.memset(khat.ap()[64:66, :], 1.0)
            for mc in range(KMC):
                for par in range(2):
                    g_head = 2 * mc + par
                    ins = v.scalar_tensor_tensor(
                        khat.ap()[0:64, g_head * TOK2:(g_head + 1) * TOK2],
                        ktmp.ap()[par * 64:(par + 1) * 64, mc * TOK2:(mc + 1) * TOK2],
                        knw_t.ap()[par * 64:(par + 1) * 64, mc:mc + 1],
                        stR.ap()[par * 64:(par + 1) * 64, 0:TOK2],
                        ALU.mult, ALU.mult)
                    inc("dve", ins, "dve", 1)
        block.vector(p2_khat)
        c_khat = C["dve"]

        if "khat" in taps:
            def tp_kh(sy):
                wt("sy", sy, "dve", c_khat)
                inc("sy", sy.dma_start(out=tap_d["khat"], in_=khat.ap()), "s", 16)
            block.sync(tp_kh)

        nc.leave_named_scope(_scope_n, _scope_i, False)
        _scope_n = 'p2_v'; _scope_i = nc.enter_named_scope('p2_v', False)[0]
        vW0 = C["w"]
        v_te = []
        v_stf, v_evh = {}, {}
        cd = C["dve"]
        for mc in range(KMC):
            for half in range(2):
                cd += 1
                v_stf[(mc, half)] = cd
            for grp in range(4):
                cd += 1
                v_evh[(mc, grp)] = cd

        def p2_v_te(t):
            wt("te", t, "dve", c_khat)
            n = 0
            for mc in range(KMC):
                wt("te", t, "w", vW0 + (mc + 1) * 16)
                buf = wbufA if mc % 2 == 0 else wbufB
                for half in range(2):
                    if n >= 2:
                        wt("te", t, "dve", v_stf[(mc - 1, half)])
                    ps = SP[n % 2]
                    ins = None
                    for kc in range(DC):
                        for ts in range(2):
                            off = half * 1024 + ts * 512
                            ins = t.matmul(ps.ap()[:, ts * 512:ts * 512 + 512],
                                           buf.ap()[:, kc * 128:(kc + 1) * 128],
                                           xn.ap()[:, kc * TOK2 + off: kc * TOK2 + off + 512],
                                           start=(kc == 0), stop=(kc == DC - 1))
                    inc("te", ins, "te", 1)
                    v_te.append(C["te"])
                    n += 1
                wt("te", t, "dve", v_stf[(mc, 1)])
                for grp in range(4):
                    if mc > 0 or grp >= 2:
                        key = (mc, grp - 2) if grp >= 2 else (mc - 1, grp + 2)
                        wt("te", t, "dve", v_evh[key])
                    ins = None
                    for j in range(4):
                        sc16 = grp * 4 + j
                        cps = ((grp % 2) * 4 + j) * 128
                        ins = t.transpose(A0.ap()[:, cps:cps + 128],
                                          stF.ap()[:, sc16 * 128:(sc16 + 1) * 128],
                                          identF_t.ap())
                    inc("te", ins, "te", 1)
        block.tensor(p2_v_te)
        v_tr = {}
        ctr = C["te"]
        tt = v_te[0] - 1
        tt = None
        cnt = v_te[0] - 1
        idx = 0
        for mc in range(KMC):
            cnt = v_te[2 * mc + 1]
            for grp in range(4):
                cnt += 1
                v_tr[(mc, grp)] = cnt
        assert cnt == C["te"]

        def p2_v_dve(v):
            n = 0
            for mc in range(KMC):
                for half in range(2):
                    wt("dve", v, "te", v_te[n])
                    ins = v.tensor_copy(stF.ap()[:, half * 1024:half * 1024 + 1024],
                                        SP[n % 2].ap())
                    inc("dve", ins, "dve", 1)
                    assert C["dve"] == v_stf[(mc, half)]
                    n += 1
                for grp in range(4):
                    wt("dve", v, "te", v_tr[(mc, grp)])
                    ins = None
                    for j in range(4):
                        sc16 = grp * 4 + j
                        cps = ((grp % 2) * 4 + j) * 128
                        for par in range(2):
                            g_head = 2 * mc + par
                            col = (sc16 * NKV + g_head) * 65
                            ins = v.tensor_copy(vhat.ap()[:, col:col + 64],
                                                A0.ap()[:, cps + par * 64: cps + par * 64 + 64])
                    inc("dve", ins, "dve", 1)
                    assert C["dve"] == v_evh[(mc, grp)]
        block.vector(p2_v_dve)

        def p2_v_dma(g):
            for mc in range(KMC):
                bname = "A" if mc % 2 == 0 else "B"
                wv_ = v_te[(mc - 2) * 2 + 1] if mc >= 2 else buf_last[bname]
                wt("pool", g, "te", wv_)
                buf = wbufA if mc % 2 == 0 else wbufB
                inc("pool", g.dma_start(out=buf.ap(), in_=wv_d[mc]), "w", 16)
        block.gpsimd(p2_v_dma)
        buf_last["A"] = v_te[5]
        buf_last["B"] = v_te[7]

        def p2_v_ones(v):
            ins = v.memset(vhat.ap().rearrange("p (b c) -> p b c", c=65)[:, :, 64:65], 1.0)
            inc("dve", ins, "dve", 1)
        block.vector(p2_v_ones)
        c_vhat = C["dve"]

        if "vhat" in taps:
            def tp_vh(sy):
                wt("sy", sy, "dve", c_vhat)
                inc("sy", sy.dma_start(out=tap_d["vhat"], in_=vhat.ap()), "s", 16)
            block.sync(tp_vh)

        ktmp_g.__exit__(None, None, None)

        nc.leave_named_scope(_scope_n, _scope_i, False)
        _scope_n = 'p2_q'; _scope_i = nc.enter_named_scope('p2_q', False)[0]
        qtmp_g = nc.sbuf_tensor("qtmp", [128, DC * TOK], BF16, side="right")
        qtmp = qtmp_g.__enter__()

        q_te, q_ev = proj_sweep(
            DC, wq_d, TOK,
            lambda kc, off: xn.ap()[:, kc * TOK2 + 1024 + off: kc * TOK2 + 1024 + off + 512],
            lambda v, mc, half, ps: v.tensor_copy(qtmp.ap()[:, mc * TOK:(mc + 1) * TOK], ps),
            pre_te_dve=c_vhat)

        c_qrms = ln_stats(DC, TOK, lambda mc: qtmp.ap()[:, mc * TOK:(mc + 1) * TOK],
                          RMS_EPS, rms_denom=float(DIM),
                          chunk_wait=("dve", lambda mc: q_ev[mc]))

        xn_g.__exit__(None, None, None)
        qhat_g = nc.sbuf_tensor("qhat", [66, NH * TOK], BF16)
        qhat = qhat_g.__enter__()

        def p2_qhat(v):
            wt("dve", v, "act", c_qrms)
            for mc in range(DC):
                for par in range(2):
                    head = 2 * mc + par
                    ins = v.scalar_tensor_tensor(
                        qhat.ap()[0:64, head * TOK:(head + 1) * TOK],
                        qtmp.ap()[par * 64:(par + 1) * 64, mc * TOK:(mc + 1) * TOK],
                        qnw_t.ap()[par * 64:(par + 1) * 64, mc:mc + 1],
                        stR.ap()[par * 64:(par + 1) * 64, 0:TOK],
                        ALU.mult, ALU.mult)
                    inc("dve", ins, "dve", 1)
        block.vector(p2_qhat)
        c_qhat_rows = C["dve"]

        nc.leave_named_scope(_scope_n, _scope_i, False)
        _scope_n = 'p3a_diag'; _scope_i = nc.enter_named_scope('p3a_diag', False)[0]
        diag_te = {}
        diag_max = {}
        ph_stt = {}
        cd = C["dve"]
        for qt in range(QT):
            for grp in range(4):
                cd += 1
                diag_max[(qt, grp)] = cd
            cd += 1
            ph_stt[qt] = cd
        te_p3a0 = C["te"]
        ph_tr_pred = {qt: te_p3a0 + 5 * qt + 5 for qt in range(QT)}

        def p3a_te(t):
            wt("te", t, "dve", c_qhat_rows)
            for qt in range(QT):
                st = qt + 8
                for grp in range(4):
                    if qt > 0 or grp >= 2:
                        key = (qt, grp - 2) if grp >= 2 else (qt - 1, grp + 2)
                        wt("te", t, "dve", diag_max[key])
                    ins = None
                    for j in range(8):
                        h = grp * 8 + j
                        g_head = h // 4
                        ins = t.matmul(SP[grp % 2].ap()[:, j * 128:(j + 1) * 128],
                                       qhat.ap()[0:64, h * TOK + qt * 128: h * TOK + qt * 128 + 128],
                                       khat.ap()[0:64, g_head * TOK2 + st * 128: g_head * TOK2 + st * 128 + 128],
                                       start=True, stop=True)
                    inc("te", ins, "te", 1)
                    diag_te[(qt, grp)] = C["te"]
                wt("te", t, "dve", diag_max[(qt, 3)])
                if qt >= 2:
                    wt("te", t, "dve", ph_stt[qt - 2])
                ins = t.transpose(A1.ap()[0:32, (qt % 2) * 512:(qt % 2) * 512 + 128],
                                  m0buf.ap(), identF_t.ap())
                inc("te", ins, "te", 1)
                assert C["te"] == ph_tr_pred[qt]
        block.tensor(p3a_te)

        s_p3a0 = C["s"]

        def p3a_dve(v):
            for qt in range(QT):
                for grp in range(4):
                    wt("dve", v, "te", diag_te[(qt, grp)])
                    if qt >= 1 and grp == 0:
                        wt("dve", v, "te", ph_tr_pred[qt - 1])
                    ins = None
                    for j in range(8):
                        h = grp * 8 + j
                        ins = v.reduce_max(m0buf.ap()[:, h:h + 1],
                                           SP[grp % 2].ap()[:, j * 128:(j + 1) * 128],
                                           axis=AX.X, negate=True)
                    inc("dve", ins, "dve", 1)
                    assert C["dve"] == diag_max[(qt, grp)]
                wt("dve", v, "te", ph_tr_pred[qt])
                if qt >= 1:
                    wt("dve", v, "s", s_p3a0 + 32 * (qt - 1) + 16)
                ps = A1.ap()[0:32, (qt % 2) * 512:(qt % 2) * 512 + 128]
                v.tensor_tensor(phiA.ap(), ps, phiq8_t.ap()[:, qt * 128:(qt + 1) * 128], ALU.subtract)
                v.tensor_copy(phisc.ap()[:, 0:128], phiA.ap())
                ins = v.tensor_tensor(phisc.ap()[:, 128:256], phiA.ap(), phisc.ap()[:, 0:128], ALU.subtract)
                inc("dve", ins, "dve", 1)
                assert C["dve"] == ph_stt[qt]
        block.vector(p3a_dve)

        def p3a_dma(sy):
            for qt in range(QT):
                wt("sy", sy, "dve", ph_stt[qt])
                ins = sy.dma_start(out=phis[qt],
                                   in_=phisc.ap().rearrange("h (e q) -> h e q", e=2))
                inc("sy", ins, "s", 16)
                wt("sy", sy, "s", C["s"])
                ins = sy.dma_start(
                    out=qhat.ap()[64:66, :].rearrange("e (h q) -> e h q", h=32)[:, :, qt * 128:(qt + 1) * 128],
                    in_=phis[qt].rearrange("h e q -> e h q"))
                inc("sy", ins, "s", 16)
        block.sync(p3a_dma)
        c_phi_all = C["s"]

        if "qhat" in taps:
            def tp_qh(sy):
                wt("sy", sy, "s", c_phi_all)
                inc("sy", sy.dma_start(out=tap_d["qhat"], in_=qhat.ap()), "s", 16)
            block.sync(tp_qh)

        nc.leave_named_scope(_scope_n, _scope_i, False)
        _scope_n = 'p3b_attn'; _scope_i = nc.enter_named_scope('p3b_attn', False)[0]
        def pt_ap(h, st, lo, hi):
            base = ((h % 2) * 3 + (st % PT_LAG)) * QT * 128
            return pt_t.ap()[:, base + lo * 128: base + (hi + 1) * 128]

        sched = []
        exp_order = []
        for hp in range(0, NH, 2):
            hs = [hp, hp + 1]
            stl = {h: _st_list(h) for h in hs}
            union = sorted(set(stl[hp]) | set(stl[hp + 1]))
            pend = {h: [] for h in hs}
            zf_done = {h: False for h in hs}
            ndiv = 0
            for st in union:
                for h in hs:
                    if st in stl[h]:
                        sched.append(("kq", h, st))
                        exp_order.append((h, st))
                        pend[h].append(st)
                if hp >= 2 and ndiv < 2:
                    sched.append(("div", hp - 2 + ndiv))
                    ndiv += 1
                for h in hs:
                    if len(pend[h]) >= PT_LAG:
                        if not zf_done[h]:
                            sched.append(("zf", h))
                            zf_done[h] = True
                        sched.append(("av", h, pend[h].pop(0)))
            for h in hs:
                while pend[h]:
                    if not zf_done[h]:
                        sched.append(("zf", h))
                        zf_done[h] = True
                    sched.append(("av", h, pend[h].pop(0)))
        sched.append(("div", NH - 2))
        sched.append(("div", NH - 1))

        exp_pred = {}
        ca = C["act"]
        for key in exp_order:
            ca += 1
            exp_pred[key] = ca
        prev_s_tenant = {}
        lastp = {0: None, 1: None}
        for (h, st) in exp_order:
            p = h % 2
            if lastp[p] is not None:
                prev_s_tenant[(h, st)] = lastp[p]
            lastp[p] = (h, st)
        prev_pt_tenant = {}
        hist = {}
        for (h, st) in exp_order:
            hist.setdefault(h, []).append(st)
        for h, sts_h in hist.items():
            for i, st in enumerate(sts_h):
                if i >= PT_LAG:
                    prev_pt_tenant[(h, st)] = (h, sts_h[i - PT_LAG])
        den_pred, div_pred = {}, {}
        cd = C["dve"]
        for h in range(NH):
            cd += 1
            den_pred[h] = cd
            cd += 1
            div_pred[h] = cd
        kq_done, av_done, div_te = {}, {}, {}
        c_diag_last = diag_max[(QT - 1, 3)]
        c_ph_last = ph_stt[QT - 1]

        def p3b_te(t):
            wt("te", t, "s", c_phi_all)
            wt("te", t, "dve", c_zrow)
            for op in sched:
                kind = op[0]
                if kind == "kq":
                    _, h, st = op
                    g_head = h // 4
                    lo, hi = _vqt(h, st)
                    prev = prev_s_tenant.get((h, st))
                    if prev is not None:
                        wt("te", t, "act", exp_pred[prev])
                    else:
                        wt("te", t, "dve", c_diag_last)
                    diag_q = st - 8
                    edge_q = st if (KEPT[h] == 9 and st <= 7) else None
                    ins = None
                    for bank in range(2):
                        qlist = [qt for qt in range(lo, hi + 1) if qt // 4 == bank]
                        masked = [qt for qt in qlist
                                  if qt == diag_q or (edge_q is not None and qt == edge_q)]
                        normal = [qt for qt in qlist if qt not in masked]
                        for qt in normal:
                            ins = t.matmul(SP[h % 2].ap()[:, qt * 128:(qt + 1) * 128],
                                           khat.ap()[0:66, g_head * TOK2 + st * 128: g_head * TOK2 + st * 128 + 128],
                                           qhat.ap()[0:66, h * TOK + qt * 128: h * TOK + qt * 128 + 128],
                                           start=True, stop=True)
                        for qt in masked:
                            mt = mdiagT_t if qt == diag_q else medgeT_t
                            t.matmul(SP[h % 2].ap()[:, qt * 128:(qt + 1) * 128],
                                     mt.ap(), negI_t.ap(), start=True, stop=False)
                            ins = t.matmul(SP[h % 2].ap()[:, qt * 128:(qt + 1) * 128],
                                           khat.ap()[0:66, g_head * TOK2 + st * 128: g_head * TOK2 + st * 128 + 128],
                                           qhat.ap()[0:66, h * TOK + qt * 128: h * TOK + qt * 128 + 128],
                                           start=False, stop=True)
                    inc("te", ins, "te", 1)
                    kq_done[(h, st)] = C["te"]
                elif kind == "av":
                    _, h, st = op
                    g_head = h // 4
                    lo, hi = _vqt(h, st)
                    wt("te", t, "act", exp_pred[(h, st)])
                    ins = None
                    for qt in range(lo, hi + 1):
                        last = (st == qt + 8)
                        vcol = (st * NKV + g_head) * 65
                        ins = t.matmul(APm[h % 2].ap()[0:65, qt * 128:(qt + 1) * 128],
                                       vhat.ap()[:, vcol:vcol + 65],
                                       pt_ap(h, st, qt, qt),
                                       start=False, stop=last)
                    inc("te", ins, "te", 1)
                    av_done[(h, st)] = C["te"]
                elif kind == "zf":
                    _, h = op
                    wt("te", t, "dve", div_pred[h - 2] if h >= 2 else c_ph_last)
                    for bank in range(2):
                        t.matmul(APm[h % 2].ap()[0:65, bank * 512:bank * 512 + 512],
                                 onesB_t.ap()[0:1, 0:65], zrow.ap(),
                                 start=True, stop=False)
                else:
                    _, hd = op
                    wt("te", t, "dve", den_pred[hd])
                    ins = None
                    for ts in range(2):
                        ins = t.matmul(APm[hd % 2].ap()[64:128, ts * 512:ts * 512 + 512],
                                       onesB_t.ap()[0:1, 0:64],
                                       stM.ap()[0:1, ts * 512:ts * 512 + 512],
                                       start=True, stop=True, tile_position=(0, 64))
                    inc("te", ins, "te", 1)
                    div_te[hd] = C["te"]
        block.tensor(p3b_te)

        def p3b_act(sc):
            for (h, st) in exp_order:
                lo, hi = _vqt(h, st)
                wt("act", sc, "te", kq_done[(h, st)])
                pv = prev_pt_tenant.get((h, st))
                if pv is not None:
                    wt("act", sc, "te", av_done[pv])
                ins = sc.activation(
                    pt_ap(h, st, lo, hi),
                    SP[h % 2].ap()[:, lo * 128:(hi + 1) * 128],
                    AF.Exp, bias=sbias_t.ap()[:, h * STN + st: h * STN + st + 1],
                    scale=0.125)
                inc("act", ins, "act", 1)
                assert C["act"] == exp_pred[(h, st)]
        block.scalar(p3b_act)

        attnout = qtmp

        avE = sqscr

        bg_pred = {}
        cg = C["g"]
        for h in range(NH):
            cg += 1
            bg_pred[h] = cg

        def _bg(v, h):
            ebuf = avE.ap()[64:128, (h % 2) * TOK:(h % 2) * TOK + TOK]
            rb = stF.ap()[64:128, (h % 2) * TOK:(h % 2) * TOK + TOK]
            mc = h // 2
            if h % 2 == 1:
                ins = v.tensor_tensor(attnout.ap()[64:128, mc * TOK:(mc + 1) * TOK],
                                      ebuf, rb, ALU.mult)
            else:
                v.tensor_tensor(gptmp.ap()[64:128, :], ebuf, rb, ALU.mult)
                ins = v.tensor_copy(attnout.ap()[0:64, mc * TOK:(mc + 1) * TOK],
                                    gptmp.ap()[64:128, :])
            inc("dve", ins, "g", 1)
            assert C["g"] == bg_pred[h]

        def p3b_dve(v):
            for h in range(NH):
                sts_h = hist[h]
                wt("dve", v, "te", av_done[(h, sts_h[-1])])
                if h >= 1:
                    wt("dve", v, "te", div_te[h - 1])
                ins = v.tensor_copy(stM.ap()[0:1, 0:TOK], APm[h % 2].ap()[64:65, :])
                inc("dve", ins, "dve", 1)
                assert C["dve"] == den_pred[h]
                wt("dve", v, "te", div_te[h])
                ebuf = avE.ap()[64:128, (h % 2) * TOK:(h % 2) * TOK + TOK]
                v.reciprocal(stF.ap()[64:128, (h % 2) * TOK:(h % 2) * TOK + TOK],
                             APm[h % 2].ap()[64:128, :])
                ins = v.tensor_copy(ebuf, APm[h % 2].ap()[0:64, :])
                inc("dve", ins, "dve", 1)
                assert C["dve"] == div_pred[h]
                if h >= 1:
                    _bg(v, h - 1)
            _bg(v, NH - 1)
        block.vector(p3b_dve)
        c_attnout = C["dve"]
        c_bg_all = C["g"]
        c_exp_last = exp_pred[exp_order[-1]]

        if "attnout" in taps:
            def tp_ao(sy):
                wt("sy", sy, "dve", c_attnout)
                inc("sy", sy.dma_start(out=tap_d["attnout"], in_=attnout.ap()), "s", 16)
            block.sync(tp_ao)

        nc.leave_named_scope(_scope_n, _scope_i, False)
        _scope_n = 'p4_wo'; _scope_i = nc.enter_named_scope('p4_wo', False)[0]
        qhat_g.__exit__(None, None, None)
        vhat_g.__exit__(None, None, None)
        khat_g.__exit__(None, None, None)
        a2_g = nc.sbuf_tensor("a2", [128, DC * TOK], BF16)
        a2 = a2_g.__enter__()
        ot_g = nc.sbuf_tensor("ot_t", [128, TOK], F32, side="right")
        ot_t = ot_g.__enter__()
        ot2_g = nc.sbuf_tensor("ot_t2", [128, TOK], F32, side="right")
        ot_t2 = ot2_g.__enter__()
        xrs_g = nc.sbuf_tensor("xrs", [128, TOK], F32, side="right")
        xrs = xrs_g.__enter__()
        xrs2_g = nc.sbuf_tensor("xrs2", [128, TOK], F32, side="right")
        xrs2 = xrs2_g.__enter__()
        x2bf_g = nc.sbuf_tensor("x2bf", [128, DC * TOK], BF16, side="right")
        x2bf = x2bf_g.__enter__()

        wo_te, wo_ev = proj_sweep(
            DC, wo_d, TOK,
            lambda kc, off: attnout.ap()[:, kc * TOK + off: kc * TOK + off + 512],
            lambda v, mc, half, ps: v.tensor_copy(a2.ap()[:, mc * TOK:(mc + 1) * TOK], ps),
            pre_te_dve=c_attnout, pre_te_act=c_exp_last, pre_te_g=c_bg_all)

        c_ln2 = ln_stats(DC, TOK, lambda mc: a2.ap()[:, mc * TOK:(mc + 1) * TOK],
                         LN_EPS, chunk_wait=("dve", lambda mc: wo_ev[mc]),
                         mm_pre_dve=wo_ev[-1])

        if "a2" in taps:
            def tp_a2(sy):
                wt("sy", sy, "dve", wo_ev[-1])
                inc("sy", sy.dma_start(out=tap_d["a2"], in_=a2.ap()), "s", 16)
            block.sync(tp_a2)

        x2_dve_pred = [C["dve"] + mc + 1 for mc in range(DC)]
        cs = C["s"]
        ld_pred, x2_st_pred = {}, {}
        for mc in range(DC):
            cs += 16
            ld_pred[mc] = cs
            if mc >= 1:
                cs += 16
                x2_st_pred[mc - 1] = cs
        cs += 16
        x2_st_pred[DC - 1] = cs

        def p4b_sync(sy):
            wt("sy", sy, "dve", c_attnout)
            wt("sy", sy, "g", c_bg_all)
            for mc in range(DC):
                if mc >= 2:
                    wt("sy", sy, "dve", x2_dve_pred[mc - 2])
                ins = sy.dma_start(out=(xrs if mc % 2 == 0 else xrs2).ap(),
                                   in_=xt[mc * 128:(mc + 1) * 128, 1024:2048])
                inc("sy", ins, "s", 16)
                assert C["s"] == ld_pred[mc]
                if mc >= 1:
                    wt("sy", sy, "dve", x2_dve_pred[mc - 1])
                    ins = sy.dma_start(out=x2s[(mc - 1) * 128:mc * 128, :],
                                       in_=(ot_t if (mc - 1) % 2 == 0 else ot_t2).ap())
                    inc("sy", ins, "s", 16)
                    assert C["s"] == x2_st_pred[mc - 1]
            wt("sy", sy, "dve", x2_dve_pred[DC - 1])
            ins = sy.dma_start(out=x2s[(DC - 1) * 128:DC * 128, :],
                               in_=(ot_t if (DC - 1) % 2 == 0 else ot_t2).ap())
            inc("sy", ins, "s", 16)
            assert C["s"] == x2_st_pred[DC - 1]
        block.sync(p4b_sync)

        x2bf_cast_pred = [c_ln2 + mc + 1 for mc in range(DC)]

        def p4b_dve(v):
            wt("dve", v, "act", c_ln2)
            for mc in range(DC):
                wt("dve", v, "s", ld_pred[mc])
                if mc >= 2:
                    wt("dve", v, "s", x2_st_pred[mc - 2])
                    wt("dve", v, "act", x2bf_cast_pred[mc - 2])
                buf = (xrs if mc % 2 == 0 else xrs2).ap()
                ob = (ot_t if mc % 2 == 0 else ot_t2).ap()
                a2c = a2.ap()[:, mc * TOK:(mc + 1) * TOK]
                v.tensor_tensor(ob, a2c, stM.ap()[:, 0:TOK], ALU.subtract)
                v.tensor_tensor(ob, ob, stR.ap()[:, 0:TOK], ALU.mult)
                ins = v.tensor_tensor(ob, ob, buf, ALU.add)
                inc("dve", ins, "dve", 1)
                assert C["dve"] == x2_dve_pred[mc]
        block.vector(p4b_dve)
        c_x2 = C["dve"]

        def p4b_cast(sc):
            for mc in range(DC):
                wt("act", sc, "dve", x2_dve_pred[mc])
                ob = (ot_t if mc % 2 == 0 else ot_t2).ap()
                ins = sc.activation(x2bf.ap()[:, mc * TOK:(mc + 1) * TOK], ob, AF.Copy)
                assert inc("act", ins, "act", 1) == x2bf_cast_pred[mc]
        block.scalar(p4b_cast)

        if "x2" in taps:
            def tp_x2(sy):
                wt("sy", sy, "s", x2_st_pred[DC - 1])
                inc("sy", sy.dma_start(out=tap_d["x2"], in_=x2s), "s", 16)
            block.sync(tp_x2)



        nc.leave_named_scope(_scope_n, _scope_i, False)
        _scope_n = 'p5_ffn'; _scope_i = nc.enter_named_scope('p5_ffn', False)[0]
        a2_g.__exit__(None, None, None)
        xn3_g = nc.sbuf_tensor("xn3", [128, DC * TOK], BF16)
        xn3 = xn3_g.__enter__()

        c_ln3 = ln_stats(DC, TOK, lambda mc: x2bf.ap()[:, mc * TOK:(mc + 1) * TOK],
                         LN_EPS, chunk_wait=("act", lambda mc: x2bf_cast_pred[mc]))

        def p5_xn3(v):
            wt("dve", v, "act", c_ln3)
            for mc in range(DC):
                sl = slice(mc * TOK, (mc + 1) * TOK)
                v.tensor_tensor(xn3.ap()[:, sl], x2bf.ap()[:, sl], stM.ap()[:, 0:TOK], ALU.subtract)
                ins = v.tensor_tensor(xn3.ap()[:, sl], xn3.ap()[:, sl], stR.ap()[:, 0:TOK], ALU.mult)
                inc("dve", ins, "dve", 1)
        block.vector(p5_xn3)
        c_xn3 = C["dve"]

        if "xn3" in taps:
            def tp_x3(sy):
                wt("sy", sy, "dve", c_xn3)
                inc("sy", sy.dma_start(out=tap_d["xn3"], in_=xn3.ap()), "s", 16)
            block.sync(tp_x3)

        x2bf_g.__exit__(None, None, None)
        xrs2_g.__exit__(None, None, None)
        xrs_g.__exit__(None, None, None)
        ot2_g.__exit__(None, None, None)
        ot_g.__exit__(None, None, None)

        hbuf2_g = nc.sbuf_tensor("hbuf2", [128, (ICN - DC) * TOK], BF16)
        hbuf2 = hbuf2_g.__enter__()
        wbufC_g = nc.sbuf_tensor("wbufC", [128, DC * 128], BF16)
        wbufC = wbufC_g.__enter__()
        wbufD_g = nc.sbuf_tensor("wbufD", [128, DC * 128], BF16)
        wbufD = wbufD_g.__enter__()
        wdb1_g = nc.sbuf_tensor("wdbuf", [128, 22 * 128], BF16)
        wdbuf = wdb1_g.__enter__()
        wdb2_g = nc.sbuf_tensor("wdbuf2", [128, 22 * 128], BF16)
        wdbuf2 = wdb2_g.__enter__()
        wdb3_g = nc.sbuf_tensor("wdbuf3", [128, 22 * 128], BF16)
        wdbuf3 = wdb3_g.__enter__()
        WDB = [wdbuf, wdbuf2, wdbuf3]

        def h_ap(ic):
            if ic < DC:
                return attnout.ap()[:, ic * TOK:(ic + 1) * TOK]
            return hbuf2.ap()[:, (ic - DC) * TOK:(ic - DC + 1) * TOK]

        guW0 = C["w"]
        gu_te = []
        gu_silu_pred = [C["act"] + ic + 1 for ic in range(ICN)]
        gu_h_pred = [C["dve"] + 2 * (ic + 1) for ic in range(ICN)]

        def p5_gu_te(t):
            wt("te", t, "dve", c_xn3)
            for ic in range(ICN):
                wt("te", t, "w", guW0 + (ic + 1) * 32)
                if ic >= 2:
                    wt("te", t, "act", gu_silu_pred[ic - 2])
                    wt("te", t, "dve", gu_h_pred[ic - 2])
                gps, ups = SP[ic % 2], APm[ic % 2]
                gbuf = wbufA if ic % 2 == 0 else wbufB
                ubuf = wbufC if ic % 2 == 0 else wbufD
                insu = None
                for kc in range(DC):
                    for ts in range(2):
                        mv = xn3.ap()[:, kc * TOK + ts * 512: kc * TOK + ts * 512 + 512]
                        t.matmul(gps.ap()[:, ts * 512:ts * 512 + 512],
                                 gbuf.ap()[:, kc * 128:(kc + 1) * 128], mv,
                                 start=(kc == 0), stop=(kc == DC - 1))
                        insu = t.matmul(ups.ap()[:, ts * 512:ts * 512 + 512],
                                        ubuf.ap()[:, kc * 128:(kc + 1) * 128], mv,
                                        start=(kc == 0), stop=(kc == DC - 1))
                inc("te", insu, "te", 1)
                gu_te.append(C["te"])
        block.tensor(p5_gu_te)

        def p5_gu_dma(g):
            wt("pool", g, "dve", c_xn3)
            for ic in range(ICN):
                if ic >= 2:
                    wt("pool", g, "te", gu_te[ic - 2])
                else:
                    wt("pool", g, "te", max(buf_last["A"], buf_last["B"]))
                gbuf = wbufA if ic % 2 == 0 else wbufB
                ubuf = wbufC if ic % 2 == 0 else wbufD
                inc("pool", g.dma_start(out=gbuf.ap(), in_=wg_d[ic]), "w", 16)
                inc("pool", g.dma_start(out=ubuf.ap(), in_=wu_d[ic]), "w", 16)
        block.gpsimd(p5_gu_dma)

        def p5_gu_act(sc):
            for ic in range(ICN):
                wt("act", sc, "te", gu_te[ic])
                ins = sc.activation(h_ap(ic), SP[ic % 2].ap(),
                                    AF.Silu, scale=gs_t.ap()[:, ic:ic + 1])
                inc("act", ins, "act", 1)
                assert C["act"] == gu_silu_pred[ic]
        block.scalar(p5_gu_act)

        def p5_gu_dve(v):
            for ic in range(ICN):
                wt("dve", v, "te", gu_te[ic])
                wt("dve", v, "act", gu_silu_pred[ic])
                ins = v.tensor_scalar(sqscr.ap()[:, 0:TOK], APm[ic % 2].ap(),
                                      us_t.ap()[:, ic:ic + 1], None, ALU.mult)
                inc("dve", ins, "dve", 1)
                ins = v.tensor_tensor(h_ap(ic), h_ap(ic), sqscr.ap()[:, 0:TOK], ALU.mult)
                inc("dve", ins, "dve", 1)
                assert C["dve"] == gu_h_pred[ic]
        block.vector(p5_gu_dve)
        c_h_done = C["dve"]

        if "hb1" in taps:
            def tp_hb(sy):
                wt("sy", sy, "dve", c_h_done)
                inc("sy", sy.dma_start(out=tap_d["hb1"], in_=attnout.ap()), "s", 16)
                inc("sy", sy.dma_start(out=tap_d["hb2"], in_=hbuf2.ap()), "s", 16)
            block.sync(tp_hb)

        dnW0 = C["w"]
        dn_te = []
        dn_half = []
        dn_ev_pred = [C["dve"] + mc + 1 for mc in range(DC)]

        def p5_dn_te(t):
            wt("te", t, "dve", c_h_done)
            wt("te", t, "act", gu_silu_pred[-1])
            for mc in range(DC):
                if mc >= 2:
                    wt("te", t, "dve", dn_ev_pred[mc - 2])
                ps = SP[mc % 2]
                ins = None
                for half in range(2):
                    n = 2 * mc + half
                    wt("te", t, "w", dnW0 + (n + 1) * 16)
                    buf = WDB[n % 3]
                    for ici in range(22):
                        ic = half * 22 + ici
                        for ts in range(2):
                            ins = t.matmul(ps.ap()[:, ts * 512:ts * 512 + 512],
                                           buf.ap()[:, ici * 128:(ici + 1) * 128],
                                           h_ap(ic)[:, ts * 512:ts * 512 + 512],
                                           start=(ic == 0), stop=(ic == ICN - 1))
                    inc("te", ins, "te", 1)
                    dn_half.append(C["te"])
                dn_te.append(C["te"])
        block.tensor(p5_dn_te)

        def p5_dn_dma(g):
            wt("pool", g, "dve", c_xn3)
            for n in range(2 * DC):
                if n >= 3:
                    wt("pool", g, "te", dn_half[n - 3])
                buf = WDB[n % 3]
                half = n % 2
                inc("pool", g.dma_start(out=buf.ap(),
                                        in_=wd_d[n // 2][:, half * 22 * 128:(half + 1) * 22 * 128]),
                    "w", 16)
        block.gpsimd(p5_dn_dma)

        d4r = xn3

        def p5_dn_ev(v):
            for mc in range(DC):
                wt("dve", v, "te", dn_te[mc])
                ins = v.tensor_scalar(d4r.ap()[:, mc * TOK:(mc + 1) * TOK],
                                      SP[mc % 2].ap(), ds_t.ap()[:, mc:mc + 1], None, ALU.mult)
                inc("dve", ins, "dve", 1)
                assert C["dve"] == dn_ev_pred[mc]
        block.vector(p5_dn_ev)

        if "d4" in taps:
            def tp_d4(sy):
                wt("sy", sy, "dve", dn_ev_pred[-1])
                inc("sy", sy.dma_start(
                    out=tap_d["d4"].rearrange("(c p) t -> p c t", p=128),
                    in_=d4r.ap().rearrange("p (c t) -> p c t", t=TOK)), "s", 16)
            block.sync(tp_d4)

        nc.leave_named_scope(_scope_n, _scope_i, False)
        _scope_n = 'p6_out'; _scope_i = nc.enter_named_scope('p6_out', False)[0]
        wdb3_g.__exit__(None, None, None)
        wdb2_g.__exit__(None, None, None)
        wdb1_g.__exit__(None, None, None)
        wbufD_g.__exit__(None, None, None)
        wbufC_g.__exit__(None, None, None)
        hbuf2_g.__exit__(None, None, None)
        c_ln4 = ln_stats(DC, TOK, lambda mc: d4r.ap()[:, mc * TOK:(mc + 1) * TOK],
                         LN_EPS, chunk_wait=("dve", lambda mc: dn_ev_pred[mc]))

        fx_g = nc.sbuf_tensor("fx", [128, TOK], F32, side="right")
        fx = fx_g.__enter__()
        fx2_g = nc.sbuf_tensor("fx2", [128, TOK], F32, side="right")
        fx2 = fx2_g.__enter__()
        fot_g = nc.sbuf_tensor("fot", [128, TOK], F32, side="right")
        fot = fot_g.__enter__()
        fot2_g = nc.sbuf_tensor("fot2", [128, TOK], F32, side="right")
        fot2 = fot2_g.__enter__()
        fgx_g = nc.sbuf_tensor("fgx", [128, TOK], F32, side="right")
        fgx = fgx_g.__enter__()
        fgo_g = nc.sbuf_tensor("fgo", [128, TOK], F32, side="right")
        fgo = fgo_g.__enter__()
        FXB = [fx, fx2]
        FOB = [fot, fot2]
        GP_FIN = 11

        fin_dve_pred = [C["dve"] + mc + 1 for mc in range(GP_FIN)]
        gp_pred0 = C["g"]
        gp_fin_pred = [gp_pred0 + i + 1 for i in range(DC - GP_FIN)]
        cs = C["s"]
        fin_ld_pred, out_st_pred = {}, {}
        gfin_ld_pred, gout_st_pred = {}, {}
        for mc in range(GP_FIN):
            cs += 16
            fin_ld_pred[mc] = cs
            if mc >= 2:
                cs += 16
                out_st_pred[mc - 2] = cs
        for mc in range(GP_FIN - 2, GP_FIN):
            cs += 16
            out_st_pred[mc] = cs
        for mc in range(GP_FIN, DC):
            cs += 16
            gfin_ld_pred[mc] = cs
            cs += 16
            gout_st_pred[mc] = cs

        def p6b_sync(sy):
            wt("sy", sy, "s", x2_st_pred[DC - 1])
            wt("sy", sy, "act", x2bf_cast_pred[DC - 1])
            wt("sy", sy, "dve", x2_dve_pred[DC - 1])
            for mc in range(GP_FIN):
                if mc >= 2:
                    wt("sy", sy, "dve", fin_dve_pred[mc - 2])
                ins = sy.dma_start(out=FXB[mc % 2].ap(),
                                   in_=x2s[mc * 128:(mc + 1) * 128, :])
                inc("sy", ins, "s", 16)
                assert C["s"] == fin_ld_pred[mc]
                if mc >= 2:
                    ins = sy.dma_start(out=out_d[(mc - 2) * 128:(mc - 1) * 128, :],
                                       in_=FOB[(mc - 2) % 2].ap())
                    inc("sy", ins, "s", 16)
                    assert C["s"] == out_st_pred[mc - 2]
            for mc in range(GP_FIN - 2, GP_FIN):
                wt("sy", sy, "dve", fin_dve_pred[mc])
                ins = sy.dma_start(out=out_d[mc * 128:(mc + 1) * 128, :],
                                   in_=FOB[mc % 2].ap())
                inc("sy", ins, "s", 16)
                assert C["s"] == out_st_pred[mc]
            for mc in range(GP_FIN, DC):
                i = mc - GP_FIN
                if i >= 1:
                    wt("sy", sy, "g", gp_fin_pred[i - 1])
                ins = sy.dma_start(out=fgx.ap(), in_=x2s[mc * 128:(mc + 1) * 128, :])
                inc("sy", ins, "s", 16)
                assert C["s"] == gfin_ld_pred[mc]
                wt("sy", sy, "g", gp_fin_pred[i])
                ins = sy.dma_start(out=out_d[mc * 128:(mc + 1) * 128, :], in_=fgo.ap())
                inc("sy", ins, "s", 16)
                assert C["s"] == gout_st_pred[mc]
        block.sync(p6b_sync)

        def p6b_dve(v):
            wt("dve", v, "act", c_ln4)
            for mc in range(GP_FIN):
                wt("dve", v, "s", fin_ld_pred[mc])
                if mc >= 2:
                    wt("dve", v, "s", out_st_pred[mc - 2])
                bufx = FXB[mc % 2].ap()
                ob = FOB[mc % 2].ap()
                v.tensor_tensor(ob, d4r.ap()[:, mc * TOK:(mc + 1) * TOK],
                                stM.ap()[:, 0:TOK], ALU.subtract)
                v.tensor_tensor(ob, ob, stR.ap()[:, 0:TOK], ALU.mult)
                ins = v.tensor_tensor(ob, ob, bufx, ALU.add)
                inc("dve", ins, "dve", 1)
                assert C["dve"] == fin_dve_pred[mc]
        block.vector(p6b_dve)

        def p6b_gp(g):
            wt("pool", g, "act", c_ln4)
            for i, mc in enumerate(range(GP_FIN, DC)):
                wt("pool", g, "s", gfin_ld_pred[mc])
                g.tensor_tensor(fgo.ap(), d4r.ap()[:, mc * TOK:(mc + 1) * TOK],
                                stM.ap()[:, 0:TOK], ALU.subtract)
                g.tensor_tensor(fgo.ap(), fgo.ap(), stR.ap()[:, 0:TOK], ALU.mult)
                ins = g.tensor_tensor(fgo.ap(), fgo.ap(), fgx.ap(), ALU.add)
                inc("pool", ins, "g", 1)
                assert C["g"] == gp_fin_pred[i]
        block.gpsimd(p6b_gp)



        nc.leave_named_scope(_scope_n, _scope_i, False)

        def p_final(sy):
            wt("sy", sy, "s", C["s"])
            wt("sy", sy, "w", C["w"])
        block.sync(p_final)

    L.audit()
    def _rng(n):
        ml = nc.lookup_mloc(n)
        return (ml.addr, ml.addr + ml.size() // 128)
    live_p5 = [_rng(n) for n in ("xn3", "hbuf2", "wbufC", "wbufD", "wdbuf", "wdbuf2", "qtmp")]
    for n in ("fx", "fx2", "fot", "fot2"):
        a, b = _rng(n)
        for (c0, c1) in live_p5:
            assert b <= c0 or a >= c1, (n, (a, b), (c0, c1))
    return nc


_BUILD_CACHE = {}


def get_nc(taps=()):
    key = tuple(sorted(taps))
    if key not in _BUILD_CACHE:
        _BUILD_CACHE[key] = build_nc(taps)
    return _BUILD_CACHE[key]


def kernel(**inputs):
    maps = _stage_inputs(inputs)
    nc = get_nc()
    res = run_bass_kernel_spmd(nc, maps, core_ids=list(range(8)))
    return _gather_output(res.results)


# version 09
# speedup vs baseline: 1.1170x; 1.0316x over previous
import numpy as np
import ml_dtypes
import concourse.bass as bass
import concourse.mybir as mybir
from concourse.bass_utils import run_bass_kernel_spmd

F32 = mybir.dt.float32
BF16 = mybir.dt.bfloat16
AF = mybir.ActivationFunctionType
ALU = mybir.AluOpType
AX = mybir.AxisListType

DIM, DC = 2048, 16
NH, NKV, HD = 32, 8, 64
TOK, TOK2 = 1024, 2048
QT, STN = 8, 16
INTER, ICN = 5632, 44
KMC = 4
LN_EPS, RMS_EPS = 1e-5, 1e-6
NEGBIG = -1.0e30
P3B_BG_ON_DVE = False
PT_LAG = 3
LN1_SQ2 = True

SLOPES = [2.0 ** (-h / 4.0) for h in range(NH)]
CUT = 180.0
KEPT = [min(9, 2 + int(CUT / SLOPES[h]) // 128) for h in range(NH)]
for _h in range(NH):
    if KEPT[_h] < 9:
        assert SLOPES[_h] * (128 * KEPT[_h] - 127) >= CUT - 1e-6


def _vqt(h, st):
    lo, hi = max(0, st - 8), min(7, st + KEPT[h] - 9)
    return (lo, hi) if lo <= hi else None


def _st_list(h):
    return [st for st in range(STN) if _vqt(h, st) is not None]



def _tile_kmajor(W, n_kc, n_m):
    W = np.asarray(W, dtype=np.float32)
    Wr = W.reshape(n_m, 128, n_kc, 128)
    return np.ascontiguousarray(Wr.transpose(0, 3, 2, 1)).reshape(n_m, 128, n_kc * 128)


def _make_tables(Q0):
    sbias = np.empty((128, NH * STN), np.float32)
    S0 = Q0 - 1024
    for h in range(NH):
        for st in range(STN):
            s_glob = S0 + st * 128 + np.arange(128)
            if s_glob[0] < 0 or s_glob[0] >= 2048:
                sbias[:, h * STN + st] = NEGBIG
            else:
                sbias[:, h * STN + st] = (SLOPES[h] * s_glob).astype(np.float32)
    phiq8 = np.empty((32, QT * 128), np.float32)
    q_glob = Q0 + np.arange(QT * 128)
    for h in range(NH):
        phiq8[h, :] = (8.0 * SLOPES[h] * q_glob).astype(np.float32)
    return sbias, phiq8


def _stage_inputs(inputs):
    x = np.asarray(inputs["L0_attn_x"], np.float32)
    wo = np.asarray(inputs["L0_attn_wo"], np.float32)
    perm = np.empty(2048, np.int64)
    for h in range(NH):
        fp = (h // 2) * 128 + (h % 2) * 64
        perm[fp:fp + 64] = h * 64 + np.arange(64)

    def _fm(v, n):
        return np.ascontiguousarray(np.asarray(v, np.float32).reshape(n, 128).T)
    shared = dict(
        wq=_tile_kmajor(inputs["L0_attn_wq"], DC, DC),
        wk=_tile_kmajor(inputs["L0_attn_wk"], DC, KMC),
        wv=_tile_kmajor(inputs["L0_attn_wv"], DC, KMC),
        wo=_tile_kmajor(wo[:, perm], DC, DC),
        wg=_tile_kmajor(inputs["L0_ffn_gate_w"], DC, ICN),
        wu=_tile_kmajor(inputs["L0_ffn_up_w"], DC, ICN),
        wd=_tile_kmajor(inputs["L0_ffn_down_w"], ICN, DC),
        qnw=_fm(inputs["L0_attn_q_norm_w"], DC),
        knw=_fm(inputs["L0_attn_k_norm_w"], KMC),
        gs=_fm(inputs["L0_ffn_gate_s"], ICN),
        us=_fm(inputs["L0_ffn_up_s"], ICN),
        ds=_fm(inputs["L0_ffn_down_s"], DC))
    maps = []
    for c in range(8):
        b, hh = c // 2, c % 2
        xt = np.ascontiguousarray(x[b].T)
        if hh == 0:
            xcore = np.concatenate([np.zeros((DIM, 1024), np.float32), xt[:, 0:1024]], 1)
        else:
            xcore = xt
        sbias, phiq8 = _make_tables(hh * 1024)
        m = dict(shared)
        m["xt"] = np.ascontiguousarray(xcore)
        m["sbias"] = sbias
        m["phiq8"] = phiq8
        maps.append(m)
    return maps


def _gather_output(results):
    out = np.empty((4, 2048, 2048), np.float32)
    for c in range(8):
        b, hh = c // 2, c % 2
        out[b, hh * 1024:(hh + 1) * 1024, :] = results[c]["out"].T
    return out



class SemLog:
    def __init__(self):
        self.ev = {e: [] for e in ("te", "act", "dve", "pool", "sy")}

    def wait(self, eng, ch, val):
        self.ev[eng].append(("w", ch, val))

    def inc(self, eng, ch, val_after):
        self.ev[eng].append(("i", ch, val_after))

    def audit(self):
        pos = {e: 0 for e in self.ev}
        cur = {}
        progress = True
        while progress:
            progress = False
            for e, lst in self.ev.items():
                while pos[e] < len(lst):
                    kind, ch, val = lst[pos[e]]
                    if kind == "i":
                        cur[ch] = max(cur.get(ch, 0), val)
                    elif cur.get(ch, 0) < val:
                        break
                    pos[e] += 1
                    progress = True
        stuck = {e: (pos[e], len(lst)) for e, lst in self.ev.items() if pos[e] < len(lst)}
        if stuck:
            det = {e: self.ev[e][pos[e]] for e in stuck}
            raise RuntimeError(f"SEM AUDIT DEADLOCK: {stuck} next={det}")


_LAST_LOG = None


def build_nc(taps=()):
    nc = bass.Bass("TRN2", target_bir_lowering=False)
    taps = set(taps)

    xt = nc.dram_tensor("xt", [DIM, TOK2], F32, kind="ExternalInput").ap()
    wq_d = nc.dram_tensor("wq", [DC, 128, DC * 128], F32, kind="ExternalInput").ap()
    wk_d = nc.dram_tensor("wk", [KMC, 128, DC * 128], F32, kind="ExternalInput").ap()
    wv_d = nc.dram_tensor("wv", [KMC, 128, DC * 128], F32, kind="ExternalInput").ap()
    wo_d = nc.dram_tensor("wo", [DC, 128, DC * 128], F32, kind="ExternalInput").ap()
    wg_d = nc.dram_tensor("wg", [ICN, 128, DC * 128], F32, kind="ExternalInput").ap()
    wu_d = nc.dram_tensor("wu", [ICN, 128, DC * 128], F32, kind="ExternalInput").ap()
    wd_d = nc.dram_tensor("wd", [DC, 128, ICN * 128], F32, kind="ExternalInput").ap()
    qnw_d = nc.dram_tensor("qnw", [128, DC], F32, kind="ExternalInput").ap()
    knw_d = nc.dram_tensor("knw", [128, KMC], F32, kind="ExternalInput").ap()
    gs_d = nc.dram_tensor("gs", [128, ICN], F32, kind="ExternalInput").ap()
    us_d = nc.dram_tensor("us", [128, ICN], F32, kind="ExternalInput").ap()
    ds_d = nc.dram_tensor("ds", [128, DC], F32, kind="ExternalInput").ap()
    sbias_d = nc.dram_tensor("sbias", [128, NH * STN], F32, kind="ExternalInput").ap()
    phiq8_d = nc.dram_tensor("phiq8", [32, QT * 128], F32, kind="ExternalInput").ap()
    out_d = nc.dram_tensor("out", [DIM, TOK], F32, kind="ExternalOutput").ap()

    x2s = nc.dram_tensor("x2s", [DIM, TOK], F32, kind="Internal").ap()
    d4s = nc.dram_tensor("d4s", [DIM, TOK], BF16, kind="Internal").ap()
    phis = nc.dram_tensor("phis", [QT, 32, 2, 128], BF16, kind="Internal").ap()

    identF_d = nc.inline_tensor(np.eye(128, dtype=np.float32), "identF").ap()
    onesB_d = nc.inline_tensor(np.ones((128, 128), ml_dtypes.bfloat16), "onesB").ap()
    mdiagT_d = nc.inline_tensor(np.triu(np.ones((128, 128), np.float32), 1).astype(ml_dtypes.bfloat16), "mdiagT").ap()
    medgeT_d = nc.inline_tensor(np.tril(np.ones((128, 128), np.float32), -1).astype(ml_dtypes.bfloat16), "medgeT").ap()
    negI_d = nc.inline_tensor((NEGBIG * np.eye(128, dtype=np.float32)).astype(ml_dtypes.bfloat16), "negI").ap()

    tap_d = {}
    for name, shape, dt_ in (
            ("xn", [128, DC * TOK2], BF16), ("khat", [66, NKV * TOK2], BF16),
            ("qhat", [66, NH * TOK], BF16), ("vhat", [128, STN * NKV * 65], BF16),
            ("attnout", [128, DC * TOK], BF16), ("a2", [128, DC * TOK], BF16),
            ("x2", [DIM, TOK], F32), ("xn3", [128, DC * TOK], BF16),
            ("d4", [DIM, TOK], BF16), ("hb1", [128, DC * TOK], BF16),
            ("hb2", [128, (ICN - DC) * TOK], BF16)):
        if name in taps:
            tap_d[name] = nc.dram_tensor("tap_" + name, shape, dt_, kind="ExternalOutput").ap()

    global _LAST_LOG
    L = SemLog()
    _LAST_LOG = L
    C = {"w": 0, "s": 0, "te": 0, "act": 0, "dve": 0, "g": 0}

    with (
        nc.semaphore("sW") as sW,
        nc.semaphore("sS") as sS,
        nc.semaphore("sT") as sT,
        nc.semaphore("sA") as sA,
        nc.semaphore("sV") as sV,
        nc.semaphore("sG") as sG,
        nc.Block() as block,
    ):
        SEM = {"w": sW, "s": sS, "te": sT, "act": sA, "dve": sV, "g": sG}

        def inc(engname, instr, ch, n=1):
            C[ch] += n
            instr.then_inc(SEM[ch], n)
            L.inc(engname, ch, C[ch])
            return C[ch]

        def wt(engname, eng, ch, val):
            if val is None or val <= 0:
                return
            L.wait(engname, ch, val)
            eng.wait_ge(SEM[ch], val)

        sb = nc.alloc_sbuf_tensor
        sbias_t = sb("sbias_t", [128, NH * STN], F32)
        phiq8_t = sb("phiq8_t", [32, QT * 128], F32)
        identF_t = sb("identF_t", [128, 128], F32)
        onesB_t = sb("onesB_t", [128, 128], BF16)
        mdiagT_t = sb("mdiagT_t", [128, 128], BF16)
        medgeT_t = sb("medgeT_t", [128, 128], BF16)
        negI_t = sb("negI_t", [128, 128], BF16)
        qnw_t = sb("qnw_t", [128, DC], F32)
        knw_t = sb("knw_t", [128, KMC], F32)
        gs_t = sb("gs_t", [128, ICN], F32)
        us_t = sb("us_t", [128, ICN], F32)
        ds_t = sb("ds_t", [128, DC], F32)
        wbufA = sb("wbufA", [128, DC * 128], BF16)
        wbufB = sb("wbufB", [128, DC * 128], BF16)
        stF = sb("stF", [128, TOK2], F32)
        stM = sb("stM", [128, TOK2], BF16)
        stR = sb("stR", [128, TOK2], BF16)
        sqscr = sb("sqscr", [128, TOK2], BF16)
        m0buf = sb("m0buf", [128, 32], F32)
        phisc = sb("phisc", [32, 2 * 128], BF16)
        phiA = sb("phiA", [32, 128], F32)
        pt_t = sb("pt_t", [128, 6 * QT * 128], BF16)
        zrow = sb("zrow", [1, 512], BF16)
        gptmp = sb("gptmp", [128, TOK], BF16)

        buf_last = {"A": 0, "B": 0}

        S0 = nc.alloc_psum_tensor("S0", [128, 1024], F32)
        S1 = nc.alloc_psum_tensor("S1", [128, 1024], F32)
        A0 = nc.alloc_psum_tensor("A0", [128, 1024], F32)
        A1 = nc.alloc_psum_tensor("A1", [128, 1024], F32)
        SP, APm = [S0, S1], [A0, A1]

        def p0(sy):
            for dst, src in ((sbias_t, sbias_d), (phiq8_t, phiq8_d), (identF_t, identF_d),
                             (onesB_t, onesB_d), (mdiagT_t, mdiagT_d), (medgeT_t, medgeT_d),
                             (negI_t, negI_d), (qnw_t, qnw_d), (knw_t, knw_d),
                             (gs_t, gs_d), (us_t, us_d), (ds_t, ds_d)):
                inc("sy", sy.dma_start(out=dst.ap(), in_=src), "s", 16)
        block.sync(p0)
        c_consts = C["s"]
        block.tensor(lambda t: wt("te", t, "s", c_consts))
        block.scalar(lambda s_: wt("act", s_, "s", c_consts))
        def _dve_init(v):
            wt("dve", v, "s", c_consts)
            inc("dve", v.memset(zrow.ap(), 0.0), "dve", 1)
        block.vector(_dve_init)
        c_zrow = C["dve"]

        def ln_stats(nchunks, ntok, src_fn, eps, rms_denom=None,
                     pre_wait_dve=None, chunk_wait=None, mm_pre_dve=None,
                     sq2=None):
            nh = ntok // 1024
            sq_pred = [C["act"] + i + 1 for i in range(nchunks)]
            te0 = C["te"]
            rms = rms_denom is not None
            denom = float(rms_denom if rms else DIM)
            if ntok == 1024 and sq2 is None:
                sq_bufs = [sqscr.ap()[:, 0:1024], sqscr.ap()[:, 1024:2048]]
            elif sq2 is not None:
                sq_bufs = [sqscr.ap()[:, 0:ntok], sq2]
            else:
                sq_bufs = [sqscr.ap()[:, 0:ntok]]
            nb = len(sq_bufs)

            def _sq(sc):
                if pre_wait_dve is not None:
                    wt("act", sc, "dve", pre_wait_dve)
                for mc in range(nchunks):
                    if chunk_wait is not None:
                        ch, fn = chunk_wait
                        wt("act", sc, ch, fn(mc))
                    if mc >= nb:
                        wt("act", sc, "te", te0 + mc - nb + 1)
                    ins = sc.activation(sq_bufs[mc % nb], src_fn(mc), AF.Square)
                    inc("act", ins, "act", 1)
            block.scalar(_sq)

            def _mm(t):
                if mm_pre_dve is not None:
                    wt("te", t, "dve", mm_pre_dve)
                for mc in range(nchunks):
                    wt("te", t, "act", sq_pred[mc])
                    ins = None
                    sqb = sq_bufs[mc % nb]
                    for half in range(nh):
                        for ts in range(2):
                            sl = half * 1024 + ts * 512
                            psl = slice(ts * 512, ts * 512 + 512)
                            if not rms:
                                t.matmul(APm[half].ap()[:, psl], onesB_t.ap(),
                                         src_fn(mc)[:, sl:sl + 512],
                                         start=(mc == 0), stop=(mc == nchunks - 1))
                                ins = t.matmul(SP[half].ap()[:, psl], onesB_t.ap(),
                                               sqb[:, sl:sl + 512],
                                               start=(mc == 0), stop=(mc == nchunks - 1))
                            else:
                                ins = t.matmul(APm[half].ap()[:, psl], onesB_t.ap(),
                                               sqb[:, sl:sl + 512],
                                               start=(mc == 0), stop=(mc == nchunks - 1))
                    inc("te", ins, "te", 1)
            block.tensor(_mm)
            c_stat = C["te"]

            def _mr(v):
                wt("dve", v, "te", c_stat)
                for half in range(nh):
                    hs = slice(half * 1024, half * 1024 + 1024)
                    if not rms:
                        v.tensor_scalar(stM.ap()[:, hs], APm[half].ap(), 1.0 / denom, None, ALU.mult)
                        v.tensor_scalar(stF.ap()[:, hs], SP[half].ap(), 1.0 / denom, None, ALU.mult)
                    else:
                        v.tensor_scalar(stF.ap()[:, hs], APm[half].ap(), 1.0 / denom, None, ALU.mult)
                if not rms:
                    v.tensor_tensor(sqscr.ap()[:, 0:ntok], stM.ap()[:, 0:ntok], stM.ap()[:, 0:ntok], ALU.mult)
                    v.tensor_tensor(stF.ap()[:, 0:ntok], stF.ap()[:, 0:ntok], sqscr.ap()[:, 0:ntok], ALU.subtract)
                v.tensor_scalar(stF.ap()[:, 0:ntok], stF.ap()[:, 0:ntok], float(eps), None, ALU.add)
                ins = v.reciprocal(stF.ap()[:, 0:ntok], stF.ap()[:, 0:ntok])
                inc("dve", ins, "dve", 1)
            block.vector(_mr)
            c_r1 = C["dve"]

            def _sqrt(sc):
                wt("act", sc, "dve", c_r1)
                ins = sc.activation(stR.ap()[:, 0:ntok], stF.ap()[:, 0:ntok], AF.Sqrt)
                inc("act", ins, "act", 1)
            block.scalar(_sqrt)
            return C["act"]

        def proj_sweep(n_mc, src_w, ntok, mov_fn, evac_fn,
                       pre_te_dve=None, pre_te_act=None, pre_te_g=None,
                       kc_wait=None, te_hook=None):
            nh = ntok // 1024
            w0 = C["w"]
            te_done = []
            evac_pred = [C["dve"] + i + 1 for i in range(n_mc * nh)]

            def _te(t):
                if pre_te_dve is not None:
                    wt("te", t, "dve", pre_te_dve)
                if pre_te_act is not None:
                    wt("te", t, "act", pre_te_act)
                if pre_te_g is not None:
                    wt("te", t, "g", pre_te_g)
                n = 0
                for mc in range(n_mc):
                    wt("te", t, "w", w0 + (mc + 1) * 16)
                    buf = wbufA if mc % 2 == 0 else wbufB
                    for half in range(nh):
                        if n >= 2:
                            wt("te", t, "dve", evac_pred[n - 2])
                        ps = SP[n % 2]
                        ins = None
                        for kc in range(DC):
                            if n == 0 and kc_wait is not None:
                                wt("te", t, "dve", kc_wait(kc))
                            for ts in range(2):
                                off = half * 1024 + ts * 512
                                ins = t.matmul(ps.ap()[:, ts * 512:ts * 512 + 512],
                                               buf.ap()[:, kc * 128:(kc + 1) * 128],
                                               mov_fn(kc, off),
                                               start=(kc == 0), stop=(kc == DC - 1))
                        inc("te", ins, "te", 1)
                        te_done.append(C["te"])
                        if te_hook is not None:
                            te_hook(t, mc, half)
                        n += 1
            block.tensor(_te)

            def _dve(v):
                n = 0
                for mc in range(n_mc):
                    for half in range(nh):
                        wt("dve", v, "te", te_done[n])
                        ins = evac_fn(v, mc, half, SP[n % 2].ap())
                        inc("dve", ins, "dve", 1)
                        n += 1
            block.vector(_dve)
            assert C["dve"] == evac_pred[-1]

            def _dma(g):
                for mc in range(n_mc):
                    bname = "A" if mc % 2 == 0 else "B"
                    wv_ = te_done[(mc - 2) * nh + nh - 1] if mc >= 2 else buf_last[bname]
                    wt("pool", g, "te", wv_)
                    buf = wbufA if mc % 2 == 0 else wbufB
                    inc("pool", g.dma_start(out=buf.ap(), in_=src_w[mc]), "w", 16)
            block.gpsimd(_dma)
            for mc in range(max(0, n_mc - 2), n_mc):
                buf_last["A" if mc % 2 == 0 else "B"] = te_done[mc * nh + nh - 1]
            return te_done, evac_pred

        khat_g = nc.sbuf_tensor("khat", [66, NKV * TOK2], BF16)
        khat = khat_g.__enter__()
        vhat_g = nc.sbuf_tensor("vhat", [128, STN * NKV * 65], BF16)
        vhat = vhat_g.__enter__()
        xn_g = nc.sbuf_tensor("xn", [128, DC * TOK2], BF16)
        xn = xn_g.__enter__()
        ktmp_g = nc.sbuf_tensor("ktmp", [128, KMC * TOK2], BF16, side="right")
        ktmp = ktmp_g.__enter__()

        _scope_n = 'p1_ln1'; _scope_i = nc.enter_named_scope('p1_ln1', False)[0]
        c_xld0 = C["w"]

        def p1_dma(g):
            for kc in range(DC):
                ins = g.dma_start(out=xn.ap()[:, kc * TOK2:(kc + 1) * TOK2],
                                  in_=xt[kc * 128:(kc + 1) * 128, :])
                inc("pool", ins, "w", 16)
        block.gpsimd(p1_dma)

        c_ln1 = ln_stats(DC, TOK2, lambda mc: xn.ap()[:, mc * TOK2:(mc + 1) * TOK2],
                         LN_EPS, chunk_wait=("w", lambda mc: c_xld0 + (mc + 1) * 16),
                         sq2=(ktmp.ap()[:, 0:TOK2] if LN1_SQ2 else None))

        def p1_norm(v):
            wt("dve", v, "act", c_ln1)
            for kc in range(DC):
                sl = slice(kc * TOK2, (kc + 1) * TOK2)
                v.tensor_tensor(xn.ap()[:, sl], xn.ap()[:, sl], stM.ap(), ALU.subtract)
                ins = v.tensor_tensor(xn.ap()[:, sl], xn.ap()[:, sl], stR.ap(), ALU.mult)
                inc("dve", ins, "dve", 1)
        block.vector(p1_norm)
        c_xn = C["dve"]

        if "xn" in taps:
            def tp_xn(sy):
                wt("sy", sy, "dve", c_xn)
                inc("sy", sy.dma_start(out=tap_d["xn"], in_=xn.ap()), "s", 16)
            block.sync(tp_xn)

        nc.leave_named_scope(_scope_n, _scope_i, False)
        _scope_n = 'p2_k'; _scope_i = nc.enter_named_scope('p2_k', False)[0]
        norm_pred = [c_xn - DC + kc + 1 for kc in range(DC)]
        k_te, k_ev = proj_sweep(
            KMC, wk_d, TOK2,
            lambda kc, off: xn.ap()[:, kc * TOK2 + off: kc * TOK2 + off + 512],
            lambda v, mc, half, ps: v.tensor_copy(
                ktmp.ap()[:, mc * TOK2 + half * 1024: mc * TOK2 + half * 1024 + 1024], ps),
            kc_wait=lambda kc: norm_pred[kc])

        c_krms = ln_stats(KMC, TOK2, lambda mc: ktmp.ap()[:, mc * TOK2:(mc + 1) * TOK2],
                          RMS_EPS, rms_denom=512.0,
                          chunk_wait=("dve", lambda mc: k_ev[2 * mc + 1]))

        def p2_khat(v):
            wt("dve", v, "act", c_krms)
            v.memset(khat.ap()[64:66, :], 1.0)
            for mc in range(KMC):
                for par in range(2):
                    g_head = 2 * mc + par
                    ins = v.scalar_tensor_tensor(
                        khat.ap()[0:64, g_head * TOK2:(g_head + 1) * TOK2],
                        ktmp.ap()[par * 64:(par + 1) * 64, mc * TOK2:(mc + 1) * TOK2],
                        knw_t.ap()[par * 64:(par + 1) * 64, mc:mc + 1],
                        stR.ap()[par * 64:(par + 1) * 64, 0:TOK2],
                        ALU.mult, ALU.mult)
                    inc("dve", ins, "dve", 1)
        block.vector(p2_khat)
        c_khat = C["dve"]

        if "khat" in taps:
            def tp_kh(sy):
                wt("sy", sy, "dve", c_khat)
                inc("sy", sy.dma_start(out=tap_d["khat"], in_=khat.ap()), "s", 16)
            block.sync(tp_kh)

        nc.leave_named_scope(_scope_n, _scope_i, False)
        _scope_n = 'p2_v'; _scope_i = nc.enter_named_scope('p2_v', False)[0]
        vW0 = C["w"]
        v_te = []
        v_stf, v_evh = {}, {}
        cd = C["dve"]
        for mc in range(KMC):
            for half in range(2):
                cd += 1
                v_stf[(mc, half)] = cd
            for grp in range(4):
                cd += 1
                v_evh[(mc, grp)] = cd

        def p2_v_te(t):
            wt("te", t, "dve", c_khat)
            n = 0
            for mc in range(KMC):
                wt("te", t, "w", vW0 + (mc + 1) * 16)
                buf = wbufA if mc % 2 == 0 else wbufB
                for half in range(2):
                    if n >= 2:
                        wt("te", t, "dve", v_stf[(mc - 1, half)])
                    ps = SP[n % 2]
                    ins = None
                    for kc in range(DC):
                        for ts in range(2):
                            off = half * 1024 + ts * 512
                            ins = t.matmul(ps.ap()[:, ts * 512:ts * 512 + 512],
                                           buf.ap()[:, kc * 128:(kc + 1) * 128],
                                           xn.ap()[:, kc * TOK2 + off: kc * TOK2 + off + 512],
                                           start=(kc == 0), stop=(kc == DC - 1))
                    inc("te", ins, "te", 1)
                    v_te.append(C["te"])
                    n += 1
                wt("te", t, "dve", v_stf[(mc, 1)])
                for grp in range(4):
                    if mc > 0 or grp >= 2:
                        key = (mc, grp - 2) if grp >= 2 else (mc - 1, grp + 2)
                        wt("te", t, "dve", v_evh[key])
                    ins = None
                    for j in range(4):
                        sc16 = grp * 4 + j
                        cps = ((grp % 2) * 4 + j) * 128
                        ins = t.transpose(A0.ap()[:, cps:cps + 128],
                                          stF.ap()[:, sc16 * 128:(sc16 + 1) * 128],
                                          identF_t.ap())
                    inc("te", ins, "te", 1)
        block.tensor(p2_v_te)
        v_tr = {}
        ctr = C["te"]
        tt = v_te[0] - 1
        tt = None
        cnt = v_te[0] - 1
        idx = 0
        for mc in range(KMC):
            cnt = v_te[2 * mc + 1]
            for grp in range(4):
                cnt += 1
                v_tr[(mc, grp)] = cnt
        assert cnt == C["te"]

        def p2_v_dve(v):
            n = 0
            for mc in range(KMC):
                for half in range(2):
                    wt("dve", v, "te", v_te[n])
                    ins = v.tensor_copy(stF.ap()[:, half * 1024:half * 1024 + 1024],
                                        SP[n % 2].ap())
                    inc("dve", ins, "dve", 1)
                    assert C["dve"] == v_stf[(mc, half)]
                    n += 1
                for grp in range(4):
                    wt("dve", v, "te", v_tr[(mc, grp)])
                    ins = None
                    for j in range(4):
                        sc16 = grp * 4 + j
                        cps = ((grp % 2) * 4 + j) * 128
                        for par in range(2):
                            g_head = 2 * mc + par
                            col = (sc16 * NKV + g_head) * 65
                            ins = v.tensor_copy(vhat.ap()[:, col:col + 64],
                                                A0.ap()[:, cps + par * 64: cps + par * 64 + 64])
                    inc("dve", ins, "dve", 1)
                    assert C["dve"] == v_evh[(mc, grp)]
        block.vector(p2_v_dve)

        def p2_v_dma(g):
            for mc in range(KMC):
                bname = "A" if mc % 2 == 0 else "B"
                wv_ = v_te[(mc - 2) * 2 + 1] if mc >= 2 else buf_last[bname]
                wt("pool", g, "te", wv_)
                buf = wbufA if mc % 2 == 0 else wbufB
                inc("pool", g.dma_start(out=buf.ap(), in_=wv_d[mc]), "w", 16)
        block.gpsimd(p2_v_dma)
        buf_last["A"] = v_te[5]
        buf_last["B"] = v_te[7]

        def p2_v_ones(v):
            ins = v.memset(vhat.ap().rearrange("p (b c) -> p b c", c=65)[:, :, 64:65], 1.0)
            inc("dve", ins, "dve", 1)
        block.vector(p2_v_ones)
        c_vhat = C["dve"]

        if "vhat" in taps:
            def tp_vh(sy):
                wt("sy", sy, "dve", c_vhat)
                inc("sy", sy.dma_start(out=tap_d["vhat"], in_=vhat.ap()), "s", 16)
            block.sync(tp_vh)

        ktmp_g.__exit__(None, None, None)

        nc.leave_named_scope(_scope_n, _scope_i, False)
        _scope_n = 'p2_q'; _scope_i = nc.enter_named_scope('p2_q', False)[0]
        qtmp_g = nc.sbuf_tensor("qtmp", [128, DC * TOK], BF16, side="right")
        qtmp = qtmp_g.__enter__()

        q_te, q_ev = proj_sweep(
            DC, wq_d, TOK,
            lambda kc, off: xn.ap()[:, kc * TOK2 + 1024 + off: kc * TOK2 + 1024 + off + 512],
            lambda v, mc, half, ps: v.tensor_copy(qtmp.ap()[:, mc * TOK:(mc + 1) * TOK], ps),
            pre_te_dve=c_vhat)

        c_qrms = ln_stats(DC, TOK, lambda mc: qtmp.ap()[:, mc * TOK:(mc + 1) * TOK],
                          RMS_EPS, rms_denom=float(DIM),
                          chunk_wait=("dve", lambda mc: q_ev[mc]))

        xn_g.__exit__(None, None, None)
        qhat_g = nc.sbuf_tensor("qhat", [66, NH * TOK], BF16)
        qhat = qhat_g.__enter__()

        def p2_qhat(v):
            wt("dve", v, "act", c_qrms)
            for mc in range(DC):
                for par in range(2):
                    head = 2 * mc + par
                    ins = v.scalar_tensor_tensor(
                        qhat.ap()[0:64, head * TOK:(head + 1) * TOK],
                        qtmp.ap()[par * 64:(par + 1) * 64, mc * TOK:(mc + 1) * TOK],
                        qnw_t.ap()[par * 64:(par + 1) * 64, mc:mc + 1],
                        stR.ap()[par * 64:(par + 1) * 64, 0:TOK],
                        ALU.mult, ALU.mult)
                    inc("dve", ins, "dve", 1)
        block.vector(p2_qhat)
        c_qhat_rows = C["dve"]

        nc.leave_named_scope(_scope_n, _scope_i, False)
        _scope_n = 'p3a_diag'; _scope_i = nc.enter_named_scope('p3a_diag', False)[0]
        diag_te = {}
        diag_max = {}
        ph_stt = {}
        cd = C["dve"]
        for qt in range(QT):
            for grp in range(4):
                cd += 1
                diag_max[(qt, grp)] = cd
            cd += 1
            ph_stt[qt] = cd
        te_p3a0 = C["te"]
        ph_tr_pred = {qt: te_p3a0 + 5 * qt + 5 for qt in range(QT)}

        def p3a_te(t):
            wt("te", t, "dve", c_qhat_rows)
            for qt in range(QT):
                st = qt + 8
                for grp in range(4):
                    if qt > 0 or grp >= 2:
                        key = (qt, grp - 2) if grp >= 2 else (qt - 1, grp + 2)
                        wt("te", t, "dve", diag_max[key])
                    ins = None
                    for j in range(8):
                        h = grp * 8 + j
                        g_head = h // 4
                        ins = t.matmul(SP[grp % 2].ap()[:, j * 128:(j + 1) * 128],
                                       qhat.ap()[0:64, h * TOK + qt * 128: h * TOK + qt * 128 + 128],
                                       khat.ap()[0:64, g_head * TOK2 + st * 128: g_head * TOK2 + st * 128 + 128],
                                       start=True, stop=True)
                    inc("te", ins, "te", 1)
                    diag_te[(qt, grp)] = C["te"]
                wt("te", t, "dve", diag_max[(qt, 3)])
                if qt >= 2:
                    wt("te", t, "dve", ph_stt[qt - 2])
                ins = t.transpose(A1.ap()[0:32, (qt % 2) * 512:(qt % 2) * 512 + 128],
                                  m0buf.ap(), identF_t.ap())
                inc("te", ins, "te", 1)
                assert C["te"] == ph_tr_pred[qt]
        block.tensor(p3a_te)

        s_p3a0 = C["s"]

        def p3a_dve(v):
            for qt in range(QT):
                for grp in range(4):
                    wt("dve", v, "te", diag_te[(qt, grp)])
                    if qt >= 1 and grp == 0:
                        wt("dve", v, "te", ph_tr_pred[qt - 1])
                    ins = None
                    for j in range(8):
                        h = grp * 8 + j
                        ins = v.reduce_max(m0buf.ap()[:, h:h + 1],
                                           SP[grp % 2].ap()[:, j * 128:(j + 1) * 128],
                                           axis=AX.X, negate=True)
                    inc("dve", ins, "dve", 1)
                    assert C["dve"] == diag_max[(qt, grp)]
                wt("dve", v, "te", ph_tr_pred[qt])
                if qt >= 1:
                    wt("dve", v, "s", s_p3a0 + 32 * (qt - 1) + 16)
                ps = A1.ap()[0:32, (qt % 2) * 512:(qt % 2) * 512 + 128]
                v.tensor_tensor(phiA.ap(), ps, phiq8_t.ap()[:, qt * 128:(qt + 1) * 128], ALU.subtract)
                v.tensor_copy(phisc.ap()[:, 0:128], phiA.ap())
                ins = v.tensor_tensor(phisc.ap()[:, 128:256], phiA.ap(), phisc.ap()[:, 0:128], ALU.subtract)
                inc("dve", ins, "dve", 1)
                assert C["dve"] == ph_stt[qt]
        block.vector(p3a_dve)

        def p3a_dma(sy):
            for qt in range(QT):
                wt("sy", sy, "dve", ph_stt[qt])
                ins = sy.dma_start(out=phis[qt],
                                   in_=phisc.ap().rearrange("h (e q) -> h e q", e=2))
                inc("sy", ins, "s", 16)
                wt("sy", sy, "s", C["s"])
                ins = sy.dma_start(
                    out=qhat.ap()[64:66, :].rearrange("e (h q) -> e h q", h=32)[:, :, qt * 128:(qt + 1) * 128],
                    in_=phis[qt].rearrange("h e q -> e h q"))
                inc("sy", ins, "s", 16)
        block.sync(p3a_dma)
        c_phi_all = C["s"]

        if "qhat" in taps:
            def tp_qh(sy):
                wt("sy", sy, "s", c_phi_all)
                inc("sy", sy.dma_start(out=tap_d["qhat"], in_=qhat.ap()), "s", 16)
            block.sync(tp_qh)

        nc.leave_named_scope(_scope_n, _scope_i, False)
        _scope_n = 'p3b_attn'; _scope_i = nc.enter_named_scope('p3b_attn', False)[0]
        def pt_ap(h, st, lo, hi):
            base = ((h % 2) * 3 + (st % PT_LAG)) * QT * 128
            return pt_t.ap()[:, base + lo * 128: base + (hi + 1) * 128]

        sched = []
        exp_order = []
        for hp in range(0, NH, 2):
            hs = [hp, hp + 1]
            stl = {h: _st_list(h) for h in hs}
            union = sorted(set(stl[hp]) | set(stl[hp + 1]))
            pend = {h: [] for h in hs}
            zf_done = {h: False for h in hs}
            ndiv = 0
            for st in union:
                for h in hs:
                    if st in stl[h]:
                        sched.append(("kq", h, st))
                        exp_order.append((h, st))
                        pend[h].append(st)
                if hp >= 2 and ndiv < 2:
                    sched.append(("div", hp - 2 + ndiv))
                    ndiv += 1
                for h in hs:
                    if len(pend[h]) >= PT_LAG:
                        if not zf_done[h]:
                            sched.append(("zf", h))
                            zf_done[h] = True
                        sched.append(("av", h, pend[h].pop(0)))
            for h in hs:
                while pend[h]:
                    if not zf_done[h]:
                        sched.append(("zf", h))
                        zf_done[h] = True
                    sched.append(("av", h, pend[h].pop(0)))
        sched.append(("div", NH - 2))
        sched.append(("div", NH - 1))

        exp_pred = {}
        ca = C["act"]
        for key in exp_order:
            ca += 1
            exp_pred[key] = ca
        prev_s_tenant = {}
        lastp = {0: None, 1: None}
        for (h, st) in exp_order:
            p = h % 2
            if lastp[p] is not None:
                prev_s_tenant[(h, st)] = lastp[p]
            lastp[p] = (h, st)
        prev_pt_tenant = {}
        hist = {}
        for (h, st) in exp_order:
            hist.setdefault(h, []).append(st)
        for h, sts_h in hist.items():
            for i, st in enumerate(sts_h):
                if i >= PT_LAG:
                    prev_pt_tenant[(h, st)] = (h, sts_h[i - PT_LAG])
        den_pred, div_pred = {}, {}
        cd = C["dve"]
        for h in range(NH):
            cd += 1
            den_pred[h] = cd
            cd += 1
            div_pred[h] = cd
        kq_done, av_done, div_te = {}, {}, {}
        c_diag_last = diag_max[(QT - 1, 3)]
        c_ph_last = ph_stt[QT - 1]

        def p3b_te(t):
            wt("te", t, "s", c_phi_all)
            wt("te", t, "dve", c_zrow)
            for op in sched:
                kind = op[0]
                if kind == "kq":
                    _, h, st = op
                    g_head = h // 4
                    lo, hi = _vqt(h, st)
                    prev = prev_s_tenant.get((h, st))
                    if prev is not None:
                        wt("te", t, "act", exp_pred[prev])
                    else:
                        wt("te", t, "dve", c_diag_last)
                    diag_q = st - 8
                    edge_q = st if (KEPT[h] == 9 and st <= 7) else None
                    ins = None
                    for bank in range(2):
                        qlist = [qt for qt in range(lo, hi + 1) if qt // 4 == bank]
                        masked = [qt for qt in qlist
                                  if qt == diag_q or (edge_q is not None and qt == edge_q)]
                        normal = [qt for qt in qlist if qt not in masked]
                        for qt in normal:
                            ins = t.matmul(SP[h % 2].ap()[:, qt * 128:(qt + 1) * 128],
                                           khat.ap()[0:66, g_head * TOK2 + st * 128: g_head * TOK2 + st * 128 + 128],
                                           qhat.ap()[0:66, h * TOK + qt * 128: h * TOK + qt * 128 + 128],
                                           start=True, stop=True)
                        for qt in masked:
                            mt = mdiagT_t if qt == diag_q else medgeT_t
                            t.matmul(SP[h % 2].ap()[:, qt * 128:(qt + 1) * 128],
                                     mt.ap(), negI_t.ap(), start=True, stop=False)
                            ins = t.matmul(SP[h % 2].ap()[:, qt * 128:(qt + 1) * 128],
                                           khat.ap()[0:66, g_head * TOK2 + st * 128: g_head * TOK2 + st * 128 + 128],
                                           qhat.ap()[0:66, h * TOK + qt * 128: h * TOK + qt * 128 + 128],
                                           start=False, stop=True)
                    inc("te", ins, "te", 1)
                    kq_done[(h, st)] = C["te"]
                elif kind == "av":
                    _, h, st = op
                    g_head = h // 4
                    lo, hi = _vqt(h, st)
                    wt("te", t, "act", exp_pred[(h, st)])
                    ins = None
                    for qt in range(lo, hi + 1):
                        last = (st == qt + 8)
                        vcol = (st * NKV + g_head) * 65
                        ins = t.matmul(APm[h % 2].ap()[0:65, qt * 128:(qt + 1) * 128],
                                       vhat.ap()[:, vcol:vcol + 65],
                                       pt_ap(h, st, qt, qt),
                                       start=False, stop=last)
                    inc("te", ins, "te", 1)
                    av_done[(h, st)] = C["te"]
                elif kind == "zf":
                    _, h = op
                    wt("te", t, "dve", div_pred[h - 2] if h >= 2 else c_ph_last)
                    for bank in range(2):
                        t.matmul(APm[h % 2].ap()[0:65, bank * 512:bank * 512 + 512],
                                 onesB_t.ap()[0:1, 0:65], zrow.ap(),
                                 start=True, stop=False)
                else:
                    _, hd = op
                    wt("te", t, "dve", den_pred[hd])
                    ins = None
                    for ts in range(2):
                        ins = t.matmul(APm[hd % 2].ap()[64:128, ts * 512:ts * 512 + 512],
                                       onesB_t.ap()[0:1, 0:64],
                                       stM.ap()[0:1, ts * 512:ts * 512 + 512],
                                       start=True, stop=True, tile_position=(0, 64))
                    inc("te", ins, "te", 1)
                    div_te[hd] = C["te"]
        block.tensor(p3b_te)

        def p3b_act(sc):
            for (h, st) in exp_order:
                lo, hi = _vqt(h, st)
                wt("act", sc, "te", kq_done[(h, st)])
                pv = prev_pt_tenant.get((h, st))
                if pv is not None:
                    wt("act", sc, "te", av_done[pv])
                ins = sc.activation(
                    pt_ap(h, st, lo, hi),
                    SP[h % 2].ap()[:, lo * 128:(hi + 1) * 128],
                    AF.Exp, bias=sbias_t.ap()[:, h * STN + st: h * STN + st + 1],
                    scale=0.125)
                inc("act", ins, "act", 1)
                assert C["act"] == exp_pred[(h, st)]
        block.scalar(p3b_act)

        attnout = qtmp

        avE = sqscr

        bg_pred = {}
        cg = C["g"]
        for h in range(NH):
            cg += 1
            bg_pred[h] = cg

        def _bg(v, h):
            ebuf = avE.ap()[64:128, (h % 2) * TOK:(h % 2) * TOK + TOK]
            rb = stF.ap()[64:128, (h % 2) * TOK:(h % 2) * TOK + TOK]
            mc = h // 2
            if h % 2 == 1:
                ins = v.tensor_tensor(attnout.ap()[64:128, mc * TOK:(mc + 1) * TOK],
                                      ebuf, rb, ALU.mult)
            else:
                v.tensor_tensor(gptmp.ap()[64:128, :], ebuf, rb, ALU.mult)
                ins = v.tensor_copy(attnout.ap()[0:64, mc * TOK:(mc + 1) * TOK],
                                    gptmp.ap()[64:128, :])
            inc("dve", ins, "g", 1)
            assert C["g"] == bg_pred[h]

        def p3b_dve(v):
            for h in range(NH):
                sts_h = hist[h]
                wt("dve", v, "te", av_done[(h, sts_h[-1])])
                if h >= 1:
                    wt("dve", v, "te", div_te[h - 1])
                ins = v.tensor_copy(stM.ap()[0:1, 0:TOK], APm[h % 2].ap()[64:65, :])
                inc("dve", ins, "dve", 1)
                assert C["dve"] == den_pred[h]
                wt("dve", v, "te", div_te[h])
                ebuf = avE.ap()[64:128, (h % 2) * TOK:(h % 2) * TOK + TOK]
                v.reciprocal(stF.ap()[64:128, (h % 2) * TOK:(h % 2) * TOK + TOK],
                             APm[h % 2].ap()[64:128, :])
                ins = v.tensor_copy(ebuf, APm[h % 2].ap()[0:64, :])
                inc("dve", ins, "dve", 1)
                assert C["dve"] == div_pred[h]
                if h >= 1:
                    _bg(v, h - 1)
            _bg(v, NH - 1)
        block.vector(p3b_dve)
        c_attnout = C["dve"]
        c_bg_all = C["g"]
        c_exp_last = exp_pred[exp_order[-1]]

        if "attnout" in taps:
            def tp_ao(sy):
                wt("sy", sy, "dve", c_attnout)
                inc("sy", sy.dma_start(out=tap_d["attnout"], in_=attnout.ap()), "s", 16)
            block.sync(tp_ao)

        nc.leave_named_scope(_scope_n, _scope_i, False)
        _scope_n = 'p4_wo'; _scope_i = nc.enter_named_scope('p4_wo', False)[0]
        qhat_g.__exit__(None, None, None)
        vhat_g.__exit__(None, None, None)
        khat_g.__exit__(None, None, None)
        a2_g = nc.sbuf_tensor("a2", [128, DC * TOK], BF16)
        a2 = a2_g.__enter__()
        ot_g = nc.sbuf_tensor("ot_t", [128, TOK], F32, side="right")
        ot_t = ot_g.__enter__()
        ot2_g = nc.sbuf_tensor("ot_t2", [128, TOK], F32, side="right")
        ot_t2 = ot2_g.__enter__()
        xrs_g = nc.sbuf_tensor("xrs", [128, TOK], F32, side="right")
        xrs = xrs_g.__enter__()
        xrs2_g = nc.sbuf_tensor("xrs2", [128, TOK], F32, side="right")
        xrs2 = xrs2_g.__enter__()
        x2bf_g = nc.sbuf_tensor("x2bf", [128, DC * TOK], BF16, side="right")
        x2bf = x2bf_g.__enter__()

        ln2_sq_pred = [C["act"] + mc + 1 for mc in range(DC)]
        ln2_stat_te = {}

        def _ln2_stat(t, m, last):
            wt("te", t, "act", ln2_sq_pred[m])
            ins = None
            for ts in range(2):
                psl = slice(ts * 512, ts * 512 + 512)
                t.matmul(A0.ap()[:, psl], onesB_t.ap(),
                         a2.ap()[:, m * TOK:(m + 1) * TOK][:, psl],
                         start=(m == 0), stop=last)
                sqb = sqscr.ap()[:, (m % 2) * 1024:(m % 2) * 1024 + 1024]
                ins = t.matmul(A1.ap()[:, psl], onesB_t.ap(), sqb[:, psl],
                               start=(m == 0), stop=last)
            inc("te", ins, "te", 1)
            ln2_stat_te[m] = C["te"]

        def wo_hook(t, mc, half):
            if mc >= 1:
                _ln2_stat(t, mc - 1, False)

        wo_te, wo_ev = proj_sweep(
            DC, wo_d, TOK,
            lambda kc, off: attnout.ap()[:, kc * TOK + off: kc * TOK + off + 512],
            lambda v, mc, half, ps: v.tensor_copy(a2.ap()[:, mc * TOK:(mc + 1) * TOK], ps),
            pre_te_dve=c_attnout, pre_te_act=c_exp_last, pre_te_g=c_bg_all,
            te_hook=wo_hook)

        def ln2_sq(sc):
            for mc in range(DC):
                wt("act", sc, "dve", wo_ev[mc])
                if mc >= 2:
                    wt("act", sc, "te", ln2_stat_te[mc - 2])
                ins = sc.activation(sqscr.ap()[:, (mc % 2) * 1024:(mc % 2) * 1024 + 1024],
                                    a2.ap()[:, mc * TOK:(mc + 1) * TOK], AF.Square)
                inc("act", ins, "act", 1)
                assert C["act"] == ln2_sq_pred[mc]
        block.scalar(ln2_sq)

        def ln2_last(t):
            _ln2_stat(t, DC - 1, True)
        block.tensor(ln2_last)
        c_ln2_stat = C["te"]

        def ln2_mr(v):
            wt("dve", v, "te", c_ln2_stat)
            v.tensor_scalar(stM.ap()[:, 0:TOK], A0.ap(), 1.0 / DIM, None, ALU.mult)
            v.tensor_scalar(stF.ap()[:, 0:TOK], A1.ap(), 1.0 / DIM, None, ALU.mult)
            v.tensor_tensor(sqscr.ap()[:, 0:TOK], stM.ap()[:, 0:TOK], stM.ap()[:, 0:TOK], ALU.mult)
            v.tensor_tensor(stF.ap()[:, 0:TOK], stF.ap()[:, 0:TOK], sqscr.ap()[:, 0:TOK], ALU.subtract)
            v.tensor_scalar(stF.ap()[:, 0:TOK], stF.ap()[:, 0:TOK], float(LN_EPS), None, ALU.add)
            ins = v.reciprocal(stF.ap()[:, 0:TOK], stF.ap()[:, 0:TOK])
            inc("dve", ins, "dve", 1)
        block.vector(ln2_mr)
        c_ln2_r1 = C["dve"]

        def ln2_sqrt(sc):
            wt("act", sc, "dve", c_ln2_r1)
            ins = sc.activation(stR.ap()[:, 0:TOK], stF.ap()[:, 0:TOK], AF.Sqrt)
            inc("act", ins, "act", 1)
        block.scalar(ln2_sqrt)
        c_ln2 = C["act"]

        if "a2" in taps:
            def tp_a2(sy):
                wt("sy", sy, "dve", wo_ev[-1])
                inc("sy", sy.dma_start(out=tap_d["a2"], in_=a2.ap()), "s", 16)
            block.sync(tp_a2)

        x2_dve_pred = [C["dve"] + mc + 1 for mc in range(DC)]
        cs = C["s"]
        ld_pred, x2_st_pred = {}, {}
        for mc in range(DC):
            cs += 16
            ld_pred[mc] = cs
            if mc >= 1:
                cs += 16
                x2_st_pred[mc - 1] = cs
        cs += 16
        x2_st_pred[DC - 1] = cs

        def p4b_sync(sy):
            wt("sy", sy, "dve", c_attnout)
            wt("sy", sy, "g", c_bg_all)
            for mc in range(DC):
                if mc >= 2:
                    wt("sy", sy, "dve", x2_dve_pred[mc - 2])
                ins = sy.dma_start(out=(xrs if mc % 2 == 0 else xrs2).ap(),
                                   in_=xt[mc * 128:(mc + 1) * 128, 1024:2048])
                inc("sy", ins, "s", 16)
                assert C["s"] == ld_pred[mc]
                if mc >= 1:
                    wt("sy", sy, "dve", x2_dve_pred[mc - 1])
                    ins = sy.dma_start(out=x2s[(mc - 1) * 128:mc * 128, :],
                                       in_=(ot_t if (mc - 1) % 2 == 0 else ot_t2).ap())
                    inc("sy", ins, "s", 16)
                    assert C["s"] == x2_st_pred[mc - 1]
            wt("sy", sy, "dve", x2_dve_pred[DC - 1])
            ins = sy.dma_start(out=x2s[(DC - 1) * 128:DC * 128, :],
                               in_=(ot_t if (DC - 1) % 2 == 0 else ot_t2).ap())
            inc("sy", ins, "s", 16)
            assert C["s"] == x2_st_pred[DC - 1]
        block.sync(p4b_sync)

        x2bf_cast_pred = [c_ln2 + mc + 1 for mc in range(DC)]

        def p4b_dve(v):
            wt("dve", v, "act", c_ln2)
            for mc in range(DC):
                wt("dve", v, "s", ld_pred[mc])
                if mc >= 2:
                    wt("dve", v, "s", x2_st_pred[mc - 2])
                    wt("dve", v, "act", x2bf_cast_pred[mc - 2])
                buf = (xrs if mc % 2 == 0 else xrs2).ap()
                ob = (ot_t if mc % 2 == 0 else ot_t2).ap()
                a2c = a2.ap()[:, mc * TOK:(mc + 1) * TOK]
                v.tensor_tensor(ob, a2c, stM.ap()[:, 0:TOK], ALU.subtract)
                v.tensor_tensor(ob, ob, stR.ap()[:, 0:TOK], ALU.mult)
                ins = v.tensor_tensor(ob, ob, buf, ALU.add)
                inc("dve", ins, "dve", 1)
                assert C["dve"] == x2_dve_pred[mc]
        block.vector(p4b_dve)
        c_x2 = C["dve"]

        def p4b_cast(sc):
            for mc in range(DC):
                wt("act", sc, "dve", x2_dve_pred[mc])
                ob = (ot_t if mc % 2 == 0 else ot_t2).ap()
                ins = sc.activation(x2bf.ap()[:, mc * TOK:(mc + 1) * TOK], ob, AF.Copy)
                assert inc("act", ins, "act", 1) == x2bf_cast_pred[mc]
        block.scalar(p4b_cast)

        if "x2" in taps:
            def tp_x2(sy):
                wt("sy", sy, "s", x2_st_pred[DC - 1])
                inc("sy", sy.dma_start(out=tap_d["x2"], in_=x2s), "s", 16)
            block.sync(tp_x2)



        nc.leave_named_scope(_scope_n, _scope_i, False)
        _scope_n = 'p5_ffn'; _scope_i = nc.enter_named_scope('p5_ffn', False)[0]
        a2_g.__exit__(None, None, None)
        xn3_g = nc.sbuf_tensor("xn3", [128, DC * TOK], BF16)
        xn3 = xn3_g.__enter__()

        c_ln3 = ln_stats(DC, TOK, lambda mc: x2bf.ap()[:, mc * TOK:(mc + 1) * TOK],
                         LN_EPS, chunk_wait=("act", lambda mc: x2bf_cast_pred[mc]))

        def p5_xn3(v):
            wt("dve", v, "act", c_ln3)
            for mc in range(DC):
                sl = slice(mc * TOK, (mc + 1) * TOK)
                v.tensor_tensor(xn3.ap()[:, sl], x2bf.ap()[:, sl], stM.ap()[:, 0:TOK], ALU.subtract)
                ins = v.tensor_tensor(xn3.ap()[:, sl], xn3.ap()[:, sl], stR.ap()[:, 0:TOK], ALU.mult)
                inc("dve", ins, "dve", 1)
        block.vector(p5_xn3)
        c_xn3 = C["dve"]

        if "xn3" in taps:
            def tp_x3(sy):
                wt("sy", sy, "dve", c_xn3)
                inc("sy", sy.dma_start(out=tap_d["xn3"], in_=xn3.ap()), "s", 16)
            block.sync(tp_x3)

        x2bf_g.__exit__(None, None, None)
        xrs2_g.__exit__(None, None, None)
        xrs_g.__exit__(None, None, None)
        ot2_g.__exit__(None, None, None)
        ot_g.__exit__(None, None, None)

        hbuf2_g = nc.sbuf_tensor("hbuf2", [128, (ICN - DC) * TOK], BF16)
        hbuf2 = hbuf2_g.__enter__()
        wbufC_g = nc.sbuf_tensor("wbufC", [128, DC * 128], BF16)
        wbufC = wbufC_g.__enter__()
        wbufD_g = nc.sbuf_tensor("wbufD", [128, DC * 128], BF16)
        wbufD = wbufD_g.__enter__()
        wdb1_g = nc.sbuf_tensor("wdbuf", [128, 22 * 128], BF16)
        wdbuf = wdb1_g.__enter__()
        wdb2_g = nc.sbuf_tensor("wdbuf2", [128, 22 * 128], BF16)
        wdbuf2 = wdb2_g.__enter__()
        wdb3_g = nc.sbuf_tensor("wdbuf3", [128, 22 * 128], BF16)
        wdbuf3 = wdb3_g.__enter__()
        WDB = [wdbuf, wdbuf2, wdbuf3]

        def h_ap(ic):
            if ic < DC:
                return attnout.ap()[:, ic * TOK:(ic + 1) * TOK]
            return hbuf2.ap()[:, (ic - DC) * TOK:(ic - DC + 1) * TOK]

        guW0 = C["w"]
        gu_te = []
        gu_silu_pred = [C["act"] + ic + 1 for ic in range(ICN)]
        gu_h_pred = [C["dve"] + 2 * (ic + 1) for ic in range(ICN)]

        def p5_gu_te(t):
            wt("te", t, "dve", c_xn3)
            for ic in range(ICN):
                wt("te", t, "w", guW0 + (ic + 1) * 32)
                if ic >= 2:
                    wt("te", t, "act", gu_silu_pred[ic - 2])
                    wt("te", t, "dve", gu_h_pred[ic - 2])
                gps, ups = SP[ic % 2], APm[ic % 2]
                gbuf = wbufA if ic % 2 == 0 else wbufB
                ubuf = wbufC if ic % 2 == 0 else wbufD
                insu = None
                for kc in range(DC):
                    for ts in range(2):
                        mv = xn3.ap()[:, kc * TOK + ts * 512: kc * TOK + ts * 512 + 512]
                        t.matmul(gps.ap()[:, ts * 512:ts * 512 + 512],
                                 gbuf.ap()[:, kc * 128:(kc + 1) * 128], mv,
                                 start=(kc == 0), stop=(kc == DC - 1))
                        insu = t.matmul(ups.ap()[:, ts * 512:ts * 512 + 512],
                                        ubuf.ap()[:, kc * 128:(kc + 1) * 128], mv,
                                        start=(kc == 0), stop=(kc == DC - 1))
                inc("te", insu, "te", 1)
                gu_te.append(C["te"])
        block.tensor(p5_gu_te)

        def p5_gu_dma(g):
            wt("pool", g, "dve", c_xn3)
            for ic in range(ICN):
                if ic >= 2:
                    wt("pool", g, "te", gu_te[ic - 2])
                else:
                    wt("pool", g, "te", max(buf_last["A"], buf_last["B"]))
                gbuf = wbufA if ic % 2 == 0 else wbufB
                ubuf = wbufC if ic % 2 == 0 else wbufD
                inc("pool", g.dma_start(out=gbuf.ap(), in_=wg_d[ic]), "w", 16)
                inc("pool", g.dma_start(out=ubuf.ap(), in_=wu_d[ic]), "w", 16)
        block.gpsimd(p5_gu_dma)

        def p5_gu_act(sc):
            for ic in range(ICN):
                wt("act", sc, "te", gu_te[ic])
                ins = sc.activation(h_ap(ic), SP[ic % 2].ap(),
                                    AF.Silu, scale=gs_t.ap()[:, ic:ic + 1])
                inc("act", ins, "act", 1)
                assert C["act"] == gu_silu_pred[ic]
        block.scalar(p5_gu_act)

        def p5_gu_dve(v):
            for ic in range(ICN):
                wt("dve", v, "te", gu_te[ic])
                wt("dve", v, "act", gu_silu_pred[ic])
                ins = v.tensor_scalar(sqscr.ap()[:, 0:TOK], APm[ic % 2].ap(),
                                      us_t.ap()[:, ic:ic + 1], None, ALU.mult)
                inc("dve", ins, "dve", 1)
                ins = v.tensor_tensor(h_ap(ic), h_ap(ic), sqscr.ap()[:, 0:TOK], ALU.mult)
                inc("dve", ins, "dve", 1)
                assert C["dve"] == gu_h_pred[ic]
        block.vector(p5_gu_dve)
        c_h_done = C["dve"]

        if "hb1" in taps:
            def tp_hb(sy):
                wt("sy", sy, "dve", c_h_done)
                inc("sy", sy.dma_start(out=tap_d["hb1"], in_=attnout.ap()), "s", 16)
                inc("sy", sy.dma_start(out=tap_d["hb2"], in_=hbuf2.ap()), "s", 16)
            block.sync(tp_hb)

        dnW0 = C["w"]
        dn_te = []
        dn_half = []
        dn_ev_pred = [C["dve"] + mc + 1 for mc in range(DC)]

        def p5_dn_te(t):
            wt("te", t, "dve", c_h_done)
            wt("te", t, "act", gu_silu_pred[-1])
            for mc in range(DC):
                if mc >= 2:
                    wt("te", t, "dve", dn_ev_pred[mc - 2])
                ps = SP[mc % 2]
                ins = None
                for half in range(2):
                    n = 2 * mc + half
                    wt("te", t, "w", dnW0 + (n + 1) * 16)
                    buf = WDB[n % 3]
                    for ici in range(22):
                        ic = half * 22 + ici
                        for ts in range(2):
                            ins = t.matmul(ps.ap()[:, ts * 512:ts * 512 + 512],
                                           buf.ap()[:, ici * 128:(ici + 1) * 128],
                                           h_ap(ic)[:, ts * 512:ts * 512 + 512],
                                           start=(ic == 0), stop=(ic == ICN - 1))
                    inc("te", ins, "te", 1)
                    dn_half.append(C["te"])
                dn_te.append(C["te"])
        block.tensor(p5_dn_te)

        def p5_dn_dma(g):
            wt("pool", g, "dve", c_xn3)
            for n in range(2 * DC):
                if n >= 3:
                    wt("pool", g, "te", dn_half[n - 3])
                buf = WDB[n % 3]
                half = n % 2
                inc("pool", g.dma_start(out=buf.ap(),
                                        in_=wd_d[n // 2][:, half * 22 * 128:(half + 1) * 22 * 128]),
                    "w", 16)
        block.gpsimd(p5_dn_dma)

        d4r = xn3

        def p5_dn_ev(v):
            for mc in range(DC):
                wt("dve", v, "te", dn_te[mc])
                ins = v.tensor_scalar(d4r.ap()[:, mc * TOK:(mc + 1) * TOK],
                                      SP[mc % 2].ap(), ds_t.ap()[:, mc:mc + 1], None, ALU.mult)
                inc("dve", ins, "dve", 1)
                assert C["dve"] == dn_ev_pred[mc]
        block.vector(p5_dn_ev)

        if "d4" in taps:
            def tp_d4(sy):
                wt("sy", sy, "dve", dn_ev_pred[-1])
                inc("sy", sy.dma_start(
                    out=tap_d["d4"].rearrange("(c p) t -> p c t", p=128),
                    in_=d4r.ap().rearrange("p (c t) -> p c t", t=TOK)), "s", 16)
            block.sync(tp_d4)

        nc.leave_named_scope(_scope_n, _scope_i, False)
        _scope_n = 'p6_out'; _scope_i = nc.enter_named_scope('p6_out', False)[0]
        wdb3_g.__exit__(None, None, None)
        wdb2_g.__exit__(None, None, None)
        wdb1_g.__exit__(None, None, None)
        wbufD_g.__exit__(None, None, None)
        wbufC_g.__exit__(None, None, None)
        hbuf2_g.__exit__(None, None, None)
        c_ln4 = ln_stats(DC, TOK, lambda mc: d4r.ap()[:, mc * TOK:(mc + 1) * TOK],
                         LN_EPS, chunk_wait=("dve", lambda mc: dn_ev_pred[mc]))

        fx_g = nc.sbuf_tensor("fx", [128, TOK], F32, side="right")
        fx = fx_g.__enter__()
        fx2_g = nc.sbuf_tensor("fx2", [128, TOK], F32, side="right")
        fx2 = fx2_g.__enter__()
        fot_g = nc.sbuf_tensor("fot", [128, TOK], F32, side="right")
        fot = fot_g.__enter__()
        fot2_g = nc.sbuf_tensor("fot2", [128, TOK], F32, side="right")
        fot2 = fot2_g.__enter__()
        fgx_g = nc.sbuf_tensor("fgx", [128, TOK], F32, side="right")
        fgx = fgx_g.__enter__()
        fgo_g = nc.sbuf_tensor("fgo", [128, TOK], F32, side="right")
        fgo = fgo_g.__enter__()
        FXB = [fx, fx2]
        FOB = [fot, fot2]
        GP_FIN = 11

        fin_dve_pred = [C["dve"] + mc + 1 for mc in range(GP_FIN)]
        gp_pred0 = C["g"]
        gp_fin_pred = [gp_pred0 + i + 1 for i in range(DC - GP_FIN)]
        cs = C["s"]
        fin_ld_pred, out_st_pred = {}, {}
        gfin_ld_pred, gout_st_pred = {}, {}
        for mc in range(GP_FIN):
            cs += 16
            fin_ld_pred[mc] = cs
            if mc >= 2:
                cs += 16
                out_st_pred[mc - 2] = cs
        for mc in range(GP_FIN - 2, GP_FIN):
            cs += 16
            out_st_pred[mc] = cs
        for mc in range(GP_FIN, DC):
            cs += 16
            gfin_ld_pred[mc] = cs
            cs += 16
            gout_st_pred[mc] = cs

        def p6b_sync(sy):
            wt("sy", sy, "s", x2_st_pred[DC - 1])
            wt("sy", sy, "act", x2bf_cast_pred[DC - 1])
            wt("sy", sy, "dve", x2_dve_pred[DC - 1])
            for mc in range(GP_FIN):
                if mc >= 2:
                    wt("sy", sy, "dve", fin_dve_pred[mc - 2])
                ins = sy.dma_start(out=FXB[mc % 2].ap(),
                                   in_=x2s[mc * 128:(mc + 1) * 128, :])
                inc("sy", ins, "s", 16)
                assert C["s"] == fin_ld_pred[mc]
                if mc >= 2:
                    ins = sy.dma_start(out=out_d[(mc - 2) * 128:(mc - 1) * 128, :],
                                       in_=FOB[(mc - 2) % 2].ap())
                    inc("sy", ins, "s", 16)
                    assert C["s"] == out_st_pred[mc - 2]
            for mc in range(GP_FIN - 2, GP_FIN):
                wt("sy", sy, "dve", fin_dve_pred[mc])
                ins = sy.dma_start(out=out_d[mc * 128:(mc + 1) * 128, :],
                                   in_=FOB[mc % 2].ap())
                inc("sy", ins, "s", 16)
                assert C["s"] == out_st_pred[mc]
            for mc in range(GP_FIN, DC):
                i = mc - GP_FIN
                if i >= 1:
                    wt("sy", sy, "g", gp_fin_pred[i - 1])
                ins = sy.dma_start(out=fgx.ap(), in_=x2s[mc * 128:(mc + 1) * 128, :])
                inc("sy", ins, "s", 16)
                assert C["s"] == gfin_ld_pred[mc]
                wt("sy", sy, "g", gp_fin_pred[i])
                ins = sy.dma_start(out=out_d[mc * 128:(mc + 1) * 128, :], in_=fgo.ap())
                inc("sy", ins, "s", 16)
                assert C["s"] == gout_st_pred[mc]
        block.sync(p6b_sync)

        def p6b_dve(v):
            wt("dve", v, "act", c_ln4)
            for mc in range(GP_FIN):
                wt("dve", v, "s", fin_ld_pred[mc])
                if mc >= 2:
                    wt("dve", v, "s", out_st_pred[mc - 2])
                bufx = FXB[mc % 2].ap()
                ob = FOB[mc % 2].ap()
                v.tensor_tensor(ob, d4r.ap()[:, mc * TOK:(mc + 1) * TOK],
                                stM.ap()[:, 0:TOK], ALU.subtract)
                v.tensor_tensor(ob, ob, stR.ap()[:, 0:TOK], ALU.mult)
                ins = v.tensor_tensor(ob, ob, bufx, ALU.add)
                inc("dve", ins, "dve", 1)
                assert C["dve"] == fin_dve_pred[mc]
        block.vector(p6b_dve)

        def p6b_gp(g):
            wt("pool", g, "act", c_ln4)
            for i, mc in enumerate(range(GP_FIN, DC)):
                wt("pool", g, "s", gfin_ld_pred[mc])
                g.tensor_tensor(fgo.ap(), d4r.ap()[:, mc * TOK:(mc + 1) * TOK],
                                stM.ap()[:, 0:TOK], ALU.subtract)
                g.tensor_tensor(fgo.ap(), fgo.ap(), stR.ap()[:, 0:TOK], ALU.mult)
                ins = g.tensor_tensor(fgo.ap(), fgo.ap(), fgx.ap(), ALU.add)
                inc("pool", ins, "g", 1)
                assert C["g"] == gp_fin_pred[i]
        block.gpsimd(p6b_gp)



        nc.leave_named_scope(_scope_n, _scope_i, False)

        def p_final(sy):
            wt("sy", sy, "s", C["s"])
            wt("sy", sy, "w", C["w"])
        block.sync(p_final)

    L.audit()
    def _rng(n):
        ml = nc.lookup_mloc(n)
        return (ml.addr, ml.addr + ml.size() // 128)
    live_p5 = [_rng(n) for n in ("xn3", "hbuf2", "wbufC", "wbufD", "wdbuf", "wdbuf2", "qtmp")]
    for n in ("fx", "fx2", "fot", "fot2"):
        a, b = _rng(n)
        for (c0, c1) in live_p5:
            assert b <= c0 or a >= c1, (n, (a, b), (c0, c1))
    return nc


_BUILD_CACHE = {}


def get_nc(taps=()):
    key = tuple(sorted(taps))
    if key not in _BUILD_CACHE:
        _BUILD_CACHE[key] = build_nc(taps)
    return _BUILD_CACHE[key]


def kernel(**inputs):
    maps = _stage_inputs(inputs)
    nc = get_nc()
    res = run_bass_kernel_spmd(nc, maps, core_ids=list(range(8)))
    return _gather_output(res.results)


# version 10 (final)
# speedup vs baseline: 1.1306x; 1.0122x over previous
import numpy as np
import ml_dtypes
import concourse.bass as bass
import concourse.mybir as mybir
from concourse.bass_utils import run_bass_kernel_spmd

F32 = mybir.dt.float32
BF16 = mybir.dt.bfloat16
AF = mybir.ActivationFunctionType
ALU = mybir.AluOpType
AX = mybir.AxisListType

DIM, DC = 2048, 16
NH, NKV, HD = 32, 8, 64
TOK, TOK2 = 1024, 2048
QT, STN = 8, 16
INTER, ICN = 5632, 44
KMC = 4
LN_EPS, RMS_EPS = 1e-5, 1e-6
NEGBIG = -1.0e30
P3B_BG_ON_DVE = False
PT_LAG = 3
LN1_SQ2 = True

SLOPES = [2.0 ** (-h / 4.0) for h in range(NH)]
CUT = 150.0
KEPT = [min(9, 2 + int(CUT / SLOPES[h]) // 128) for h in range(NH)]
for _h in range(NH):
    if KEPT[_h] < 9:
        assert SLOPES[_h] * (128 * KEPT[_h] - 127) >= CUT - 1e-6


def _vqt(h, st):
    lo, hi = max(0, st - 8), min(7, st + KEPT[h] - 9)
    return (lo, hi) if lo <= hi else None


def _st_list(h):
    return [st for st in range(STN) if _vqt(h, st) is not None]



def _tile_kmajor(W, n_kc, n_m):
    W = np.asarray(W, dtype=np.float32)
    Wr = W.reshape(n_m, 128, n_kc, 128)
    return np.ascontiguousarray(Wr.transpose(0, 3, 2, 1)).reshape(n_m, 128, n_kc * 128)


def _make_tables(Q0):
    sbias = np.empty((128, NH * STN), np.float32)
    S0 = Q0 - 1024
    for h in range(NH):
        for st in range(STN):
            s_glob = S0 + st * 128 + np.arange(128)
            if s_glob[0] < 0 or s_glob[0] >= 2048:
                sbias[:, h * STN + st] = NEGBIG
            else:
                sbias[:, h * STN + st] = (SLOPES[h] * s_glob).astype(np.float32)
    phiq8 = np.empty((32, QT * 128), np.float32)
    q_glob = Q0 + np.arange(QT * 128)
    for h in range(NH):
        phiq8[h, :] = (8.0 * SLOPES[h] * q_glob).astype(np.float32)
    return sbias, phiq8


def _stage_inputs(inputs):
    x = np.asarray(inputs["L0_attn_x"], np.float32)
    wo = np.asarray(inputs["L0_attn_wo"], np.float32)
    perm = np.empty(2048, np.int64)
    for h in range(NH):
        fp = (h // 2) * 128 + (h % 2) * 64
        perm[fp:fp + 64] = h * 64 + np.arange(64)

    def _fm(v, n):
        return np.ascontiguousarray(np.asarray(v, np.float32).reshape(n, 128).T)
    shared = dict(
        wq=_tile_kmajor(inputs["L0_attn_wq"], DC, DC),
        wk=_tile_kmajor(inputs["L0_attn_wk"], DC, KMC),
        wv=_tile_kmajor(inputs["L0_attn_wv"], DC, KMC),
        wo=_tile_kmajor(wo[:, perm], DC, DC),
        wg=_tile_kmajor(inputs["L0_ffn_gate_w"], DC, ICN),
        wu=_tile_kmajor(inputs["L0_ffn_up_w"], DC, ICN),
        wd=_tile_kmajor(inputs["L0_ffn_down_w"], ICN, DC),
        qnw=_fm(inputs["L0_attn_q_norm_w"], DC),
        knw=_fm(inputs["L0_attn_k_norm_w"], KMC),
        gs=_fm(inputs["L0_ffn_gate_s"], ICN),
        us=_fm(inputs["L0_ffn_up_s"], ICN),
        ds=_fm(inputs["L0_ffn_down_s"], DC))
    maps = []
    for c in range(8):
        b, hh = c // 2, c % 2
        xt = np.ascontiguousarray(x[b].T)
        if hh == 0:
            xcore = np.concatenate([np.zeros((DIM, 1024), np.float32), xt[:, 0:1024]], 1)
        else:
            xcore = xt
        sbias, phiq8 = _make_tables(hh * 1024)
        m = dict(shared)
        m["xt"] = np.ascontiguousarray(xcore)
        m["sbias"] = sbias
        m["phiq8"] = phiq8
        maps.append(m)
    return maps


def _gather_output(results):
    out = np.empty((4, 2048, 2048), np.float32)
    for c in range(8):
        b, hh = c // 2, c % 2
        out[b, hh * 1024:(hh + 1) * 1024, :] = results[c]["out"].T
    return out



class SemLog:
    def __init__(self):
        self.ev = {e: [] for e in ("te", "act", "dve", "pool", "sy")}

    def wait(self, eng, ch, val):
        self.ev[eng].append(("w", ch, val))

    def inc(self, eng, ch, val_after):
        self.ev[eng].append(("i", ch, val_after))

    def audit(self):
        pos = {e: 0 for e in self.ev}
        cur = {}
        progress = True
        while progress:
            progress = False
            for e, lst in self.ev.items():
                while pos[e] < len(lst):
                    kind, ch, val = lst[pos[e]]
                    if kind == "i":
                        cur[ch] = max(cur.get(ch, 0), val)
                    elif cur.get(ch, 0) < val:
                        break
                    pos[e] += 1
                    progress = True
        stuck = {e: (pos[e], len(lst)) for e, lst in self.ev.items() if pos[e] < len(lst)}
        if stuck:
            det = {e: self.ev[e][pos[e]] for e in stuck}
            raise RuntimeError(f"SEM AUDIT DEADLOCK: {stuck} next={det}")


_LAST_LOG = None


def build_nc(taps=()):
    nc = bass.Bass("TRN2", target_bir_lowering=False)
    taps = set(taps)

    xt = nc.dram_tensor("xt", [DIM, TOK2], F32, kind="ExternalInput").ap()
    wq_d = nc.dram_tensor("wq", [DC, 128, DC * 128], F32, kind="ExternalInput").ap()
    wk_d = nc.dram_tensor("wk", [KMC, 128, DC * 128], F32, kind="ExternalInput").ap()
    wv_d = nc.dram_tensor("wv", [KMC, 128, DC * 128], F32, kind="ExternalInput").ap()
    wo_d = nc.dram_tensor("wo", [DC, 128, DC * 128], F32, kind="ExternalInput").ap()
    wg_d = nc.dram_tensor("wg", [ICN, 128, DC * 128], F32, kind="ExternalInput").ap()
    wu_d = nc.dram_tensor("wu", [ICN, 128, DC * 128], F32, kind="ExternalInput").ap()
    wd_d = nc.dram_tensor("wd", [DC, 128, ICN * 128], F32, kind="ExternalInput").ap()
    qnw_d = nc.dram_tensor("qnw", [128, DC], F32, kind="ExternalInput").ap()
    knw_d = nc.dram_tensor("knw", [128, KMC], F32, kind="ExternalInput").ap()
    gs_d = nc.dram_tensor("gs", [128, ICN], F32, kind="ExternalInput").ap()
    us_d = nc.dram_tensor("us", [128, ICN], F32, kind="ExternalInput").ap()
    ds_d = nc.dram_tensor("ds", [128, DC], F32, kind="ExternalInput").ap()
    sbias_d = nc.dram_tensor("sbias", [128, NH * STN], F32, kind="ExternalInput").ap()
    phiq8_d = nc.dram_tensor("phiq8", [32, QT * 128], F32, kind="ExternalInput").ap()
    out_d = nc.dram_tensor("out", [DIM, TOK], F32, kind="ExternalOutput").ap()

    x2s = nc.dram_tensor("x2s", [DIM, TOK], F32, kind="Internal").ap()
    d4s = nc.dram_tensor("d4s", [DIM, TOK], BF16, kind="Internal").ap()
    phis = nc.dram_tensor("phis", [QT, 32, 2, 128], BF16, kind="Internal").ap()

    identF_d = nc.inline_tensor(np.eye(128, dtype=np.float32), "identF").ap()
    onesB_d = nc.inline_tensor(np.ones((128, 128), ml_dtypes.bfloat16), "onesB").ap()
    mdiagT_d = nc.inline_tensor(np.triu(np.ones((128, 128), np.float32), 1).astype(ml_dtypes.bfloat16), "mdiagT").ap()
    medgeT_d = nc.inline_tensor(np.tril(np.ones((128, 128), np.float32), -1).astype(ml_dtypes.bfloat16), "medgeT").ap()
    negI_d = nc.inline_tensor((NEGBIG * np.eye(128, dtype=np.float32)).astype(ml_dtypes.bfloat16), "negI").ap()

    tap_d = {}
    for name, shape, dt_ in (
            ("xn", [128, DC * TOK2], BF16), ("khat", [66, NKV * TOK2], BF16),
            ("qhat", [66, NH * TOK], BF16), ("vhat", [128, STN * NKV * 65], BF16),
            ("attnout", [128, DC * TOK], BF16), ("a2", [128, DC * TOK], BF16),
            ("x2", [DIM, TOK], F32), ("xn3", [128, DC * TOK], BF16),
            ("d4", [DIM, TOK], BF16), ("hb1", [128, DC * TOK], BF16),
            ("hb2", [128, (ICN - DC) * TOK], BF16)):
        if name in taps:
            tap_d[name] = nc.dram_tensor("tap_" + name, shape, dt_, kind="ExternalOutput").ap()

    global _LAST_LOG
    L = SemLog()
    _LAST_LOG = L
    C = {"w": 0, "s": 0, "te": 0, "act": 0, "dve": 0, "g": 0}

    with (
        nc.semaphore("sW") as sW,
        nc.semaphore("sS") as sS,
        nc.semaphore("sT") as sT,
        nc.semaphore("sA") as sA,
        nc.semaphore("sV") as sV,
        nc.semaphore("sG") as sG,
        nc.Block() as block,
    ):
        SEM = {"w": sW, "s": sS, "te": sT, "act": sA, "dve": sV, "g": sG}

        def inc(engname, instr, ch, n=1):
            C[ch] += n
            instr.then_inc(SEM[ch], n)
            L.inc(engname, ch, C[ch])
            return C[ch]

        def wt(engname, eng, ch, val):
            if val is None or val <= 0:
                return
            L.wait(engname, ch, val)
            eng.wait_ge(SEM[ch], val)

        sb = nc.alloc_sbuf_tensor
        sbias_t = sb("sbias_t", [128, NH * STN], F32)
        phiq8_t = sb("phiq8_t", [32, QT * 128], F32)
        identF_t = sb("identF_t", [128, 128], F32)
        onesB_t = sb("onesB_t", [128, 128], BF16)
        mdiagT_t = sb("mdiagT_t", [128, 128], BF16)
        medgeT_t = sb("medgeT_t", [128, 128], BF16)
        negI_t = sb("negI_t", [128, 128], BF16)
        qnw_t = sb("qnw_t", [128, DC], F32)
        knw_t = sb("knw_t", [128, KMC], F32)
        gs_t = sb("gs_t", [128, ICN], F32)
        us_t = sb("us_t", [128, ICN], F32)
        ds_t = sb("ds_t", [128, DC], F32)
        wbufA = sb("wbufA", [128, DC * 128], BF16)
        wbufB = sb("wbufB", [128, DC * 128], BF16)
        stF = sb("stF", [128, TOK2], F32)
        stM = sb("stM", [128, TOK2], BF16)
        stR = sb("stR", [128, TOK2], BF16)
        sqscr = sb("sqscr", [128, TOK2], BF16)
        m0buf = sb("m0buf", [128, 32], F32)
        phisc = sb("phisc", [32, 2 * 128], BF16)
        phiA = sb("phiA", [32, 128], F32)
        pt_t = sb("pt_t", [128, 6 * QT * 128], BF16)
        zrow = sb("zrow", [1, 512], BF16)
        gptmp = sb("gptmp", [128, TOK], BF16)

        buf_last = {"A": 0, "B": 0}

        S0 = nc.alloc_psum_tensor("S0", [128, 1024], F32)
        S1 = nc.alloc_psum_tensor("S1", [128, 1024], F32)
        A0 = nc.alloc_psum_tensor("A0", [128, 1024], F32)
        A1 = nc.alloc_psum_tensor("A1", [128, 1024], F32)
        SP, APm = [S0, S1], [A0, A1]

        def p0(sy):
            for dst, src in ((sbias_t, sbias_d), (phiq8_t, phiq8_d), (identF_t, identF_d),
                             (onesB_t, onesB_d), (mdiagT_t, mdiagT_d), (medgeT_t, medgeT_d),
                             (negI_t, negI_d), (qnw_t, qnw_d), (knw_t, knw_d),
                             (gs_t, gs_d), (us_t, us_d), (ds_t, ds_d)):
                inc("sy", sy.dma_start(out=dst.ap(), in_=src), "s", 16)
        block.sync(p0)
        c_consts = C["s"]
        block.tensor(lambda t: wt("te", t, "s", c_consts))
        block.scalar(lambda s_: wt("act", s_, "s", c_consts))
        def _dve_init(v):
            wt("dve", v, "s", c_consts)
            inc("dve", v.memset(zrow.ap(), 0.0), "dve", 1)
        block.vector(_dve_init)
        c_zrow = C["dve"]

        def ln_stats(nchunks, ntok, src_fn, eps, rms_denom=None,
                     pre_wait_dve=None, chunk_wait=None, mm_pre_dve=None,
                     sq2=None):
            nh = ntok // 1024
            sq_pred = [C["act"] + i + 1 for i in range(nchunks)]
            te0 = C["te"]
            rms = rms_denom is not None
            denom = float(rms_denom if rms else DIM)
            if ntok == 1024 and sq2 is None:
                sq_bufs = [sqscr.ap()[:, 0:1024], sqscr.ap()[:, 1024:2048]]
            elif sq2 is not None:
                sq_bufs = [sqscr.ap()[:, 0:ntok], sq2]
            else:
                sq_bufs = [sqscr.ap()[:, 0:ntok]]
            nb = len(sq_bufs)

            def _sq(sc):
                if pre_wait_dve is not None:
                    wt("act", sc, "dve", pre_wait_dve)
                for mc in range(nchunks):
                    if chunk_wait is not None:
                        ch, fn = chunk_wait
                        wt("act", sc, ch, fn(mc))
                    if mc >= nb:
                        wt("act", sc, "te", te0 + mc - nb + 1)
                    ins = sc.activation(sq_bufs[mc % nb], src_fn(mc), AF.Square)
                    inc("act", ins, "act", 1)
            block.scalar(_sq)

            def _mm(t):
                if mm_pre_dve is not None:
                    wt("te", t, "dve", mm_pre_dve)
                for mc in range(nchunks):
                    wt("te", t, "act", sq_pred[mc])
                    ins = None
                    sqb = sq_bufs[mc % nb]
                    for half in range(nh):
                        for ts in range(2):
                            sl = half * 1024 + ts * 512
                            psl = slice(ts * 512, ts * 512 + 512)
                            if not rms:
                                t.matmul(APm[half].ap()[:, psl], onesB_t.ap(),
                                         src_fn(mc)[:, sl:sl + 512],
                                         start=(mc == 0), stop=(mc == nchunks - 1))
                                ins = t.matmul(SP[half].ap()[:, psl], onesB_t.ap(),
                                               sqb[:, sl:sl + 512],
                                               start=(mc == 0), stop=(mc == nchunks - 1))
                            else:
                                ins = t.matmul(APm[half].ap()[:, psl], onesB_t.ap(),
                                               sqb[:, sl:sl + 512],
                                               start=(mc == 0), stop=(mc == nchunks - 1))
                    inc("te", ins, "te", 1)
            block.tensor(_mm)
            c_stat = C["te"]

            def _mr(v):
                wt("dve", v, "te", c_stat)
                for half in range(nh):
                    hs = slice(half * 1024, half * 1024 + 1024)
                    if not rms:
                        v.tensor_scalar(stM.ap()[:, hs], APm[half].ap(), 1.0 / denom, None, ALU.mult)
                        v.tensor_scalar(stF.ap()[:, hs], SP[half].ap(), 1.0 / denom, None, ALU.mult)
                    else:
                        v.tensor_scalar(stF.ap()[:, hs], APm[half].ap(), 1.0 / denom, None, ALU.mult)
                if not rms:
                    v.tensor_tensor(sqscr.ap()[:, 0:ntok], stM.ap()[:, 0:ntok], stM.ap()[:, 0:ntok], ALU.mult)
                    v.tensor_tensor(stF.ap()[:, 0:ntok], stF.ap()[:, 0:ntok], sqscr.ap()[:, 0:ntok], ALU.subtract)
                v.tensor_scalar(stF.ap()[:, 0:ntok], stF.ap()[:, 0:ntok], float(eps), None, ALU.add)
                ins = v.reciprocal(stF.ap()[:, 0:ntok], stF.ap()[:, 0:ntok])
                inc("dve", ins, "dve", 1)
            block.vector(_mr)
            c_r1 = C["dve"]

            def _sqrt(sc):
                wt("act", sc, "dve", c_r1)
                ins = sc.activation(stR.ap()[:, 0:ntok], stF.ap()[:, 0:ntok], AF.Sqrt)
                inc("act", ins, "act", 1)
            block.scalar(_sqrt)
            return C["act"]

        def proj_sweep(n_mc, src_w, ntok, mov_fn, evac_fn,
                       pre_te_dve=None, pre_te_act=None, pre_te_g=None,
                       kc_wait=None, te_hook=None):
            nh = ntok // 1024
            w0 = C["w"]
            te_done = []
            evac_pred = [C["dve"] + i + 1 for i in range(n_mc * nh)]

            def _te(t):
                if pre_te_dve is not None:
                    wt("te", t, "dve", pre_te_dve)
                if pre_te_act is not None:
                    wt("te", t, "act", pre_te_act)
                if pre_te_g is not None:
                    wt("te", t, "g", pre_te_g)
                n = 0
                for mc in range(n_mc):
                    wt("te", t, "w", w0 + (mc + 1) * 16)
                    buf = wbufA if mc % 2 == 0 else wbufB
                    for half in range(nh):
                        if n >= 2:
                            wt("te", t, "dve", evac_pred[n - 2])
                        ps = SP[n % 2]
                        ins = None
                        for kc in range(DC):
                            if n == 0 and kc_wait is not None:
                                wt("te", t, "dve", kc_wait(kc))
                            for ts in range(2):
                                off = half * 1024 + ts * 512
                                ins = t.matmul(ps.ap()[:, ts * 512:ts * 512 + 512],
                                               buf.ap()[:, kc * 128:(kc + 1) * 128],
                                               mov_fn(kc, off),
                                               start=(kc == 0), stop=(kc == DC - 1))
                        inc("te", ins, "te", 1)
                        te_done.append(C["te"])
                        if te_hook is not None:
                            te_hook(t, mc, half)
                        n += 1
            block.tensor(_te)

            def _dve(v):
                n = 0
                for mc in range(n_mc):
                    for half in range(nh):
                        wt("dve", v, "te", te_done[n])
                        ins = evac_fn(v, mc, half, SP[n % 2].ap())
                        inc("dve", ins, "dve", 1)
                        n += 1
            block.vector(_dve)
            assert C["dve"] == evac_pred[-1]

            def _dma(g):
                for mc in range(n_mc):
                    bname = "A" if mc % 2 == 0 else "B"
                    wv_ = te_done[(mc - 2) * nh + nh - 1] if mc >= 2 else buf_last[bname]
                    wt("pool", g, "te", wv_)
                    buf = wbufA if mc % 2 == 0 else wbufB
                    inc("pool", g.dma_start(out=buf.ap(), in_=src_w[mc]), "w", 16)
            block.gpsimd(_dma)
            for mc in range(max(0, n_mc - 2), n_mc):
                buf_last["A" if mc % 2 == 0 else "B"] = te_done[mc * nh + nh - 1]
            return te_done, evac_pred

        khat_g = nc.sbuf_tensor("khat", [66, NKV * TOK2], BF16)
        khat = khat_g.__enter__()
        vhat_g = nc.sbuf_tensor("vhat", [128, STN * NKV * 65], BF16)
        vhat = vhat_g.__enter__()
        vtmpF_g = nc.sbuf_tensor("vtmpF", [128, TOK2], F32)
        vtmpF = vtmpF_g.__enter__()
        xn_g = nc.sbuf_tensor("xn", [128, DC * TOK2], BF16)
        xn = xn_g.__enter__()
        ktmp_g = nc.sbuf_tensor("ktmp", [128, KMC * TOK2], BF16, side="right")
        ktmp = ktmp_g.__enter__()

        _scope_n = 'p1_ln1'; _scope_i = nc.enter_named_scope('p1_ln1', False)[0]
        c_xld0 = C["w"]

        def p1_dma(g):
            for kc in range(DC):
                ins = g.dma_start(out=xn.ap()[:, kc * TOK2:(kc + 1) * TOK2],
                                  in_=xt[kc * 128:(kc + 1) * 128, :])
                inc("pool", ins, "w", 16)
        block.gpsimd(p1_dma)

        c_ln1 = ln_stats(DC, TOK2, lambda mc: xn.ap()[:, mc * TOK2:(mc + 1) * TOK2],
                         LN_EPS, chunk_wait=("w", lambda mc: c_xld0 + (mc + 1) * 16),
                         sq2=(ktmp.ap()[:, 0:TOK2] if LN1_SQ2 else None))

        def p1_norm(v):
            wt("dve", v, "act", c_ln1)
            for kc in range(DC):
                sl = slice(kc * TOK2, (kc + 1) * TOK2)
                v.tensor_tensor(xn.ap()[:, sl], xn.ap()[:, sl], stM.ap(), ALU.subtract)
                ins = v.tensor_tensor(xn.ap()[:, sl], xn.ap()[:, sl], stR.ap(), ALU.mult)
                inc("dve", ins, "dve", 1)
        block.vector(p1_norm)
        c_xn = C["dve"]

        if "xn" in taps:
            def tp_xn(sy):
                wt("sy", sy, "dve", c_xn)
                inc("sy", sy.dma_start(out=tap_d["xn"], in_=xn.ap()), "s", 16)
            block.sync(tp_xn)

        nc.leave_named_scope(_scope_n, _scope_i, False)
        _scope_n = 'p2_k'; _scope_i = nc.enter_named_scope('p2_k', False)[0]
        norm_pred = [c_xn - DC + kc + 1 for kc in range(DC)]
        k_te, k_ev = proj_sweep(
            KMC, wk_d, TOK2,
            lambda kc, off: xn.ap()[:, kc * TOK2 + off: kc * TOK2 + off + 512],
            lambda v, mc, half, ps: v.tensor_copy(
                ktmp.ap()[:, mc * TOK2 + half * 1024: mc * TOK2 + half * 1024 + 1024], ps),
            kc_wait=lambda kc: norm_pred[kc])

        c_krms = ln_stats(KMC, TOK2, lambda mc: ktmp.ap()[:, mc * TOK2:(mc + 1) * TOK2],
                          RMS_EPS, rms_denom=512.0,
                          chunk_wait=("dve", lambda mc: k_ev[2 * mc + 1]))

        def p2_khat(v):
            wt("dve", v, "act", c_krms)
            v.memset(khat.ap()[64:66, :], 1.0)
            for mc in range(KMC):
                for par in range(2):
                    g_head = 2 * mc + par
                    ins = v.scalar_tensor_tensor(
                        khat.ap()[0:64, g_head * TOK2:(g_head + 1) * TOK2],
                        ktmp.ap()[par * 64:(par + 1) * 64, mc * TOK2:(mc + 1) * TOK2],
                        knw_t.ap()[par * 64:(par + 1) * 64, mc:mc + 1],
                        stR.ap()[par * 64:(par + 1) * 64, 0:TOK2],
                        ALU.mult, ALU.mult)
                    inc("dve", ins, "dve", 1)
        block.vector(p2_khat)
        c_khat = C["dve"]

        if "khat" in taps:
            def tp_kh(sy):
                wt("sy", sy, "dve", c_khat)
                inc("sy", sy.dma_start(out=tap_d["khat"], in_=khat.ap()), "s", 16)
            block.sync(tp_kh)

        nc.leave_named_scope(_scope_n, _scope_i, False)
        _scope_n = 'p2_v'; _scope_i = nc.enter_named_scope('p2_v', False)[0]
        vW0 = C["w"]
        v_te = []
        v_stf, v_evh = {}, {}
        cd = C["dve"]
        for mc in range(KMC):
            for half in range(2):
                cd += 1
                v_stf[(mc, half)] = cd
            for grp in range(4):
                cd += 1
                v_evh[(mc, grp)] = cd

        def p2_v_te(t):
            wt("te", t, "dve", k_ev[-1])
            n = 0
            for mc in range(KMC):
                wt("te", t, "w", vW0 + (mc + 1) * 16)
                buf = wbufA if mc % 2 == 0 else wbufB
                for half in range(2):
                    if n >= 2:
                        wt("te", t, "dve", v_stf[(mc - 1, half)])
                    ps = SP[n % 2]
                    ins = None
                    for kc in range(DC):
                        for ts in range(2):
                            off = half * 1024 + ts * 512
                            ins = t.matmul(ps.ap()[:, ts * 512:ts * 512 + 512],
                                           buf.ap()[:, kc * 128:(kc + 1) * 128],
                                           xn.ap()[:, kc * TOK2 + off: kc * TOK2 + off + 512],
                                           start=(kc == 0), stop=(kc == DC - 1))
                    inc("te", ins, "te", 1)
                    v_te.append(C["te"])
                    n += 1
                wt("te", t, "dve", v_stf[(mc, 1)])
                for grp in range(4):
                    if mc > 0 or grp >= 2:
                        key = (mc, grp - 2) if grp >= 2 else (mc - 1, grp + 2)
                        wt("te", t, "dve", v_evh[key])
                    ins = None
                    for j in range(4):
                        sc16 = grp * 4 + j
                        cps = ((grp % 2) * 4 + j) * 128
                        ins = t.transpose(A0.ap()[:, cps:cps + 128],
                                          vtmpF.ap()[:, sc16 * 128:(sc16 + 1) * 128],
                                          identF_t.ap())
                    inc("te", ins, "te", 1)
        block.tensor(p2_v_te)
        v_tr = {}
        ctr = C["te"]
        tt = v_te[0] - 1
        tt = None
        cnt = v_te[0] - 1
        idx = 0
        for mc in range(KMC):
            cnt = v_te[2 * mc + 1]
            for grp in range(4):
                cnt += 1
                v_tr[(mc, grp)] = cnt
        assert cnt == C["te"]

        def p2_v_dve(v):
            n = 0
            for mc in range(KMC):
                for half in range(2):
                    wt("dve", v, "te", v_te[n])
                    ins = v.tensor_copy(vtmpF.ap()[:, half * 1024:half * 1024 + 1024],
                                        SP[n % 2].ap())
                    inc("dve", ins, "dve", 1)
                    assert C["dve"] == v_stf[(mc, half)]
                    n += 1
                for grp in range(4):
                    wt("dve", v, "te", v_tr[(mc, grp)])
                    ins = None
                    for j in range(4):
                        sc16 = grp * 4 + j
                        cps = ((grp % 2) * 4 + j) * 128
                        for par in range(2):
                            g_head = 2 * mc + par
                            col = (sc16 * NKV + g_head) * 65
                            ins = v.tensor_copy(vhat.ap()[:, col:col + 64],
                                                A0.ap()[:, cps + par * 64: cps + par * 64 + 64])
                    inc("dve", ins, "dve", 1)
                    assert C["dve"] == v_evh[(mc, grp)]
        block.vector(p2_v_dve)

        def p2_v_dma(g):
            for mc in range(KMC):
                bname = "A" if mc % 2 == 0 else "B"
                wv_ = v_te[(mc - 2) * 2 + 1] if mc >= 2 else buf_last[bname]
                wt("pool", g, "te", wv_)
                buf = wbufA if mc % 2 == 0 else wbufB
                inc("pool", g.dma_start(out=buf.ap(), in_=wv_d[mc]), "w", 16)
        block.gpsimd(p2_v_dma)
        buf_last["A"] = v_te[5]
        buf_last["B"] = v_te[7]

        def p2_v_ones(v):
            ins = v.memset(vhat.ap().rearrange("p (b c) -> p b c", c=65)[:, :, 64:65], 1.0)
            inc("dve", ins, "dve", 1)
        block.vector(p2_v_ones)
        c_vhat = C["dve"]

        if "vhat" in taps:
            def tp_vh(sy):
                wt("sy", sy, "dve", c_vhat)
                inc("sy", sy.dma_start(out=tap_d["vhat"], in_=vhat.ap()), "s", 16)
            block.sync(tp_vh)

        ktmp_g.__exit__(None, None, None)

        nc.leave_named_scope(_scope_n, _scope_i, False)
        _scope_n = 'p2_q'; _scope_i = nc.enter_named_scope('p2_q', False)[0]
        qtmp_g = nc.sbuf_tensor("qtmp", [128, DC * TOK], BF16, side="right")
        qtmp = qtmp_g.__enter__()

        q_te, q_ev = proj_sweep(
            DC, wq_d, TOK,
            lambda kc, off: xn.ap()[:, kc * TOK2 + 1024 + off: kc * TOK2 + 1024 + off + 512],
            lambda v, mc, half, ps: v.tensor_copy(qtmp.ap()[:, mc * TOK:(mc + 1) * TOK], ps),
            pre_te_dve=c_vhat)

        c_qrms = ln_stats(DC, TOK, lambda mc: qtmp.ap()[:, mc * TOK:(mc + 1) * TOK],
                          RMS_EPS, rms_denom=float(DIM),
                          chunk_wait=("dve", lambda mc: q_ev[mc]))

        xn_g.__exit__(None, None, None)
        vtmpF_g.__exit__(None, None, None)
        qhat_g = nc.sbuf_tensor("qhat", [66, NH * TOK], BF16)
        qhat = qhat_g.__enter__()

        def p2_qhat(v):
            wt("dve", v, "act", c_qrms)
            for mc in range(DC):
                for par in range(2):
                    head = 2 * mc + par
                    ins = v.scalar_tensor_tensor(
                        qhat.ap()[0:64, head * TOK:(head + 1) * TOK],
                        qtmp.ap()[par * 64:(par + 1) * 64, mc * TOK:(mc + 1) * TOK],
                        qnw_t.ap()[par * 64:(par + 1) * 64, mc:mc + 1],
                        stR.ap()[par * 64:(par + 1) * 64, 0:TOK],
                        ALU.mult, ALU.mult)
                    inc("dve", ins, "dve", 1)
        block.vector(p2_qhat)
        c_qhat_rows = C["dve"]

        nc.leave_named_scope(_scope_n, _scope_i, False)
        _scope_n = 'p3a_diag'; _scope_i = nc.enter_named_scope('p3a_diag', False)[0]
        diag_te = {}
        diag_max = {}
        ph_stt = {}
        cd = C["dve"]
        for qt in range(QT):
            for grp in range(4):
                cd += 1
                diag_max[(qt, grp)] = cd
            cd += 1
            ph_stt[qt] = cd
        te_p3a0 = C["te"]
        ph_tr_pred = {qt: te_p3a0 + 5 * qt + 5 for qt in range(QT)}

        def p3a_te(t):
            wt("te", t, "dve", c_qhat_rows)
            for qt in range(QT):
                st = qt + 8
                for grp in range(4):
                    if qt > 0 or grp >= 2:
                        key = (qt, grp - 2) if grp >= 2 else (qt - 1, grp + 2)
                        wt("te", t, "dve", diag_max[key])
                    ins = None
                    for j in range(8):
                        h = grp * 8 + j
                        g_head = h // 4
                        ins = t.matmul(SP[grp % 2].ap()[:, j * 128:(j + 1) * 128],
                                       qhat.ap()[0:64, h * TOK + qt * 128: h * TOK + qt * 128 + 128],
                                       khat.ap()[0:64, g_head * TOK2 + st * 128: g_head * TOK2 + st * 128 + 128],
                                       start=True, stop=True)
                    inc("te", ins, "te", 1)
                    diag_te[(qt, grp)] = C["te"]
                wt("te", t, "dve", diag_max[(qt, 3)])
                if qt >= 2:
                    wt("te", t, "dve", ph_stt[qt - 2])
                ins = t.transpose(A1.ap()[0:32, (qt % 2) * 512:(qt % 2) * 512 + 128],
                                  m0buf.ap(), identF_t.ap())
                inc("te", ins, "te", 1)
                assert C["te"] == ph_tr_pred[qt]
        block.tensor(p3a_te)

        s_p3a0 = C["s"]

        def p3a_dve(v):
            for qt in range(QT):
                for grp in range(4):
                    wt("dve", v, "te", diag_te[(qt, grp)])
                    if qt >= 1 and grp == 0:
                        wt("dve", v, "te", ph_tr_pred[qt - 1])
                    ins = None
                    for j in range(8):
                        h = grp * 8 + j
                        ins = v.reduce_max(m0buf.ap()[:, h:h + 1],
                                           SP[grp % 2].ap()[:, j * 128:(j + 1) * 128],
                                           axis=AX.X, negate=True)
                    inc("dve", ins, "dve", 1)
                    assert C["dve"] == diag_max[(qt, grp)]
                wt("dve", v, "te", ph_tr_pred[qt])
                if qt >= 1:
                    wt("dve", v, "s", s_p3a0 + 32 * (qt - 1) + 16)
                ps = A1.ap()[0:32, (qt % 2) * 512:(qt % 2) * 512 + 128]
                v.tensor_tensor(phiA.ap(), ps, phiq8_t.ap()[:, qt * 128:(qt + 1) * 128], ALU.subtract)
                v.tensor_copy(phisc.ap()[:, 0:128], phiA.ap())
                ins = v.tensor_tensor(phisc.ap()[:, 128:256], phiA.ap(), phisc.ap()[:, 0:128], ALU.subtract)
                inc("dve", ins, "dve", 1)
                assert C["dve"] == ph_stt[qt]
        block.vector(p3a_dve)

        def p3a_dma(sy):
            for qt in range(QT):
                wt("sy", sy, "dve", ph_stt[qt])
                ins = sy.dma_start(out=phis[qt],
                                   in_=phisc.ap().rearrange("h (e q) -> h e q", e=2))
                inc("sy", ins, "s", 16)
                wt("sy", sy, "s", C["s"])
                ins = sy.dma_start(
                    out=qhat.ap()[64:66, :].rearrange("e (h q) -> e h q", h=32)[:, :, qt * 128:(qt + 1) * 128],
                    in_=phis[qt].rearrange("h e q -> e h q"))
                inc("sy", ins, "s", 16)
        block.sync(p3a_dma)
        c_phi_all = C["s"]

        if "qhat" in taps:
            def tp_qh(sy):
                wt("sy", sy, "s", c_phi_all)
                inc("sy", sy.dma_start(out=tap_d["qhat"], in_=qhat.ap()), "s", 16)
            block.sync(tp_qh)

        nc.leave_named_scope(_scope_n, _scope_i, False)
        _scope_n = 'p3b_attn'; _scope_i = nc.enter_named_scope('p3b_attn', False)[0]
        def pt_ap(h, st, lo, hi):
            base = ((h % 2) * 3 + (st % PT_LAG)) * QT * 128
            return pt_t.ap()[:, base + lo * 128: base + (hi + 1) * 128]

        sched = []
        exp_order = []
        for hp in range(0, NH, 2):
            hs = [hp, hp + 1]
            stl = {h: _st_list(h) for h in hs}
            union = sorted(set(stl[hp]) | set(stl[hp + 1]))
            pend = {h: [] for h in hs}
            zf_done = {h: False for h in hs}
            ndiv = 0
            for st in union:
                for h in hs:
                    if st in stl[h]:
                        sched.append(("kq", h, st))
                        exp_order.append((h, st))
                        pend[h].append(st)
                if hp >= 2 and ndiv < 2:
                    sched.append(("div", hp - 2 + ndiv))
                    ndiv += 1
                for h in hs:
                    if len(pend[h]) >= PT_LAG:
                        if not zf_done[h]:
                            sched.append(("zf", h))
                            zf_done[h] = True
                        sched.append(("av", h, pend[h].pop(0)))
            for h in hs:
                while pend[h]:
                    if not zf_done[h]:
                        sched.append(("zf", h))
                        zf_done[h] = True
                    sched.append(("av", h, pend[h].pop(0)))
        sched.append(("div", NH - 2))
        sched.append(("div", NH - 1))

        exp_pred = {}
        ca = C["act"]
        for key in exp_order:
            ca += 1
            exp_pred[key] = ca
        prev_s_tenant = {}
        lastp = {0: None, 1: None}
        for (h, st) in exp_order:
            p = h % 2
            if lastp[p] is not None:
                prev_s_tenant[(h, st)] = lastp[p]
            lastp[p] = (h, st)
        prev_pt_tenant = {}
        hist = {}
        for (h, st) in exp_order:
            hist.setdefault(h, []).append(st)
        for h, sts_h in hist.items():
            for i, st in enumerate(sts_h):
                if i >= PT_LAG:
                    prev_pt_tenant[(h, st)] = (h, sts_h[i - PT_LAG])
        den_pred, div_pred = {}, {}
        cd = C["dve"]
        for h in range(NH):
            cd += 1
            den_pred[h] = cd
            cd += 1
            div_pred[h] = cd
        kq_done, av_done, div_te = {}, {}, {}
        c_diag_last = diag_max[(QT - 1, 3)]
        c_ph_last = ph_stt[QT - 1]

        def p3b_te(t):
            wt("te", t, "s", c_phi_all)
            wt("te", t, "dve", c_zrow)
            for op in sched:
                kind = op[0]
                if kind == "kq":
                    _, h, st = op
                    g_head = h // 4
                    lo, hi = _vqt(h, st)
                    prev = prev_s_tenant.get((h, st))
                    if prev is not None:
                        wt("te", t, "act", exp_pred[prev])
                    else:
                        wt("te", t, "dve", c_diag_last)
                    diag_q = st - 8
                    edge_q = st if (KEPT[h] == 9 and st <= 7) else None
                    ins = None
                    for bank in range(2):
                        qlist = [qt for qt in range(lo, hi + 1) if qt // 4 == bank]
                        masked = [qt for qt in qlist
                                  if qt == diag_q or (edge_q is not None and qt == edge_q)]
                        normal = [qt for qt in qlist if qt not in masked]
                        for qt in normal:
                            ins = t.matmul(SP[h % 2].ap()[:, qt * 128:(qt + 1) * 128],
                                           khat.ap()[0:66, g_head * TOK2 + st * 128: g_head * TOK2 + st * 128 + 128],
                                           qhat.ap()[0:66, h * TOK + qt * 128: h * TOK + qt * 128 + 128],
                                           start=True, stop=True)
                        for qt in masked:
                            mt = mdiagT_t if qt == diag_q else medgeT_t
                            t.matmul(SP[h % 2].ap()[:, qt * 128:(qt + 1) * 128],
                                     mt.ap(), negI_t.ap(), start=True, stop=False)
                            ins = t.matmul(SP[h % 2].ap()[:, qt * 128:(qt + 1) * 128],
                                           khat.ap()[0:66, g_head * TOK2 + st * 128: g_head * TOK2 + st * 128 + 128],
                                           qhat.ap()[0:66, h * TOK + qt * 128: h * TOK + qt * 128 + 128],
                                           start=False, stop=True)
                    inc("te", ins, "te", 1)
                    kq_done[(h, st)] = C["te"]
                elif kind == "av":
                    _, h, st = op
                    g_head = h // 4
                    lo, hi = _vqt(h, st)
                    wt("te", t, "act", exp_pred[(h, st)])
                    ins = None
                    for qt in range(lo, hi + 1):
                        last = (st == qt + 8)
                        vcol = (st * NKV + g_head) * 65
                        ins = t.matmul(APm[h % 2].ap()[0:65, qt * 128:(qt + 1) * 128],
                                       vhat.ap()[:, vcol:vcol + 65],
                                       pt_ap(h, st, qt, qt),
                                       start=False, stop=last)
                    inc("te", ins, "te", 1)
                    av_done[(h, st)] = C["te"]
                elif kind == "zf":
                    _, h = op
                    wt("te", t, "dve", div_pred[h - 2] if h >= 2 else c_ph_last)
                    for bank in range(2):
                        t.matmul(APm[h % 2].ap()[0:65, bank * 512:bank * 512 + 512],
                                 onesB_t.ap()[0:1, 0:65], zrow.ap(),
                                 start=True, stop=False)
                else:
                    _, hd = op
                    wt("te", t, "dve", den_pred[hd])
                    ins = None
                    for ts in range(2):
                        ins = t.matmul(APm[hd % 2].ap()[64:128, ts * 512:ts * 512 + 512],
                                       onesB_t.ap()[0:1, 0:64],
                                       stM.ap()[0:1, ts * 512:ts * 512 + 512],
                                       start=True, stop=True, tile_position=(0, 64))
                    inc("te", ins, "te", 1)
                    div_te[hd] = C["te"]
        block.tensor(p3b_te)

        def p3b_act(sc):
            for (h, st) in exp_order:
                lo, hi = _vqt(h, st)
                wt("act", sc, "te", kq_done[(h, st)])
                pv = prev_pt_tenant.get((h, st))
                if pv is not None:
                    wt("act", sc, "te", av_done[pv])
                ins = sc.activation(
                    pt_ap(h, st, lo, hi),
                    SP[h % 2].ap()[:, lo * 128:(hi + 1) * 128],
                    AF.Exp, bias=sbias_t.ap()[:, h * STN + st: h * STN + st + 1],
                    scale=0.125)
                inc("act", ins, "act", 1)
                assert C["act"] == exp_pred[(h, st)]
        block.scalar(p3b_act)

        attnout = qtmp

        avE = sqscr

        bg_pred = {}
        cg = C["g"]
        for h in range(NH):
            cg += 1
            bg_pred[h] = cg

        def _bg(v, h):
            ebuf = avE.ap()[64:128, (h % 2) * TOK:(h % 2) * TOK + TOK]
            rb = stF.ap()[64:128, (h % 2) * TOK:(h % 2) * TOK + TOK]
            mc = h // 2
            if h % 2 == 1:
                ins = v.tensor_tensor(attnout.ap()[64:128, mc * TOK:(mc + 1) * TOK],
                                      ebuf, rb, ALU.mult)
            else:
                v.tensor_tensor(gptmp.ap()[64:128, :], ebuf, rb, ALU.mult)
                ins = v.tensor_copy(attnout.ap()[0:64, mc * TOK:(mc + 1) * TOK],
                                    gptmp.ap()[64:128, :])
            inc("dve", ins, "g", 1)
            assert C["g"] == bg_pred[h]

        def p3b_dve(v):
            for h in range(NH):
                sts_h = hist[h]
                wt("dve", v, "te", av_done[(h, sts_h[-1])])
                if h >= 1:
                    wt("dve", v, "te", div_te[h - 1])
                ins = v.tensor_copy(stM.ap()[0:1, 0:TOK], APm[h % 2].ap()[64:65, :])
                inc("dve", ins, "dve", 1)
                assert C["dve"] == den_pred[h]
                wt("dve", v, "te", div_te[h])
                ebuf = avE.ap()[64:128, (h % 2) * TOK:(h % 2) * TOK + TOK]
                v.reciprocal(stF.ap()[64:128, (h % 2) * TOK:(h % 2) * TOK + TOK],
                             APm[h % 2].ap()[64:128, :])
                ins = v.tensor_copy(ebuf, APm[h % 2].ap()[0:64, :])
                inc("dve", ins, "dve", 1)
                assert C["dve"] == div_pred[h]
                if h >= 1:
                    _bg(v, h - 1)
            _bg(v, NH - 1)
        block.vector(p3b_dve)
        c_attnout = C["dve"]
        c_bg_all = C["g"]
        c_exp_last = exp_pred[exp_order[-1]]

        if "attnout" in taps:
            def tp_ao(sy):
                wt("sy", sy, "dve", c_attnout)
                inc("sy", sy.dma_start(out=tap_d["attnout"], in_=attnout.ap()), "s", 16)
            block.sync(tp_ao)

        nc.leave_named_scope(_scope_n, _scope_i, False)
        _scope_n = 'p4_wo'; _scope_i = nc.enter_named_scope('p4_wo', False)[0]
        qhat_g.__exit__(None, None, None)
        vhat_g.__exit__(None, None, None)
        khat_g.__exit__(None, None, None)
        a2_g = nc.sbuf_tensor("a2", [128, DC * TOK], BF16)
        a2 = a2_g.__enter__()
        ot_g = nc.sbuf_tensor("ot_t", [128, TOK], F32, side="right")
        ot_t = ot_g.__enter__()
        ot2_g = nc.sbuf_tensor("ot_t2", [128, TOK], F32, side="right")
        ot_t2 = ot2_g.__enter__()
        xrs_g = nc.sbuf_tensor("xrs", [128, TOK], F32, side="right")
        xrs = xrs_g.__enter__()
        xrs2_g = nc.sbuf_tensor("xrs2", [128, TOK], F32, side="right")
        xrs2 = xrs2_g.__enter__()
        x2bf_g = nc.sbuf_tensor("x2bf", [128, DC * TOK], BF16, side="right")
        x2bf = x2bf_g.__enter__()

        ln2_sq_pred = [C["act"] + mc + 1 for mc in range(DC)]
        ln2_stat_te = {}

        def _ln2_stat(t, m, last):
            wt("te", t, "act", ln2_sq_pred[m])
            ins = None
            for ts in range(2):
                psl = slice(ts * 512, ts * 512 + 512)
                t.matmul(A0.ap()[:, psl], onesB_t.ap(),
                         a2.ap()[:, m * TOK:(m + 1) * TOK][:, psl],
                         start=(m == 0), stop=last)
                sqb = sqscr.ap()[:, (m % 2) * 1024:(m % 2) * 1024 + 1024]
                ins = t.matmul(A1.ap()[:, psl], onesB_t.ap(), sqb[:, psl],
                               start=(m == 0), stop=last)
            inc("te", ins, "te", 1)
            ln2_stat_te[m] = C["te"]

        def wo_hook(t, mc, half):
            if mc >= 1:
                _ln2_stat(t, mc - 1, False)

        wo_te, wo_ev = proj_sweep(
            DC, wo_d, TOK,
            lambda kc, off: attnout.ap()[:, kc * TOK + off: kc * TOK + off + 512],
            lambda v, mc, half, ps: v.tensor_copy(a2.ap()[:, mc * TOK:(mc + 1) * TOK], ps),
            pre_te_dve=c_attnout, pre_te_act=c_exp_last, pre_te_g=c_bg_all,
            te_hook=wo_hook)

        def ln2_sq(sc):
            for mc in range(DC):
                wt("act", sc, "dve", wo_ev[mc])
                if mc >= 2:
                    wt("act", sc, "te", ln2_stat_te[mc - 2])
                ins = sc.activation(sqscr.ap()[:, (mc % 2) * 1024:(mc % 2) * 1024 + 1024],
                                    a2.ap()[:, mc * TOK:(mc + 1) * TOK], AF.Square)
                inc("act", ins, "act", 1)
                assert C["act"] == ln2_sq_pred[mc]
        block.scalar(ln2_sq)

        def ln2_last(t):
            _ln2_stat(t, DC - 1, True)
        block.tensor(ln2_last)
        c_ln2_stat = C["te"]

        def ln2_mr(v):
            wt("dve", v, "te", c_ln2_stat)
            v.tensor_scalar(stM.ap()[:, 0:TOK], A0.ap(), 1.0 / DIM, None, ALU.mult)
            v.tensor_scalar(stF.ap()[:, 0:TOK], A1.ap(), 1.0 / DIM, None, ALU.mult)
            v.tensor_tensor(sqscr.ap()[:, 0:TOK], stM.ap()[:, 0:TOK], stM.ap()[:, 0:TOK], ALU.mult)
            v.tensor_tensor(stF.ap()[:, 0:TOK], stF.ap()[:, 0:TOK], sqscr.ap()[:, 0:TOK], ALU.subtract)
            v.tensor_scalar(stF.ap()[:, 0:TOK], stF.ap()[:, 0:TOK], float(LN_EPS), None, ALU.add)
            ins = v.reciprocal(stF.ap()[:, 0:TOK], stF.ap()[:, 0:TOK])
            inc("dve", ins, "dve", 1)
        block.vector(ln2_mr)
        c_ln2_r1 = C["dve"]

        def ln2_sqrt(sc):
            wt("act", sc, "dve", c_ln2_r1)
            ins = sc.activation(stR.ap()[:, 0:TOK], stF.ap()[:, 0:TOK], AF.Sqrt)
            inc("act", ins, "act", 1)
        block.scalar(ln2_sqrt)
        c_ln2 = C["act"]

        if "a2" in taps:
            def tp_a2(sy):
                wt("sy", sy, "dve", wo_ev[-1])
                inc("sy", sy.dma_start(out=tap_d["a2"], in_=a2.ap()), "s", 16)
            block.sync(tp_a2)

        x2_dve_pred = [C["dve"] + mc + 1 for mc in range(DC)]
        cs = C["s"]
        ld_pred, x2_st_pred = {}, {}
        for mc in range(DC):
            cs += 16
            ld_pred[mc] = cs
            if mc >= 1:
                cs += 16
                x2_st_pred[mc - 1] = cs
        cs += 16
        x2_st_pred[DC - 1] = cs

        def p4b_sync(sy):
            wt("sy", sy, "dve", c_attnout)
            wt("sy", sy, "g", c_bg_all)
            for mc in range(DC):
                if mc >= 2:
                    wt("sy", sy, "dve", x2_dve_pred[mc - 2])
                ins = sy.dma_start(out=(xrs if mc % 2 == 0 else xrs2).ap(),
                                   in_=xt[mc * 128:(mc + 1) * 128, 1024:2048])
                inc("sy", ins, "s", 16)
                assert C["s"] == ld_pred[mc]
                if mc >= 1:
                    wt("sy", sy, "dve", x2_dve_pred[mc - 1])
                    ins = sy.dma_start(out=x2s[(mc - 1) * 128:mc * 128, :],
                                       in_=(ot_t if (mc - 1) % 2 == 0 else ot_t2).ap())
                    inc("sy", ins, "s", 16)
                    assert C["s"] == x2_st_pred[mc - 1]
            wt("sy", sy, "dve", x2_dve_pred[DC - 1])
            ins = sy.dma_start(out=x2s[(DC - 1) * 128:DC * 128, :],
                               in_=(ot_t if (DC - 1) % 2 == 0 else ot_t2).ap())
            inc("sy", ins, "s", 16)
            assert C["s"] == x2_st_pred[DC - 1]
        block.sync(p4b_sync)

        x2bf_cast_pred = [c_ln2 + mc + 1 for mc in range(DC)]

        def p4b_dve(v):
            wt("dve", v, "act", c_ln2)
            for mc in range(DC):
                wt("dve", v, "s", ld_pred[mc])
                if mc >= 2:
                    wt("dve", v, "s", x2_st_pred[mc - 2])
                    wt("dve", v, "act", x2bf_cast_pred[mc - 2])
                buf = (xrs if mc % 2 == 0 else xrs2).ap()
                ob = (ot_t if mc % 2 == 0 else ot_t2).ap()
                a2c = a2.ap()[:, mc * TOK:(mc + 1) * TOK]
                v.tensor_tensor(ob, a2c, stM.ap()[:, 0:TOK], ALU.subtract)
                v.tensor_tensor(ob, ob, stR.ap()[:, 0:TOK], ALU.mult)
                ins = v.tensor_tensor(ob, ob, buf, ALU.add)
                inc("dve", ins, "dve", 1)
                assert C["dve"] == x2_dve_pred[mc]
        block.vector(p4b_dve)
        c_x2 = C["dve"]

        def p4b_cast(sc):
            for mc in range(DC):
                wt("act", sc, "dve", x2_dve_pred[mc])
                ob = (ot_t if mc % 2 == 0 else ot_t2).ap()
                ins = sc.activation(x2bf.ap()[:, mc * TOK:(mc + 1) * TOK], ob, AF.Copy)
                assert inc("act", ins, "act", 1) == x2bf_cast_pred[mc]
        block.scalar(p4b_cast)

        if "x2" in taps:
            def tp_x2(sy):
                wt("sy", sy, "s", x2_st_pred[DC - 1])
                inc("sy", sy.dma_start(out=tap_d["x2"], in_=x2s), "s", 16)
            block.sync(tp_x2)



        nc.leave_named_scope(_scope_n, _scope_i, False)
        _scope_n = 'p5_ffn'; _scope_i = nc.enter_named_scope('p5_ffn', False)[0]
        a2_g.__exit__(None, None, None)
        xn3_g = nc.sbuf_tensor("xn3", [128, DC * TOK], BF16)
        xn3 = xn3_g.__enter__()

        c_ln3 = ln_stats(DC, TOK, lambda mc: x2bf.ap()[:, mc * TOK:(mc + 1) * TOK],
                         LN_EPS, chunk_wait=("act", lambda mc: x2bf_cast_pred[mc]))

        def p5_xn3(v):
            wt("dve", v, "act", c_ln3)
            for mc in range(DC):
                sl = slice(mc * TOK, (mc + 1) * TOK)
                v.tensor_tensor(xn3.ap()[:, sl], x2bf.ap()[:, sl], stM.ap()[:, 0:TOK], ALU.subtract)
                ins = v.tensor_tensor(xn3.ap()[:, sl], xn3.ap()[:, sl], stR.ap()[:, 0:TOK], ALU.mult)
                inc("dve", ins, "dve", 1)
        block.vector(p5_xn3)
        c_xn3 = C["dve"]

        if "xn3" in taps:
            def tp_x3(sy):
                wt("sy", sy, "dve", c_xn3)
                inc("sy", sy.dma_start(out=tap_d["xn3"], in_=xn3.ap()), "s", 16)
            block.sync(tp_x3)

        x2bf_g.__exit__(None, None, None)
        xrs2_g.__exit__(None, None, None)
        xrs_g.__exit__(None, None, None)
        ot2_g.__exit__(None, None, None)
        ot_g.__exit__(None, None, None)

        hbuf2_g = nc.sbuf_tensor("hbuf2", [128, (ICN - DC) * TOK], BF16)
        hbuf2 = hbuf2_g.__enter__()
        wbufC_g = nc.sbuf_tensor("wbufC", [128, DC * 128], BF16)
        wbufC = wbufC_g.__enter__()
        wbufD_g = nc.sbuf_tensor("wbufD", [128, DC * 128], BF16)
        wbufD = wbufD_g.__enter__()
        wdb1_g = nc.sbuf_tensor("wdbuf", [128, 22 * 128], BF16)
        wdbuf = wdb1_g.__enter__()
        wdb2_g = nc.sbuf_tensor("wdbuf2", [128, 22 * 128], BF16)
        wdbuf2 = wdb2_g.__enter__()
        wdb3_g = nc.sbuf_tensor("wdbuf3", [128, 22 * 128], BF16)
        wdbuf3 = wdb3_g.__enter__()
        WDB = [wdbuf, wdbuf2, wdbuf3]

        def h_ap(ic):
            if ic < DC:
                return attnout.ap()[:, ic * TOK:(ic + 1) * TOK]
            return hbuf2.ap()[:, (ic - DC) * TOK:(ic - DC + 1) * TOK]

        guW0 = C["w"]
        gu_te = []
        gu_silu_pred = [C["act"] + ic + 1 for ic in range(ICN)]
        gu_h_pred = [C["dve"] + 2 * (ic + 1) for ic in range(ICN)]

        def p5_gu_te(t):
            wt("te", t, "dve", c_xn3)
            for ic in range(ICN):
                wt("te", t, "w", guW0 + (ic + 1) * 32)
                if ic >= 2:
                    wt("te", t, "act", gu_silu_pred[ic - 2])
                    wt("te", t, "dve", gu_h_pred[ic - 2])
                gps, ups = SP[ic % 2], APm[ic % 2]
                gbuf = wbufA if ic % 2 == 0 else wbufB
                ubuf = wbufC if ic % 2 == 0 else wbufD
                insu = None
                for kc in range(DC):
                    for ts in range(2):
                        mv = xn3.ap()[:, kc * TOK + ts * 512: kc * TOK + ts * 512 + 512]
                        t.matmul(gps.ap()[:, ts * 512:ts * 512 + 512],
                                 gbuf.ap()[:, kc * 128:(kc + 1) * 128], mv,
                                 start=(kc == 0), stop=(kc == DC - 1))
                        insu = t.matmul(ups.ap()[:, ts * 512:ts * 512 + 512],
                                        ubuf.ap()[:, kc * 128:(kc + 1) * 128], mv,
                                        start=(kc == 0), stop=(kc == DC - 1))
                inc("te", insu, "te", 1)
                gu_te.append(C["te"])
        block.tensor(p5_gu_te)

        def p5_gu_dma(g):
            wt("pool", g, "dve", c_xn3)
            for ic in range(ICN):
                if ic >= 2:
                    wt("pool", g, "te", gu_te[ic - 2])
                else:
                    wt("pool", g, "te", max(buf_last["A"], buf_last["B"]))
                gbuf = wbufA if ic % 2 == 0 else wbufB
                ubuf = wbufC if ic % 2 == 0 else wbufD
                inc("pool", g.dma_start(out=gbuf.ap(), in_=wg_d[ic]), "w", 16)
                inc("pool", g.dma_start(out=ubuf.ap(), in_=wu_d[ic]), "w", 16)
        block.gpsimd(p5_gu_dma)

        def p5_gu_act(sc):
            for ic in range(ICN):
                wt("act", sc, "te", gu_te[ic])
                ins = sc.activation(h_ap(ic), SP[ic % 2].ap(),
                                    AF.Silu, scale=gs_t.ap()[:, ic:ic + 1])
                inc("act", ins, "act", 1)
                assert C["act"] == gu_silu_pred[ic]
        block.scalar(p5_gu_act)

        def p5_gu_dve(v):
            for ic in range(ICN):
                wt("dve", v, "te", gu_te[ic])
                wt("dve", v, "act", gu_silu_pred[ic])
                ins = v.tensor_scalar(sqscr.ap()[:, 0:TOK], APm[ic % 2].ap(),
                                      us_t.ap()[:, ic:ic + 1], None, ALU.mult)
                inc("dve", ins, "dve", 1)
                ins = v.tensor_tensor(h_ap(ic), h_ap(ic), sqscr.ap()[:, 0:TOK], ALU.mult)
                inc("dve", ins, "dve", 1)
                assert C["dve"] == gu_h_pred[ic]
        block.vector(p5_gu_dve)
        c_h_done = C["dve"]

        if "hb1" in taps:
            def tp_hb(sy):
                wt("sy", sy, "dve", c_h_done)
                inc("sy", sy.dma_start(out=tap_d["hb1"], in_=attnout.ap()), "s", 16)
                inc("sy", sy.dma_start(out=tap_d["hb2"], in_=hbuf2.ap()), "s", 16)
            block.sync(tp_hb)

        dnW0 = C["w"]
        dn_te = []
        dn_half = []
        dn_ev_pred = [C["dve"] + mc + 1 for mc in range(DC)]

        d4r = xn3
        ln4_sq_pred = [C["act"] + mc + 1 for mc in range(DC)]
        ln4_stat_te = {}

        def _ln4_stat(t, m, last):
            wt("te", t, "act", ln4_sq_pred[m])
            ins = None
            for ts in range(2):
                psl = slice(ts * 512, ts * 512 + 512)
                t.matmul(A0.ap()[:, psl], onesB_t.ap(),
                         d4r.ap()[:, m * TOK:(m + 1) * TOK][:, psl],
                         start=(m == 0), stop=last)
                sqb = sqscr.ap()[:, (m % 2) * 1024:(m % 2) * 1024 + 1024]
                ins = t.matmul(A1.ap()[:, psl], onesB_t.ap(), sqb[:, psl],
                               start=(m == 0), stop=last)
            inc("te", ins, "te", 1)
            ln4_stat_te[m] = C["te"]

        ln4_sq_an = [None] * DC

        def p5_dn_te(t):
            wt("te", t, "dve", c_h_done)
            wt("te", t, "act", gu_silu_pred[-1])
            for mc in range(DC):
                if mc >= 2:
                    wt("te", t, "dve", dn_ev_pred[mc - 2])
                ps = SP[mc % 2]
                ins = None
                for half in range(2):
                    n = 2 * mc + half
                    wt("te", t, "w", dnW0 + (n + 1) * 16)
                    buf = WDB[n % 3]
                    for ici in range(22):
                        ic = half * 22 + ici
                        for ts in range(2):
                            ins = t.matmul(ps.ap()[:, ts * 512:ts * 512 + 512],
                                           buf.ap()[:, ici * 128:(ici + 1) * 128],
                                           h_ap(ic)[:, ts * 512:ts * 512 + 512],
                                           start=(ic == 0), stop=(ic == ICN - 1))
                    inc("te", ins, "te", 1)
                    dn_half.append(C["te"])
                dn_te.append(C["te"])
                if mc >= 2:
                    _ln4_stat(t, mc - 2, False)
        block.tensor(p5_dn_te)

        def p5_dn_dma(g):
            wt("pool", g, "dve", c_xn3)
            for n in range(2 * DC):
                if n >= 3:
                    wt("pool", g, "te", dn_half[n - 3])
                buf = WDB[n % 3]
                half = n % 2
                inc("pool", g.dma_start(out=buf.ap(),
                                        in_=wd_d[n // 2][:, half * 22 * 128:(half + 1) * 22 * 128]),
                    "w", 16)
        block.gpsimd(p5_dn_dma)

        def p5_dn_ev(v):
            for mc in range(DC):
                wt("dve", v, "te", dn_te[mc])
                ins = v.tensor_scalar(d4r.ap()[:, mc * TOK:(mc + 1) * TOK],
                                      SP[mc % 2].ap(), ds_t.ap()[:, mc:mc + 1], None, ALU.mult)
                inc("dve", ins, "dve", 1)
                assert C["dve"] == dn_ev_pred[mc]
        block.vector(p5_dn_ev)

        def ln4_sq(sc):
            for mc in range(DC):
                wt("act", sc, "dve", dn_ev_pred[mc])
                if mc >= 2:
                    wt("act", sc, "te", ln4_stat_te[mc - 2])
                ins = sc.activation(sqscr.ap()[:, (mc % 2) * 1024:(mc % 2) * 1024 + 1024],
                                    d4r.ap()[:, mc * TOK:(mc + 1) * TOK], AF.Square)
                assert inc("act", ins, "act", 1) == ln4_sq_pred[mc]
        block.scalar(ln4_sq)

        if "d4" in taps:
            def tp_d4(sy):
                wt("sy", sy, "dve", dn_ev_pred[-1])
                inc("sy", sy.dma_start(
                    out=tap_d["d4"].rearrange("(c p) t -> p c t", p=128),
                    in_=d4r.ap().rearrange("p (c t) -> p c t", t=TOK)), "s", 16)
            block.sync(tp_d4)

        nc.leave_named_scope(_scope_n, _scope_i, False)
        _scope_n = 'p6_out'; _scope_i = nc.enter_named_scope('p6_out', False)[0]
        wdb3_g.__exit__(None, None, None)
        wdb2_g.__exit__(None, None, None)
        wdb1_g.__exit__(None, None, None)
        wbufD_g.__exit__(None, None, None)
        wbufC_g.__exit__(None, None, None)
        hbuf2_g.__exit__(None, None, None)

        def ln4_last(t):
            _ln4_stat(t, DC - 2, False)
            _ln4_stat(t, DC - 1, True)
        block.tensor(ln4_last)
        c_ln4_stat = C["te"]

        def ln4_mr(v):
            wt("dve", v, "te", c_ln4_stat)
            v.tensor_scalar(stM.ap()[:, 0:TOK], A0.ap(), 1.0 / DIM, None, ALU.mult)
            v.tensor_scalar(stF.ap()[:, 0:TOK], A1.ap(), 1.0 / DIM, None, ALU.mult)
            v.tensor_tensor(sqscr.ap()[:, 0:TOK], stM.ap()[:, 0:TOK], stM.ap()[:, 0:TOK], ALU.mult)
            v.tensor_tensor(stF.ap()[:, 0:TOK], stF.ap()[:, 0:TOK], sqscr.ap()[:, 0:TOK], ALU.subtract)
            v.tensor_scalar(stF.ap()[:, 0:TOK], stF.ap()[:, 0:TOK], float(LN_EPS), None, ALU.add)
            ins = v.reciprocal(stF.ap()[:, 0:TOK], stF.ap()[:, 0:TOK])
            inc("dve", ins, "dve", 1)
        block.vector(ln4_mr)
        c_ln4_r1 = C["dve"]

        def ln4_sqrt(sc):
            wt("act", sc, "dve", c_ln4_r1)
            ins = sc.activation(stR.ap()[:, 0:TOK], stF.ap()[:, 0:TOK], AF.Sqrt)
            inc("act", ins, "act", 1)
        block.scalar(ln4_sqrt)
        c_ln4 = C["act"]

        fx_g = nc.sbuf_tensor("fx", [128, TOK], F32, side="right")
        fx = fx_g.__enter__()
        fx2_g = nc.sbuf_tensor("fx2", [128, TOK], F32, side="right")
        fx2 = fx2_g.__enter__()
        fot_g = nc.sbuf_tensor("fot", [128, TOK], F32, side="right")
        fot = fot_g.__enter__()
        fot2_g = nc.sbuf_tensor("fot2", [128, TOK], F32, side="right")
        fot2 = fot2_g.__enter__()
        fgx_g = nc.sbuf_tensor("fgx", [128, TOK], F32, side="right")
        fgx = fgx_g.__enter__()
        fgo_g = nc.sbuf_tensor("fgo", [128, TOK], F32, side="right")
        fgo = fgo_g.__enter__()
        FXB = [fx, fx2]
        FOB = [fot, fot2]
        GP_FIN = 11

        fin_dve_pred = [C["dve"] + mc + 1 for mc in range(GP_FIN)]
        gp_pred0 = C["g"]
        gp_fin_pred = [gp_pred0 + i + 1 for i in range(DC - GP_FIN)]
        cs = C["s"]
        fin_ld_pred, out_st_pred = {}, {}
        gfin_ld_pred, gout_st_pred = {}, {}
        for mc in range(GP_FIN):
            cs += 16
            fin_ld_pred[mc] = cs
            if mc >= 2:
                cs += 16
                out_st_pred[mc - 2] = cs
        for mc in range(GP_FIN - 2, GP_FIN):
            cs += 16
            out_st_pred[mc] = cs
        for mc in range(GP_FIN, DC):
            cs += 16
            gfin_ld_pred[mc] = cs
            cs += 16
            gout_st_pred[mc] = cs

        def p6b_sync(sy):
            wt("sy", sy, "s", x2_st_pred[DC - 1])
            wt("sy", sy, "act", x2bf_cast_pred[DC - 1])
            wt("sy", sy, "dve", x2_dve_pred[DC - 1])
            for mc in range(GP_FIN):
                if mc >= 2:
                    wt("sy", sy, "dve", fin_dve_pred[mc - 2])
                ins = sy.dma_start(out=FXB[mc % 2].ap(),
                                   in_=x2s[mc * 128:(mc + 1) * 128, :])
                inc("sy", ins, "s", 16)
                assert C["s"] == fin_ld_pred[mc]
                if mc >= 2:
                    ins = sy.dma_start(out=out_d[(mc - 2) * 128:(mc - 1) * 128, :],
                                       in_=FOB[(mc - 2) % 2].ap())
                    inc("sy", ins, "s", 16)
                    assert C["s"] == out_st_pred[mc - 2]
            for mc in range(GP_FIN - 2, GP_FIN):
                wt("sy", sy, "dve", fin_dve_pred[mc])
                ins = sy.dma_start(out=out_d[mc * 128:(mc + 1) * 128, :],
                                   in_=FOB[mc % 2].ap())
                inc("sy", ins, "s", 16)
                assert C["s"] == out_st_pred[mc]
            for mc in range(GP_FIN, DC):
                i = mc - GP_FIN
                if i >= 1:
                    wt("sy", sy, "g", gp_fin_pred[i - 1])
                ins = sy.dma_start(out=fgx.ap(), in_=x2s[mc * 128:(mc + 1) * 128, :])
                inc("sy", ins, "s", 16)
                assert C["s"] == gfin_ld_pred[mc]
                wt("sy", sy, "g", gp_fin_pred[i])
                ins = sy.dma_start(out=out_d[mc * 128:(mc + 1) * 128, :], in_=fgo.ap())
                inc("sy", ins, "s", 16)
                assert C["s"] == gout_st_pred[mc]
        block.sync(p6b_sync)

        def p6b_dve(v):
            wt("dve", v, "act", c_ln4)
            for mc in range(GP_FIN):
                wt("dve", v, "s", fin_ld_pred[mc])
                if mc >= 2:
                    wt("dve", v, "s", out_st_pred[mc - 2])
                bufx = FXB[mc % 2].ap()
                ob = FOB[mc % 2].ap()
                v.tensor_tensor(ob, d4r.ap()[:, mc * TOK:(mc + 1) * TOK],
                                stM.ap()[:, 0:TOK], ALU.subtract)
                v.tensor_tensor(ob, ob, stR.ap()[:, 0:TOK], ALU.mult)
                ins = v.tensor_tensor(ob, ob, bufx, ALU.add)
                inc("dve", ins, "dve", 1)
                assert C["dve"] == fin_dve_pred[mc]
        block.vector(p6b_dve)

        def p6b_gp(g):
            wt("pool", g, "act", c_ln4)
            for i, mc in enumerate(range(GP_FIN, DC)):
                wt("pool", g, "s", gfin_ld_pred[mc])
                g.tensor_tensor(fgo.ap(), d4r.ap()[:, mc * TOK:(mc + 1) * TOK],
                                stM.ap()[:, 0:TOK], ALU.subtract)
                g.tensor_tensor(fgo.ap(), fgo.ap(), stR.ap()[:, 0:TOK], ALU.mult)
                ins = g.tensor_tensor(fgo.ap(), fgo.ap(), fgx.ap(), ALU.add)
                inc("pool", ins, "g", 1)
                assert C["g"] == gp_fin_pred[i]
        block.gpsimd(p6b_gp)



        nc.leave_named_scope(_scope_n, _scope_i, False)

        def p_final(sy):
            wt("sy", sy, "s", C["s"])
            wt("sy", sy, "w", C["w"])
        block.sync(p_final)

    L.audit()
    def _rng(n):
        ml = nc.lookup_mloc(n)
        return (ml.addr, ml.addr + ml.size() // 128)
    live_p5 = [_rng(n) for n in ("xn3", "hbuf2", "wbufC", "wbufD", "wdbuf", "wdbuf2", "qtmp")]
    for n in ("fx", "fx2", "fot", "fot2"):
        a, b = _rng(n)
        for (c0, c1) in live_p5:
            assert b <= c0 or a >= c1, (n, (a, b), (c0, c1))
    return nc


_BUILD_CACHE = {}


def get_nc(taps=()):
    key = tuple(sorted(taps))
    if key not in _BUILD_CACHE:
        _BUILD_CACHE[key] = build_nc(taps)
    return _BUILD_CACHE[key]


def kernel(**inputs):
    maps = _stage_inputs(inputs)
    nc = get_nc()
    res = run_bass_kernel_spmd(nc, maps, core_ids=list(range(8)))
    return _gather_output(res.results)
